# version 03
# baseline (speedup 1.0000x reference)
import numpy as np
from contextlib import ExitStack

import concourse.bass as bass
import concourse.mybir as mybir
from concourse.bass_utils import run_bass_kernel_spmd

F32 = mybir.dt.float32
BF16 = mybir.dt.bfloat16
AF = mybir.ActivationFunctionType
ALU = mybir.AluOpType
AX = mybir.AxisListType

NCORES = 8
B, T, D = 16, 2048, 1024
BPC = B // NCORES
JPC = T // NCORES
H, HD = 16, 64
E, FF1 = 16, 2048
FF2 = 4096
INV_D = 1.0 / D
EPS = 1e-5

ENGS = ("sync", "tensor", "vector", "scalar", "gpsimd")
ARENA_COLS = 52100


class Arena:
    def __init__(self, ap_f32):
        self.f = ap_f32
        self.bf = ap_f32.bitcast(BF16)
        self.off = 0
        self.marks = []

    def mark(self):
        self.marks.append(self.off)

    def release(self):
        self.off = self.marks.pop()

    def f32(self, parts, cols):
        off = (self.off + 3) & ~3
        self.off = off + 4 * cols
        assert self.off <= ARENA_COLS * 4, "arena overflow"
        return self.f[0:parts, off // 4: off // 4 + cols]

    def b16(self, parts, cols):
        off = (self.off + 3) & ~3
        self.off = off + 2 * cols
        self.off = (self.off + 3) & ~3
        assert self.off <= ARENA_COLS * 4, "arena overflow"
        return self.bf[0:parts, off // 2: off // 2 + cols]


class Plan:
    def __init__(self, nc, ctx):
        self.nc = nc
        self.ops = {e: [] for e in ENGS}
        self.sem = {e: ctx.enter_context(nc.semaphore("sem_" + e)) for e in ENGS}
        self.cnt = {e: 0 for e in ENGS}
        self.bsem = {e: ctx.enter_context(nc.semaphore("bar_" + e))
                     for e in ("sync", "gpsimd")}
        self.bcnt = {e: 0 for e in ("sync", "gpsimd")}
        self.ch = {c: ctx.enter_context(nc.semaphore("ch_" + c))
                   for c in ("ld0", "ld1")}
        self.chcnt = {c: 0 for c in self.ch}

    def dma_ch(self, queue, chan, out, in_, **kw):
        sem = self.ch[chan]
        prev = self.chcnt[chan]
        if prev:
            self.ops[queue].append(
                lambda e, sem=sem, prev=prev: e.wait_ge(sem, prev))
        self.chcnt[chan] += 16

        def fn(e, out=out, in_=in_, kw=kw, sem=sem):
            e.dma_start(out=out, in_=in_, **kw).then_inc(sem, 16)

        self.ops[queue].append(fn)
        return self.chcnt[chan]

    def wait_ch(self, eng, chan, value):
        if value <= 0:
            return
        sem = self.ch[chan]
        self.ops[eng].append(lambda e, sem=sem, value=value: e.wait_ge(sem, value))

    def emit(self, eng, fn, inc=0):
        if eng == "vector":
            sem = self.sem[eng]
            prev = self.cnt[eng]
            if prev:
                self.ops[eng].append(
                    lambda e, sem=sem, prev=prev: e.wait_ge(sem, prev))
            self.cnt[eng] += 1
            self.ops[eng].append(
                lambda e, fn=fn, sem=sem: fn(e).then_inc(sem, 1))
            return self.cnt[eng]
        if inc:
            sem = self.sem[eng]
            prev = self.cnt[eng]
            if eng in ("tensor", "scalar") and prev:
                self.ops[eng].append(
                    lambda e, sem=sem, prev=prev: e.wait_ge(sem, prev))
            self.cnt[eng] += inc

            def wrapped(e, fn=fn, sem=sem, inc=inc):
                fn(e).then_inc(sem, inc)

            self.ops[eng].append(wrapped)
            return self.cnt[eng]
        self.ops[eng].append(fn)
        return self.cnt[eng]

    def wait(self, eng, producer_eng, value):
        if value <= 0:
            return
        if producer_eng in ("sync", "gpsimd"):
            value = self.cnt[producer_eng]
        sem = self.sem[producer_eng]
        self.ops[eng].append(lambda e, sem=sem, value=value: e.wait_ge(sem, value))

    def barrier(self):
        for eng in ENGS:
            if eng in self.bsem:
                sem_d = self.sem[eng]
                val_d = self.cnt[eng]
                if val_d:
                    self.ops[eng].append(
                        lambda e, sem=sem_d, val=val_d: e.wait_ge(sem, val))
                self.bcnt[eng] += 1
                sem = self.bsem[eng]
                self.ops[eng].append(
                    lambda e, sem=sem: e.nop().then_inc(sem, 1))
            else:
                self.emit(eng, lambda e: e.nop(), inc=1)
        snap = dict(self.cnt)
        bsnap = dict(self.bcnt)
        for eng in ENGS:
            for other in ENGS:
                if other == eng:
                    continue
                if other in self.bsem:
                    sem = self.bsem[other]
                    val = bsnap[other]
                    self.ops[eng].append(
                        lambda e, sem=sem, val=val: e.wait_ge(sem, val))
                else:
                    self.wait(eng, other, snap[other])

    def finalize(self, block):
        for eng in ENGS:
            ops = self.ops[eng]
            getattr(block, eng)(lambda engobj, ops=ops: [fn(engobj) for fn in ops])


def dma(P, queue, out, in_, **kw):
    P.wait(queue, queue, P.cnt[queue])
    return P.emit(
        queue, lambda e, out=out, in_=in_, kw=kw: e.dma_start(out=out, in_=in_, **kw),
        inc=16)


def mm(P, out, lhsT, rhs, start, stop, inc=0):
    return P.emit("tensor", lambda te, out=out, l=lhsT, r=rhs, a=start, b=stop:
                  te.matmul(out, l, r, start=a, stop=b), inc=inc)


def transp(P, out, in_, iden, inc=1):
    return P.emit("tensor", lambda te, out=out, i=in_, d=iden:
                  te.transpose(out, i, d), inc=inc)


def _mk_env():
    nc = bass.Bass("TRN2", target_bir_lowering=False)
    ctx = ExitStack()
    arena_t = ctx.enter_context(nc.sbuf_tensor("arena", [128, ARENA_COLS], F32))
    A = Arena(arena_t[:, :])
    pbig = ctx.enter_context(nc.psum_tensor("pbig", [128, 2048], F32))
    pst0 = ctx.enter_context(nc.psum_tensor("pst0", [128, 512], F32))
    pst1 = ctx.enter_context(nc.psum_tensor("pst1", [128, 512], F32))
    ppv = ctx.enter_context(nc.psum_tensor("ppv", [128, 512], F32))
    pbf = ctx.enter_context(nc.psum_tensor("pbf", [128, 512], BF16))
    P = Plan(nc, ctx)
    return dict(nc=nc, ctx=ctx, A=A, P=P, pbig=pbig, pst0=pst0, pst1=pst1,
                ppv=ppv, pbf=pbf)


def emit_l0(env, debug=False, x1loc=None):
    nc, ctx, A, P = env["nc"], env["ctx"], env["A"], env["P"]
    pbig, pst0, pst1, ppv, pbf = (env["pbig"], env["pst0"], env["pst1"],
                                  env["ppv"], env["pbf"])
    di = lambda n, s, d=F32: nc.dram_tensor(n, s, d, kind="ExternalInput").ap()

    x_in = di("x", [B, T, D])
    wqt, wkt, wvt, wot = (di(n, [D, D]) for n in ("wqt", "wkt", "wvt", "wot"))
    bq, bk, bv, bo = (di(n, [1, D]) for n in ("bq", "bk", "bv", "bo"))
    wqst, wkst, wvst = (di(n, [HD, 3]) for n in ("wqst", "wkst", "wvst"))
    bqs, bks, bvs = (di(n, [1, 3]) for n in ("bqs", "bks", "bvs"))
    temp = di("temp", [1, 1])
    rw_d = di("rw", [D, E])
    rb_d = di("rb", [1, E])
    w1_d = di("w1", [E, D, FF1])
    b1_d = di("b1r", [E, 1, FF1])
    w2_d = di("w2", [E, FF1, D])
    b2_d = di("b2r", [E, 1, D])
    idb_d = di("iden", [128, 128], BF16)
    idf_d = di("idenf", [128, 128])
    onef_d = di("onesf", [1, 512])
    iota_d = di("iota", [128, E])
    iotam_d = di("iotam", [128, E])

    if x1loc is None:
        x1loc = nc.dram_tensor("x1", [B, JPC, D], F32, kind="ExternalOutput").ap()

    def scr(name, shape, dtyp=F32):
        kind = "ExternalOutput" if debug else "Internal"
        return nc.dram_tensor(name, shape, dtyp, kind=kind).ap()

    xnbf = scr("xnbf", [B, 8, 128, T], BF16)
    stm = scr("stm", [B, JPC, D])
    cdr = scr("cdr", [B, E, JPC])
    PST = (pst0, pst1)
    NTL = JPC // 128

    A.mark()
    idb = A.b16(128, 128)
    idf = A.f32(128, 128)
    onef = A.f32(1, 512)
    iota_s = A.f32(128, E)
    iotam_s = A.f32(128, E)
    tempv = A.f32(128, 1)
    rw_s = A.f32(128, 8 * E)
    rb_s = A.f32(1, E)
    bq_s = A.f32(1, D)
    bk_s = A.f32(1, D)
    bv_s = A.f32(1, D)
    bo_s = A.f32(1, D)
    wqs_s = A.b16(HD, 3)
    wks_s = A.b16(HD, 3)
    wvs_s = A.b16(HD, 3)
    bqs_s = A.f32(1, 3)
    bks_s = A.f32(1, 3)
    bvs_s = A.f32(1, 3)
    eps_c = A.f32(128, 1)

    dma(P, "sync", idb, idb_d)
    dma(P, "sync", idf, idf_d)
    dma(P, "sync", onef, onef_d)
    dma(P, "sync", iota_s, iota_d)
    dma(P, "sync", iotam_s, iotam_d)
    dma(P, "sync", tempv, temp.partition_broadcast(128))
    dma(P, "sync", rb_s, rb_d)
    for dc in range(8):
        dma(P, "sync", rw_s[:, dc * E:(dc + 1) * E], rw_d[dc * 128:(dc + 1) * 128, :])
    dma(P, "sync", bq_s, bq)
    dma(P, "sync", bk_s, bk)
    dma(P, "sync", bv_s, bv)
    dma(P, "sync", bo_s, bo)
    dma(P, "gpsimd", wqs_s, wqst)
    dma(P, "gpsimd", wks_s, wkst)
    dma(P, "gpsimd", wvs_s, wvst)
    dma(P, "sync", bqs_s, bqs)
    dma(P, "sync", bks_s, bks)
    w_pro = dma(P, "sync", bvs_s, bvs)
    g_pro = P.cnt["gpsimd"]
    P.wait("vector", "sync", w_pro)
    P.wait("vector", "gpsimd", g_pro)
    P.emit("vector", lambda v: v.tensor_scalar(
        out=bqs_s, in0=bqs_s, scalar1=tempv[0:1, 0:1], scalar2=None, op0=ALU.mult))
    P.emit("vector", lambda v: v.memset(eps_c, EPS))
    P.barrier()

    A.mark()
    xt_b = [A.f32(128, D) for _ in range(4)]
    xn_b = [A.f32(128, D) for _ in range(4)]
    xsq = A.f32(128, D)
    xnf = A.f32(128, D)
    stat = A.f32(128, 16)
    sd4 = A.f32(128, 4)
    rstd4 = A.f32(128, 4)
    msq4 = A.f32(128, 4)
    asm_b = A.b16(128, 8 * 512)
    Lb = A.f32(128, NTL * E)
    curb = A.f32(128, NTL * E)
    eqb = A.f32(128, NTL * E)
    ub = eqb
    umb = A.f32(128, NTL * E)
    maskb = A.f32(128, NTL * E)
    mx0 = A.f32(128, NTL)
    nmx0 = A.f32(128, NTL)
    mxr = A.f32(128, NTL)
    selb = A.f32(128, NTL)
    den16 = A.f32(128, NTL)
    rden = A.f32(128, NTL)
    ctmp = A.f32(16, 128)
    ctile = A.f32(128, E)
    Sl = lambda t_, ti: t_[:, ti * E:(ti + 1) * E]

    for b in range(B):
        prev_xn_reads = 0
        prev_sq = 0
        prev_lg = 0
        prev_vxn = 0
        prev_asm = 0
        for g in range(4):
            tis = [g * 4 + i for i in range(4)]
            P.wait("sync", "vector", prev_vxn)
            P.wait("sync", "scalar", prev_sq)
            w_x = 0
            for i, ti in enumerate(tis):
                w_x = dma(P, "sync", xt_b[i], x_in[b, ti * 128:(ti + 1) * 128, :])
            P.wait("vector", "sync", w_x)
            P.wait("scalar", "sync", w_x)
            for i in range(4):
                P.emit("vector", lambda v, i=i: v.reduce_sum(
                    out=stat[:, i:i + 1], in_=xt_b[i], axis=AX.X))
            vsq = []
            for i in range(4):
                if i >= 1:
                    P.wait("scalar", "vector", vsq[i - 1])
                sqw = P.emit("scalar", lambda s, i=i: s.activation(
                    out=xsq, in_=xt_b[i], func=AF.Square), inc=1)
                P.wait("vector", "scalar", sqw)
                vsq.append(P.emit("vector", lambda v, i=i: v.reduce_sum(
                    out=stat[:, 4 + i:5 + i], in_=xsq, axis=AX.X), inc=1))
            prev_sq = P.cnt["scalar"]
            vm = 0
            for i in range(4):
                vm = P.emit("vector", lambda v, i=i: v.tensor_scalar(
                    out=stat[:, 8 + i:9 + i], in0=stat[:, i:i + 1], scalar1=INV_D,
                    scalar2=None, op0=ALU.mult), inc=1)
            P.wait("scalar", "vector", vm)
            am = P.emit("scalar", lambda s: s.activation(
                out=msq4, in_=stat[:, 8:12], func=AF.Square), inc=1)
            P.wait("vector", "scalar", am)
            vnv = 0
            for i in range(4):
                vnv = P.emit("vector", lambda v, i=i: v.scalar_tensor_tensor(
                    out=stat[:, 12 + i:13 + i], in0=stat[:, 4 + i:5 + i],
                    scalar=INV_D, in1=msq4[:, i:i + 1],
                    op0=ALU.mult, op1=ALU.subtract), inc=1)
            P.wait("scalar", "vector", vnv)
            asd = P.emit("scalar", lambda s: s.activation(
                out=sd4, in_=stat[:, 12:16], func=AF.Sqrt, bias=eps_c), inc=1)
            P.wait("vector", "scalar", asd)
            P.wait("vector", "tensor", prev_xn_reads)
            P.wait("vector", "sync", prev_asm)
            P.emit("vector", lambda v: v.reciprocal(out=rstd4, in_=sd4))
            for i in range(4):
                P.emit("vector", lambda v, i=i: v.tensor_scalar(
                    out=xn_b[i], in0=xt_b[i], scalar1=stat[:, 8 + i:9 + i],
                    scalar2=None, op0=ALU.subtract))
            vxn = 0
            for i in range(4):
                vxn = P.emit("vector", lambda v, i=i: v.tensor_scalar(
                    out=xn_b[i], in0=xn_b[i], scalar1=rstd4[:, i:i + 1],
                    scalar2=None, op0=ALU.mult), inc=1)
            prev_vxn = vxn

            for i, ti in enumerate(tis):
                P.wait("tensor", "vector", vxn)
                P.wait("tensor", "scalar", P.cnt["scalar"])
                P.wait("tensor", "vector", P.cnt["vector"])
                is_loc = ti < NTL
                if is_loc:
                    P.wait("scalar", "tensor", prev_lg)
                evs = []
                for dc in range(8):
                    reg = pbig[:, (dc % 2) * 512:(dc % 2) * 512 + 128]
                    if dc >= 2:
                        P.wait("tensor", "scalar", evs[dc - 2])
                        P.wait("tensor", "vector", P.cnt["vector"])
                    tw = transp(P, reg, xn_b[i][:, dc * 128:(dc + 1) * 128], idf)
                    P.wait("scalar", "tensor", tw)
                    sf = P.emit("scalar", lambda s, dc=dc, reg=reg: s.copy(
                        out=xnf[:, dc * 128:(dc + 1) * 128], in_=reg), inc=1)
                    evs.append(sf)
                    P.wait("vector", "scalar", sf)
                    P.emit("vector", lambda v, dc=dc, i=i: v.tensor_copy(
                        out=asm_b[:, dc * 512 + i * 128: dc * 512 + i * 128 + 128],
                        in_=xnf[:, dc * 128:(dc + 1) * 128]))
                if is_loc:
                    lreg = pbig[:, 1920:1920 + E]
                    P.wait("tensor", "scalar", evs[-1])
                    for dc in range(8):
                        mm(P, lreg, xnf[:, dc * 128:(dc + 1) * 128],
                           rw_s[:, dc * E:(dc + 1) * E], dc == 0, False)
                    lg = mm(P, lreg, onef[0:1, 0:128], rb_s, False, True, inc=1)
                    prev_lg = lg
                    P.wait("vector", "tensor", lg)
                    P.emit("vector", lambda v, ti=ti, lreg=lreg: v.tensor_copy(
                        out=Sl(Lb, ti), in_=lreg), inc=1)
                prev_xn_reads = P.cnt["tensor"]
            P.wait("sync", "vector", P.cnt["vector"])
            for dc in range(8):
                prev_asm = dma(P, "sync", xnbf[b, dc, :, g * 512:(g + 1) * 512],
                               asm_b[:, dc * 512:(dc + 1) * 512])

        P.wait("vector", "vector", P.cnt["vector"])
        P.emit("vector", lambda v: v.tensor_copy(out=curb, in_=Lb))
        for ti in range(NTL):
            P.emit("vector", lambda v, ti=ti: v.reduce_max(
                out=nmx0[:, ti:ti + 1], in_=Sl(Lb, ti), axis=AX.X, negate=True))
        vtk = 0
        for r in range(4):
            for ti in range(NTL):
                P.emit("vector", lambda v, ti=ti: v.reduce_max(
                    out=mxr[:, ti:ti + 1], in_=Sl(curb, ti), axis=AX.X))
            if r == 0:
                P.emit("vector", lambda v: v.tensor_copy(out=mx0, in_=mxr))
            for ti in range(NTL):
                P.emit("vector", lambda v, ti=ti: v.tensor_scalar(
                    out=Sl(eqb, ti), in0=Sl(curb, ti), scalar1=mxr[:, ti:ti + 1],
                    scalar2=None, op0=ALU.is_equal))
            for ti in range(NTL):
                P.emit("vector", lambda v, ti=ti: v.tensor_tensor(
                    out=Sl(umb, ti), in0=Sl(eqb, ti), in1=iotam_s, op=ALU.mult))
            for ti in range(NTL):
                P.emit("vector", lambda v, ti=ti: v.tensor_scalar(
                    out=Sl(umb, ti), in0=Sl(umb, ti), scalar1=999.0, scalar2=None,
                    op0=ALU.add))
            for ti in range(NTL):
                P.emit("vector", lambda v, ti=ti: v.tensor_reduce(
                    out=selb[:, ti:ti + 1], in_=Sl(umb, ti), axis=AX.X, op=ALU.min))
            for ti in range(NTL):
                P.emit("vector", lambda v, ti=ti: v.tensor_scalar(
                    out=Sl(eqb, ti), in0=iota_s, scalar1=selb[:, ti:ti + 1],
                    scalar2=None, op0=ALU.is_equal))
            if r == 0:
                P.emit("vector", lambda v: v.tensor_copy(out=maskb, in_=eqb))
            else:
                vtk = P.emit("vector", lambda v: v.tensor_tensor(
                    out=maskb, in0=maskb, in1=eqb, op=ALU.add),
                    inc=(1 if r == 3 else 0))
            if r < 3:
                for ti in range(NTL):
                    P.emit("vector", lambda v, ti=ti: v.scalar_tensor_tensor(
                        out=Sl(curb, ti), in0=Sl(eqb, ti), scalar=-1e30,
                        in1=Sl(curb, ti), op0=ALU.mult, op1=ALU.add))
        P.wait("scalar", "vector", vtk)
        au = 0
        for ti in range(NTL):
            au = P.emit("scalar", lambda s, ti=ti: s.activation(
                out=Sl(ub, ti), in_=Sl(Lb, ti), func=AF.Exp,
                bias=nmx0[:, ti:ti + 1], scale=1.0), inc=1)
        P.wait("vector", "scalar", au)
        P.emit("vector", lambda v: v.tensor_tensor(out=ub, in0=ub, in1=maskb,
                                                   op=ALU.mult))
        for ti in range(NTL):
            P.emit("vector", lambda v, ti=ti: v.reduce_sum(
                out=den16[:, ti:ti + 1], in_=Sl(ub, ti), axis=AX.X))
        P.emit("vector", lambda v: v.reciprocal(out=rden, in_=den16))
        prev_ct = 0
        for ti in range(NTL):
            P.wait("vector", "gpsimd", prev_ct)
            vc = P.emit("vector", lambda v, ti=ti: v.tensor_scalar(
                out=ctile, in0=Sl(ub, ti), scalar1=rden[:, ti:ti + 1],
                scalar2=None, op0=ALU.mult), inc=1)
            P.wait("tensor", "vector", vc)
            tc = transp(P, pbig[0:E, 512:640], ctile, idf)
            P.wait("vector", "tensor", tc)
            vt2 = P.emit("vector", lambda v: v.tensor_copy(
                out=ctmp, in_=pbig[0:E, 512:640]), inc=1)
            P.wait("gpsimd", "vector", vt2)
            prev_ct = dma(P, "gpsimd", cdr[b, :, ti * 128:(ti + 1) * 128], ctmp)
        P.wait("vector", "gpsimd", prev_ct)
    P.barrier()
    A.release()

    A.mark()
    xnb = A.b16(128, 8 * T)
    ofm = A.b16(128, 8 * JPC)
    vt = A.b16(128, 16 * 65)
    pt = [A.b16(128, 256) for _ in range(2)]
    nmax = A.f32(128, 2)
    nm2 = A.f32(2, 128)
    nmrow = A.f32(1, JPC)
    wq_fc = A.b16(128, 8 * 128)
    wk_fc = A.b16(128, 8 * 128)
    wv_fc = A.b16(128, 8 * 128)
    q_sb = A.b16(64, 2 * JPC)
    k_sb = A.b16(64, 2 * T)
    v_sb = A.b16(64, 2 * T)
    k3 = A.f32(3, T)
    v3 = A.f32(3, T)
    r1_ = A.f32(3, 256)
    r2_ = A.f32(3, 256)
    r3_ = A.f32(3, 256)
    r4_ = A.f32(3, 256)
    crt1 = A.f32(3, 256)
    crt2 = A.f32(3, 256)
    cross_b = A.b16(3, T)
    qstp_b = A.b16(3, JPC)
    rc_row = A.f32(1, 256)
    ot_s = A.b16(64, 256)
    wo_fc = A.b16(128, 8 * 128)
    sf_s = A.f32(128, 256)
    stma = A.f32(128, 2 * D)

    P.emit("vector", lambda v: v.memset(
        vt.rearrange("p (k c) -> p k c", c=65)[:, :, 64:65], 1.0))

    for b in range(B):
        P.wait("sync", "tensor", P.cnt["tensor"])
        P.wait("sync", "vector", P.cnt["vector"])
        wxb = 0
        for dc in range(8):
            wxb = dma(P, "sync", xnb[:, dc * T:(dc + 1) * T], xnbf[b, dc, :, :])
        for fc in range(8):
            P.wait("gpsimd", "tensor", P.cnt["tensor"])
            wW = 0
            for dc in range(8):
                for (wsrc, wdst) in ((wqt, wq_fc), (wkt, wk_fc), (wvt, wv_fc)):
                    wW = dma(P, "gpsimd", wdst[:, dc * 128:(dc + 1) * 128],
                             wsrc[dc * 128:(dc + 1) * 128, fc * 128:(fc + 1) * 128])
            P.wait("tensor", "gpsimd", wW)
            P.wait("tensor", "sync", wxb)
            P.wait("tensor", "vector", P.cnt["vector"])
            for (wfc, dst, brow) in ((wk_fc, k_sb, bk_s), (wv_fc, v_sb, bv_s)):
                evl = 0
                for tg in range(4):
                    reg = pbig[:, tg * 512:(tg + 1) * 512]
                    for dc in range(8):
                        mm(P, reg, wfc[:, dc * 128:(dc + 1) * 128],
                           xnb[:, dc * T + tg * 512: dc * T + tg * 512 + 512],
                           dc == 0, False)
                    w = mm(P, reg, brow[0:1, fc * 128:(fc + 1) * 128],
                           onef[0:1, 0:512], False, True, inc=1)
                    P.wait("vector", "tensor", w)
                    for hh2 in range(2):
                        evl = P.emit("vector",
                                     lambda v, reg=reg, dst=dst, tg=tg, hh2=hh2:
                                     v.tensor_copy(
                                         out=dst[0:64, hh2 * T + tg * 512:
                                                 hh2 * T + (tg + 1) * 512],
                                         in_=reg[64 * hh2:64 * hh2 + 64, :]), inc=1)
                P.wait("tensor", "vector", evl)
            reg = pbig[:, 0:256]
            for dc in range(8):
                mm(P, reg, wq_fc[:, dc * 128:(dc + 1) * 128],
                   xnb[:, dc * T: dc * T + 256], dc == 0, False)
            w = mm(P, reg, bq_s[0:1, fc * 128:(fc + 1) * 128],
                   onef[0:1, 0:256], False, True, inc=1)
            P.wait("vector", "tensor", w)
            for hh2 in range(2):
                P.emit("vector", lambda v, reg=reg, hh2=hh2: v.tensor_copy(
                    out=q_sb[0:64, hh2 * JPC: hh2 * JPC + 256],
                    in_=reg[64 * hh2:64 * hh2 + 64, :]))
            evq = P.emit("vector", lambda v: v.tensor_scalar(
                out=q_sb[0:64, :], in0=q_sb[0:64, :], scalar1=tempv[0:64, 0:1],
                scalar2=None, op0=ALU.mult), inc=1)
            P.wait("tensor", "vector", evq)
            for hh in range(2):
                h, ho, hq = 2 * fc + hh, hh * T, hh * JPC
                for (wst, bst, src_, soff, swid, dst3) in (
                        (wqs_s, bqs_s, q_sb, hq, JPC, qstp_b),
                        (wks_s, bks_s, k_sb, ho, T, k3),
                        (wvs_s, bvs_s, v_sb, ho, T, v3)):
                    stw = 0
                    for tg in range(swid // 512 if swid >= 512 else 1):
                        wdt = min(512, swid)
                        reg3 = pbig[0:3, tg * 512:tg * 512 + wdt]
                        mm(P, reg3, wst, src_[0:64, soff + tg * 512:
                                              soff + tg * 512 + wdt], True, False)
                        stw = mm(P, reg3, bst, onef[0:1, 0:wdt], False, True,
                                 inc=(1 if tg == max(0, swid // 512 - 1) else 0))
                    P.wait("vector", "tensor", stw)
                    svw = P.emit("vector", lambda v, dst3=dst3, swid=swid:
                                 v.tensor_copy(out=dst3, in_=pbig[0:3, 0:swid]),
                                 inc=1)
                    P.wait("tensor", "vector", svw)
                vk = P.cnt["vector"]
                P.wait("gpsimd", "vector", vk)
                for tg in range(8):
                    sl = slice(tg * 256, (tg + 1) * 256)
                    rw_ = 0
                    for (srt, dstp, sh) in ((k3, r1_, 1), (k3, r2_, 2),
                                            (v3, r3_, 1), (v3, r4_, 2)):
                        for jj in range(3):
                            j = (jj + sh) % 3
                            rw_ = dma(P, "gpsimd", dstp[jj:jj + 1, :],
                                      srt[j:j + 1, sl])
                    P.wait("vector", "gpsimd", rw_)
                    P.emit("vector", lambda v: v.tensor_tensor(
                        out=crt1, in0=r1_, in1=r4_, op=ALU.mult))
                    P.emit("vector", lambda v: v.tensor_tensor(
                        out=crt2, in0=r2_, in1=r3_, op=ALU.mult))
                    vcr = P.emit("vector", lambda v, sl=sl: v.tensor_tensor(
                        out=cross_b[:, sl], in0=crt1, in1=crt2, op=ALU.subtract),
                        inc=1)
                    P.wait("gpsimd", "vector", vcr)
                gq = P.cnt["gpsimd"]
                P.wait("tensor", "vector", P.cnt["vector"])
                P.wait("tensor", "gpsimd", gq)
                for qt in range(2):
                    sw = 0
                    for kg in range(4):
                        reg = pbig[:, kg * 512:(kg + 1) * 512]
                        mm(P, reg, q_sb[0:64, hq + qt * 128: hq + (qt + 1) * 128],
                           k_sb[0:64, ho + kg * 512: ho + (kg + 1) * 512],
                           True, False)
                        sw = mm(P, reg, qstp_b[:, qt * 128:(qt + 1) * 128],
                                cross_b[:, kg * 512:(kg + 1) * 512], False, True,
                                inc=(1 if kg == 3 else 0))
                    P.wait("vector", "tensor", sw)
                    rw2 = P.emit("vector", lambda v, qt=qt: v.reduce_max(
                        out=nmax[:, qt:qt + 1], in_=pbig[:, 0:T], axis=AX.X,
                        negate=True), inc=1)
                    P.wait("tensor", "vector", rw2)
                tn = transp(P, pbig[0:2, 0:128], nmax, idf)
                P.wait("vector", "tensor", tn)
                vn = P.emit("vector", lambda v: v.tensor_copy(
                    out=nm2, in_=pbig[0:2, 0:128]), inc=1)
                P.wait("gpsimd", "vector", vn)
                gn = dma(P, "gpsimd", nmrow, nm2)
                P.wait("tensor", "vector", vn)
                vvt = []
                for kc in range(16):
                    reg = pbf[:, (kc % 8) * 64:(kc % 8) * 64 + 64]
                    if kc >= 1:
                        P.wait("tensor", "vector", vvt[kc - 1])
                    tw = transp(P, reg, v_sb[0:64, ho + kc * 128: ho + (kc + 1) * 128],
                                idb[0:64, 0:64])
                    P.wait("vector", "tensor", tw)
                    vvt.append(P.emit("vector", lambda v, reg=reg, kc=kc:
                                      v.tensor_copy(out=vt[:, kc * 65:kc * 65 + 64],
                                                    in_=reg), inc=1))
                P.wait("tensor", "gpsimd", gn)
                P.wait("tensor", "vector", vvt[-1])
                aexp, pvw = [], []
                for kc in range(16):
                    reg = PST[kc % 2][:, 0:256]
                    if kc >= 2:
                        P.wait("tensor", "scalar", aexp[kc - 2])
                    mm(P, reg, k_sb[0:64, ho + kc * 128: ho + (kc + 1) * 128],
                       q_sb[0:64, hq: hq + 256], True, False)
                    mm(P, reg, cross_b[:, kc * 128:(kc + 1) * 128],
                       qstp_b[:, 0:256], False, False)
                    stw = mm(P, reg, onef[0:1, 0:128], nmrow[0:1, 0:256],
                             False, True, inc=1)
                    P.wait("scalar", "tensor", stw)
                    if kc >= 2:
                        P.wait("scalar", "tensor", pvw[kc - 2])
                    aexp.append(P.emit("scalar", lambda s, reg=reg, kc=kc:
                                       s.activation(out=pt[kc % 2], in_=reg,
                                                    func=AF.Exp), inc=1))
                    P.wait("tensor", "scalar", aexp[kc])
                    pvw.append(mm(P, ppv[0:65, 0:256], vt[:, kc * 65:(kc + 1) * 65],
                                  pt[kc % 2], kc == 0, kc == 15, inc=1))
                P.wait("vector", "tensor", pvw[15])
                vrc = P.emit("vector", lambda v: v.reciprocal(
                    out=rc_row, in_=ppv[64:65, 0:256]), inc=1)
                P.wait("tensor", "vector", vrc)
                bc = mm(P, pbig[0:64, 0:256], onef[0:1, 0:64], rc_row,
                        True, True, inc=1)
                P.wait("scalar", "tensor", pvw[15])
                P.wait("scalar", "vector", vrc)
                aev = P.emit("scalar", lambda s: s.copy(
                    out=ot_s, in_=ppv[0:64, 0:256]), inc=1)
                P.wait("vector", "tensor", bc)
                P.wait("vector", "scalar", aev)
                hs2 = 64 * hh
                vo = P.emit("vector", lambda v, fc=fc, hs2=hs2: v.tensor_tensor(
                    out=ofm[hs2:hs2 + 64, fc * JPC: fc * JPC + 256],
                    in0=ot_s, in1=pbig[0:64, 0:256], op=ALU.mult), inc=1)
                P.wait("tensor", "vector", vo)
        P.wait("tensor", "vector", P.cnt["vector"])
        vt2l = []
        for fo in range(8):
            P.wait("gpsimd", "tensor", P.cnt["tensor"])
            wW = 0
            for dc in range(8):
                wW = dma(P, "gpsimd", wo_fc[:, dc * 128:(dc + 1) * 128],
                         wot[dc * 128:(dc + 1) * 128, fo * 128:(fo + 1) * 128])
            P.wait("tensor", "gpsimd", wW)
            reg = pst0[:, 0:256]
            for dc in range(8):
                mm(P, reg, wo_fc[:, dc * 128:(dc + 1) * 128],
                   ofm[:, dc * JPC: dc * JPC + 256], dc == 0, False)
            mw = mm(P, reg, bo_s[0:1, fo * 128:(fo + 1) * 128], onef[0:1, 0:256],
                    False, True, inc=1)
            P.wait("scalar", "tensor", mw)
            aev = P.emit("scalar", lambda s, reg=reg: s.copy(out=sf_s, in_=reg),
                         inc=1)
            P.wait("tensor", "scalar", aev)
            for ts_ in range(2):
                regt = pbig[:, (ts_ % 2) * 512:(ts_ % 2) * 512 + 128]
                if len(vt2l) >= 2:
                    P.wait("tensor", "vector", vt2l[-2])
                tw = transp(P, regt, sf_s[:, ts_ * 128:(ts_ + 1) * 128], idf)
                P.wait("vector", "tensor", tw)
                vt2l.append(P.emit("vector",
                                   lambda v, regt=regt, ts_=ts_, fo=fo: v.tensor_copy(
                                       out=stma[:, ts_ * D + fo * 128:
                                                ts_ * D + fo * 128 + 128],
                                       in_=regt), inc=1))
            P.wait("scalar", "tensor", P.cnt["tensor"])
        P.wait("sync", "vector", P.cnt["vector"])
        for ts_ in range(2):
            dma(P, "sync", stm[b, ts_ * 128:(ts_ + 1) * 128, :],
                stma[:, ts_ * D:(ts_ + 1) * D])
        P.wait("vector", "sync", P.cnt["sync"])
    P.barrier()
    A.release()

    A.mark()
    xn_sg = A.b16(128, 8 * 1024)
    acc = [A.f32(128, 1024) for _ in range(8)]
    w1_s = [A.b16(128, 8 * FF1)]
    w2_s = A.b16(128, 16 * D)
    hm_s = A.b16(128, 16 * 512)
    b1_row = A.f32(1, FF1)
    b2_row = A.f32(1, D)
    ce_row = A.f32(1, 1024)
    ce_s = A.f32(128, 512)
    gt_s = A.f32(128, 128)
    gtm = A.f32(128, 1024)
    xst = A.f32(128, 1024)
    sst = A.f32(128, 1024)

    for sg in range(4):
        bs = [sg * 4 + i for i in range(4)]
        P.wait("sync", "tensor", P.cnt["tensor"])
        wx = 0
        for dc in range(8):
            for i, bb_ in enumerate(bs):
                wx = dma(P, "sync", xn_sg[:, dc * 1024 + i * 256: dc * 1024 + (i + 1) * 256],
                         xnbf[bb_, dc, :, 0:JPC])
        P.wait("vector", "tensor", P.cnt["tensor"])
        P.wait("vector", "scalar", P.cnt["scalar"])
        for oc in range(8):
            P.emit("vector", lambda v, oc=oc: v.memset(acc[oc], 0.0))
        P.wait("tensor", "sync", wx)
        for e in range(E):
            wbuf = w1_s[0]
            P.wait("gpsimd", "tensor", P.cnt["tensor"])
            w1w = 0
            for dc in range(8):
                w1w = dma(P, "gpsimd", wbuf[:, dc * FF1:(dc + 1) * FF1],
                          w1_d[e, dc * 128:(dc + 1) * 128, :])
            w2w = 0
            for fcc in range(16):
                w2w = dma(P, "gpsimd", w2_s[:, fcc * D:(fcc + 1) * D],
                          w2_d[e, fcc * 128:(fcc + 1) * 128, :])
            P.wait("sync", "tensor", P.cnt["tensor"])
            bw1 = dma(P, "sync", b1_row, b1_d[e])
            bw2 = dma(P, "sync", b2_row, b2_d[e])
            for i, bb_ in enumerate(bs):
                dma(P, "sync", ce_row[0:1, i * 256:(i + 1) * 256],
                    cdr[bb_, e:e + 1, :])
            cw = P.cnt["sync"]
            for tt in range(2):
                P.wait("tensor", "sync", cw)
                bcw = mm(P, pbig[:, 0:512], onef[0:1, 0:128],
                         ce_row[0:1, tt * 512:(tt + 1) * 512], True, True, inc=1)
                P.wait("vector", "tensor", bcw)
                vce = P.emit("vector", lambda v: v.tensor_copy(
                    out=ce_s, in_=pbig[:, 0:512]), inc=1)
                P.wait("tensor", "gpsimd", w2w)
                ag = []
                for fcc in range(16):
                    reg = PST[fcc % 2][:, :]
                    if fcc >= 2:
                        P.wait("tensor", "scalar", ag[fcc - 2])
                    for dc in range(8):
                        mm(P, reg,
                           wbuf[:, dc * FF1 + fcc * 128: dc * FF1 + fcc * 128 + 128],
                           xn_sg[:, dc * 1024 + tt * 512: dc * 1024 + tt * 512 + 512],
                           dc == 0, False)
                    hw = mm(P, reg, b1_row[0:1, fcc * 128:(fcc + 1) * 128],
                            onef[0:1, 0:512], False, True, inc=1)
                    P.wait("scalar", "tensor", hw)
                    ag.append(P.emit("scalar", lambda s, reg=reg, fcc=fcc:
                                     s.activation(
                                         out=hm_s[:, fcc * 512:(fcc + 1) * 512],
                                         in_=reg, func=AF.Gelu), inc=1))
                P.wait("vector", "scalar", ag[-1])
                P.wait("vector", "vector", vce)
                vhm = 0
                for fcc in range(16):
                    vhm = P.emit("vector", lambda v, fcc=fcc: v.tensor_tensor(
                        out=hm_s[:, fcc * 512:(fcc + 1) * 512],
                        in0=hm_s[:, fcc * 512:(fcc + 1) * 512], in1=ce_s,
                        op=ALU.mult), inc=(1 if fcc == 15 else 0))
                P.wait("tensor", "vector", vhm)
                va = []
                for oc in range(8):
                    reg = PST[oc % 2][:, :]
                    if oc >= 2:
                        P.wait("tensor", "vector", va[oc - 2])
                    for fcc in range(16):
                        mm(P, reg, w2_s[:, fcc * D + oc * 128: fcc * D + oc * 128 + 128],
                           hm_s[:, fcc * 512:(fcc + 1) * 512], fcc == 0, False)
                    ow = mm(P, reg, b2_row[0:1, oc * 128:(oc + 1) * 128],
                            ce_row[0:1, tt * 512:(tt + 1) * 512], False, True, inc=1)
                    P.wait("vector", "tensor", ow)
                    va.append(P.emit("vector", lambda v, reg=reg, oc=oc, tt=tt:
                                     v.tensor_tensor(
                                         out=acc[oc][:, tt * 512:(tt + 1) * 512],
                                         in0=acc[oc][:, tt * 512:(tt + 1) * 512],
                                         in1=reg, op=ALU.add), inc=1))
                P.wait("tensor", "vector", va[-1])
                P.wait("scalar", "tensor", P.cnt["tensor"])
        prev_out = 0
        for t8 in range(8):
            bb_ = sg * 4 + t8 // 2
            j0 = (t8 % 2) * 128
            P.wait("sync", "vector", P.cnt["vector"])
            wxx = dma(P, "sync", xst, x_in[bb_, j0:j0 + 128, :])
            wss = dma(P, "sync", sst, stm[bb_, j0:j0 + 128, :])
            P.wait("vector", "sync", prev_out)
            P.wait("tensor", "vector", P.cnt["vector"])
            vgt = []
            for oc in range(8):
                asil = P.emit("scalar", lambda s, oc=oc, t8=t8: s.activation(
                    out=gt_s, in_=acc[oc][:, t8 * 128:(t8 + 1) * 128],
                    func=AF.Silu), inc=1)
                P.wait("tensor", "scalar", asil)
                regt = pbig[:, (oc % 2) * 512:(oc % 2) * 512 + 128]
                if oc >= 2:
                    P.wait("tensor", "vector", vgt[oc - 2])
                tw = transp(P, regt, gt_s, idf)
                P.wait("scalar", "tensor", tw)
                P.wait("vector", "tensor", tw)
                vgt.append(P.emit("vector", lambda v, regt=regt, oc=oc:
                                  v.tensor_copy(
                                      out=gtm[:, oc * 128:(oc + 1) * 128],
                                      in_=regt), inc=1))
            P.wait("vector", "sync", wxx)
            P.wait("vector", "sync", wss)
            P.wait("vector", "vector", vgt[-1])
            vmu = P.emit("vector", lambda v: v.tensor_tensor(
                out=gtm, in0=gtm, in1=sst, op=ALU.mult), inc=1)
            P.wait("vector", "vector", vmu)
            vf = P.emit("vector", lambda v: v.tensor_tensor(
                out=gtm, in0=gtm, in1=xst, op=ALU.add), inc=1)
            P.wait("sync", "vector", vf)
            prev_out = dma(P, "sync", x1loc[bb_, j0:j0 + 128, :], gtm)
    A.release()
    A.release()
    return dict(idf_d=idf_d, onef_d=onef_d)


def build_l0(debug=False):
    env = _mk_env()
    P, nc, ctx = env["P"], env["nc"], env["ctx"]
    emit_l0(env, debug=debug)
    fin = P.cnt["sync"]
    P.wait("sync", "sync", fin)
    P.wait("gpsimd", "sync", fin)
    block = ctx.enter_context(nc.Block())
    P.finalize(block)
    ctx.close()
    return nc


NTOK = B * JPC


def emit_l1(env, debug=False, xs_src=None, shared_d=None):
    nc, ctx, A, P = env["nc"], env["ctx"], env["A"], env["P"]
    pbig, pst0, pst1, ppv = env["pbig"], env["pst0"], env["pst1"], env["ppv"]
    di = lambda n, s, d=F32: nc.dram_tensor(n, s, d, kind="ExternalInput").ap()

    xs_in = di("xs", [B, JPC, D]) if xs_src is None else xs_src
    ln1g_r = di("ln1g", [1, D])
    ln1b_r = di("ln1b", [1, D])
    ln2g_f = di("ln2gf", [128, 8])
    ln2b_f = di("ln2bf", [128, 8])
    tmk_f = di("tmkf", [128, 8])
    tmv_f = di("tmvf", [128, 8])
    tmr_f = di("tmrf", [128, 8])
    td_f = di("tdf", [128, 8])
    tf_f = di("tff", [128, 8])
    cmk_f = di("cmkf", [128, 8])
    cmr_f = di("cmrf", [128, 8])
    wkt = di("wkt1", [D, D])
    wvt = di("wvt1", [D, D])
    wrt = di("wrt1", [D, D])
    wot = di("wot1", [D, D])
    wckt = di("wckt", [D, FF2])
    wcvt = di("wcvt", [FF2, D])
    wcrt = di("wcrt", [D, D])
    w1_d = di("w1f", [D, FF2])
    w2_d = di("w2f", [FF2, D])
    b1_r = di("b1f", [1, FF2])
    b2_r = di("b2f", [1, D])
    qph_f = di("qphf", [128, 32])
    qwr_f = di("qwrf", [128, 32])
    qwi_f = di("qwif", [128, 32])
    if shared_d is None:
        idf_d = di("idenf", [128, 128])
        onef_d = di("onesf", [1, 512])
    else:
        idf_d = shared_d["idf_d"]
        onef_d = shared_d["onef_d"]
    onec_d = di("onescol", [128, 1])

    out_d = nc.dram_tensor("out", [B, JPC, D], F32, kind="ExternalOutput").ap()

    def scr(name, shape, dtyp=F32):
        kind = "ExternalOutput" if debug else "Internal"
        return nc.dram_tensor(name, shape, dtyp, kind=kind).ap()

    xn1f = scr("xn1f", [8, 128, NTOK])
    xn1b = scr("xn1b", [8, 128, NTOK], BF16)
    xxf = scr("xxf", [8, 128, NTOK])
    kf = scr("kf", [8, 128, NTOK])
    vf = scr("vf", [8, 128, NTOK])
    rf = scr("rf", [8, 128, NTOK])
    rwb = scr("rwb", [8, 128, NTOK], BF16)
    x2f = scr("x2f", [8, 128, NTOK])
    yyf = scr("yyf", [8, 128, NTOK])
    sqb = scr("sqb", [32, 128, NTOK], BF16)
    s1f = scr("s1f", [8, 128, NTOK])
    PST = (pst0, pst1)

    A.mark()
    idf = A.f32(128, 128)
    onef = A.f32(1, 512)
    onec = A.f32(128, 1)
    eps_c = A.f32(128, 1)
    ln1g = A.f32(1, D)
    ln1b = A.f32(1, D)
    ln2g = A.f32(128, 8)
    ln2b = A.f32(128, 8)
    tmk = A.f32(128, 8)
    tmv = A.f32(128, 8)
    tmr = A.f32(128, 8)
    tfu = A.f32(128, 8)
    wneg = A.f32(128, 8)
    cmk = A.f32(128, 8)
    cmr = A.f32(128, 8)
    g1bc = A.f32(128, D)
    b1bc = A.f32(128, D)
    tdtmp = A.f32(128, 8)

    for (dst, srcd) in ((idf, idf_d), (onef, onef_d), (onec, onec_d),
                        (ln1g, ln1g_r), (ln1b, ln1b_r), (ln2g, ln2g_f),
                        (ln2b, ln2b_f), (tmk, tmk_f), (tmv, tmv_f),
                        (tmr, tmr_f), (tfu, tf_f), (tdtmp, td_f),
                        (cmk, cmk_f), (cmr, cmr_f)):
        dma(P, "sync", dst, srcd)
    w_pro = P.cnt["sync"]
    P.wait("vector", "sync", w_pro)
    P.emit("vector", lambda v: v.memset(eps_c, EPS))
    P.wait("scalar", "sync", w_pro)
    aexp = P.emit("scalar", lambda s: s.activation(
        out=wneg, in_=tdtmp, func=AF.Exp), inc=1)
    P.wait("vector", "scalar", aexp)
    P.emit("vector", lambda v: v.tensor_scalar(
        out=wneg, in0=wneg, scalar1=-1.0, scalar2=None, op0=ALU.mult))
    P.wait("tensor", "sync", w_pro)
    for (row, bc, bank) in ((ln1g, g1bc, 0), (ln1b, b1bc, 1)):
        for hlf in range(2):
            P.wait("tensor", "vector", P.cnt["vector"])
            w = mm(P, pbig[:, bank * 512: bank * 512 + 512],
                   onef[0:1, 0:128], row[0:1, hlf * 512:(hlf + 1) * 512],
                   True, True, inc=1)
            P.wait("vector", "tensor", w)
            P.emit("vector", lambda v, bc=bc, hlf=hlf, bank=bank: v.tensor_copy(
                out=bc[:, hlf * 512:(hlf + 1) * 512],
                in_=pbig[:, bank * 512: bank * 512 + 512]))
    P.barrier()

    NT2 = NTOK // 128

    A.mark()
    xt_b = [A.f32(128, D) for _ in range(4)]
    xn_b = [A.f32(128, D) for _ in range(4)]
    xx_b = [A.f32(128, D) for _ in range(4)]
    xsq = A.f32(128, D)
    stat = A.f32(128, 16)
    stat2 = A.f32(128, 16)
    sd4 = A.f32(128, 4)
    rstd4 = A.f32(128, 4)
    msq4 = A.f32(128, 4)
    asm_f = A.f32(128, 8 * 512)
    asm_b = A.b16(128, 8 * 512)
    asm_x = A.f32(128, 8 * 512)

    def tm_stats(src_tiles, scol, n=4):
        for i in range(n):
            P.emit("vector", lambda v, i=i: v.reduce_sum(
                out=stat[:, scol + i:scol + i + 1], in_=src_tiles[i], axis=AX.X))
        vsq = []
        for i in range(n):
            if i >= 1:
                P.wait("scalar", "vector", vsq[i - 1])
            sqw = P.emit("scalar", lambda s, i=i: s.activation(
                out=xsq, in_=src_tiles[i], func=AF.Square), inc=1)
            P.wait("vector", "scalar", sqw)
            vsq.append(P.emit("vector", lambda v, i=i: v.reduce_sum(
                out=stat[:, scol + 4 + i:scol + 5 + i], in_=xsq, axis=AX.X), inc=1))
        vm = 0
        for i in range(n):
            vm = P.emit("vector", lambda v, i=i: v.tensor_scalar(
                out=stat[:, scol + 8 + i:scol + 9 + i],
                in0=stat[:, scol + i:scol + i + 1], scalar1=INV_D,
                scalar2=None, op0=ALU.mult), inc=1)
        P.wait("scalar", "vector", vm)
        am = P.emit("scalar", lambda s: s.activation(
            out=msq4, in_=stat[:, scol + 8:scol + 12], func=AF.Square), inc=1)
        P.wait("vector", "scalar", am)
        vnv = 0
        for i in range(n):
            vnv = P.emit("vector", lambda v, i=i: v.scalar_tensor_tensor(
                out=stat[:, scol + 12 + i:scol + 13 + i],
                in0=stat[:, scol + 4 + i:scol + 5 + i], scalar=INV_D,
                in1=msq4[:, i:i + 1], op0=ALU.mult, op1=ALU.subtract), inc=1)
        P.wait("scalar", "vector", vnv)
        asd = P.emit("scalar", lambda s: s.activation(
            out=sd4, in_=stat[:, scol + 12:scol + 16], func=AF.Sqrt,
            bias=eps_c), inc=1)
        P.wait("vector", "scalar", asd)
        P.emit("vector", lambda v: v.reciprocal(out=rstd4, in_=sd4))

    prev_rd = 0
    prev_asm = 0
    for g in range(NT2 // 4):
        tis = [g * 4 + i for i in range(4)]
        P.wait("sync", "vector", P.cnt["vector"])
        P.wait("sync", "scalar", P.cnt["scalar"])
        w_x = 0
        for i, ti in enumerate(tis):
            bb_, jj = ti // 2, (ti % 2) * 128
            w_x = dma(P, "sync", xt_b[i], xs_in[bb_, jj:jj + 128, :])
        P.wait("vector", "sync", w_x)
        P.wait("scalar", "sync", w_x)
        tm_stats(xt_b, 0)
        for i in range(4):
            P.emit("vector", lambda v, i=i: v.tensor_scalar(
                out=xn_b[i], in0=xt_b[i], scalar1=stat[:, 8 + i:9 + i],
                scalar2=None, op0=ALU.subtract))
        for i in range(4):
            P.emit("vector", lambda v, i=i: v.tensor_scalar(
                out=xn_b[i], in0=xn_b[i], scalar1=rstd4[:, i:i + 1],
                scalar2=None, op0=ALU.mult))
        P.wait("scalar", "vector", P.cnt["vector"])
        tm_stats(xn_b, 0)
        for i in range(4):
            P.emit("vector", lambda v, i=i: v.tensor_scalar(
                out=xx_b[i], in0=xn_b[i], scalar1=stat[:, 8 + i:9 + i],
                scalar2=None, op0=ALU.subtract))
        for i in range(4):
            P.emit("vector", lambda v, i=i: v.tensor_scalar(
                out=xx_b[i], in0=xx_b[i], scalar1=rstd4[:, i:i + 1],
                scalar2=None, op0=ALU.mult))
        for i in range(4):
            P.emit("vector", lambda v, i=i: v.tensor_tensor(
                out=xx_b[i], in0=xx_b[i], in1=g1bc, op=ALU.mult))
        for i in range(4):
            P.emit("vector", lambda v, i=i: v.tensor_tensor(
                out=xx_b[i], in0=xx_b[i], in1=b1bc, op=ALU.add))
        vready = P.cnt["vector"]
        P.wait("tensor", "vector", vready)
        P.wait("vector", "sync", prev_asm)
        P.wait("scalar", "sync", prev_asm)
        evs = []
        for i in range(4):
            for dc in range(8):
                reg = pbig[:, (dc % 2) * 512:(dc % 2) * 512 + 128]
                reg2 = pbig[:, 1024 + (dc % 2) * 512:1024 + (dc % 2) * 512 + 128]
                k = i * 8 + dc
                if k >= 2:
                    P.wait("tensor", "scalar", evs[k - 2])
                tw = transp(P, reg, xn_b[i][:, dc * 128:(dc + 1) * 128], idf)
                tw2 = transp(P, reg2, xx_b[i][:, dc * 128:(dc + 1) * 128], idf)
                P.wait("scalar", "tensor", tw2)
                sf = P.emit("scalar", lambda s, reg=reg, dc=dc, i=i: s.copy(
                    out=asm_f[:, dc * 512 + i * 128: dc * 512 + i * 128 + 128],
                    in_=reg), inc=1)
                sx = P.emit("scalar", lambda s, reg2=reg2, dc=dc, i=i: s.copy(
                    out=asm_x[:, dc * 512 + i * 128: dc * 512 + i * 128 + 128],
                    in_=reg2), inc=1)
                evs.append(sx)
                P.wait("vector", "scalar", sf)
                P.emit("vector", lambda v, dc=dc, i=i: v.tensor_copy(
                    out=asm_b[:, dc * 512 + i * 128: dc * 512 + i * 128 + 128],
                    in_=asm_f[:, dc * 512 + i * 128: dc * 512 + i * 128 + 128]))
        P.wait("sync", "scalar", P.cnt["scalar"])
        P.wait("sync", "vector", P.cnt["vector"])
        for dc in range(8):
            dma(P, "sync", xn1f[dc, :, g * 512:(g + 1) * 512],
                asm_f[:, dc * 512:(dc + 1) * 512])
            dma(P, "sync", xn1b[dc, :, g * 512:(g + 1) * 512],
                asm_b[:, dc * 512:(dc + 1) * 512])
            prev_asm = dma(P, "sync", xxf[dc, :, g * 512:(g + 1) * 512],
                           asm_x[:, dc * 512:(dc + 1) * 512])
    P.barrier()
    A.release()

    A.mark()
    wk_s = A.b16(128, 8 * D)
    wv_s = A.b16(128, 8 * D)
    wr_s = A.b16(128, 8 * D)
    km_s = A.b16(128, 8 * 512)
    vm_s = A.b16(128, 8 * 512)
    rm_s = A.b16(128, 8 * 512)
    xwin = [A.f32(128, 768) for _ in range(2)]
    kev = A.f32(128, 512)
    vev = A.f32(128, 512)
    rev = A.f32(128, 512)

    wW = 0
    for dc in range(8):
        for (wsrc, wdst) in ((wkt, wk_s), (wvt, wv_s), (wrt, wr_s)):
            wW = dma(P, "gpsimd", wdst[:, dc * D:(dc + 1) * D],
                     wsrc[dc * 128:(dc + 1) * 128, :])
    for TT in range(8):
        t0 = TT * 512
        lo = max(0, t0 - 256)
        for ic in range(8):
            P.wait("sync", "vector", P.cnt["vector"])
            wx = P.dma_ch("sync", "ld%d" % (ic % 2), xwin[ic % 2][:, 0:t0 + 512 - lo],
                          xxf[ic, :, lo:t0 + 512])
            P.wait_ch("vector", "ld%d" % (ic % 2), wx)
            cur = xwin[ic % 2][:, 256:768] if TT > 0 else xwin[ic % 2][:, 0:512]
            for (mix, msb, mvec) in ((0, km_s, tmk), (1, vm_s, tmv), (2, rm_s, tmr)):
                dsl = msb[:, ic * 512:(ic + 1) * 512]
                if TT == 0:
                    P.emit("vector", lambda v, dsl=dsl, cur=cur, mvec=mvec, ic=ic:
                           v.tensor_scalar(out=dsl[:, 0:256], in0=cur[:, 0:256],
                                           scalar1=mvec[:, ic:ic + 1], scalar2=None,
                                           op0=ALU.mult))
                    sh = xwin[ic % 2][:, 0:256]
                    cu2 = xwin[ic % 2][:, 256:512]
                    P.emit("vector", lambda v, cu2=cu2, sh=sh: v.tensor_tensor(
                        out=kev[:, 0:256], in0=cu2, in1=sh, op=ALU.subtract))
                    P.emit("vector", lambda v, dsl=dsl, sh=sh, mvec=mvec, ic=ic:
                           v.scalar_tensor_tensor(
                               out=dsl[:, 256:512], in0=kev[:, 0:256],
                               scalar=mvec[:, ic:ic + 1], in1=sh,
                               op0=ALU.mult, op1=ALU.add))
                else:
                    sh = xwin[ic % 2][:, 0:512]
                    P.emit("vector", lambda v, cur=cur, sh=sh: v.tensor_tensor(
                        out=kev, in0=cur, in1=sh, op=ALU.subtract))
                    P.emit("vector", lambda v, dsl=dsl, sh=sh, mvec=mvec, ic=ic:
                           v.scalar_tensor_tensor(
                               out=dsl, in0=kev, scalar=mvec[:, ic:ic + 1],
                               in1=sh, op0=ALU.mult, op1=ALU.add))
        vmix = P.cnt["vector"]
        P.wait("tensor", "vector", vmix)
        if TT == 0:
            P.wait("tensor", "gpsimd", wW)
        for oc in range(8):
            for (wsb, msb, psm) in ((wk_s, km_s, pst0), (wv_s, vm_s, pst1),
                                    (wr_s, rm_s, ppv)):
                for ic in range(8):
                    mm(P, psm[:, :], wsb[:, ic * D + oc * 128: ic * D + oc * 128 + 128],
                       msb[:, ic * 512:(ic + 1) * 512], ic == 0, ic == 7,
                       inc=(1 if ic == 7 else 0))
            kw_ = P.cnt["tensor"]
            P.wait("scalar", "tensor", kw_)
            ke = P.emit("scalar", lambda s: s.copy(out=kev, in_=pst0[:, :]), inc=1)
            re_ = P.emit("scalar", lambda s: s.copy(out=rev, in_=ppv[:, :]), inc=1)
            P.wait("vector", "tensor", kw_)
            ve = P.emit("vector", lambda v: v.tensor_copy(out=vev, in_=pst1[:, :]),
                        inc=1)
            P.wait("sync", "scalar", re_)
            P.wait("sync", "vector", ve)
            dma(P, "sync", kf[oc, :, t0:t0 + 512], kev)
            dma(P, "sync", vf[oc, :, t0:t0 + 512], vev)
            w_st = dma(P, "sync", rf[oc, :, t0:t0 + 512], rev)
            P.wait("tensor", "sync", w_st)
    P.barrier()
    A.release()

    A.mark()
    aa = A.f32(128, 2048)
    bb = A.f32(128, 2048)
    pp = A.f32(128, 2048)
    kb_s = A.f32(128, 2048)
    vb_s = A.f32(128, 2048)
    rb_s = A.f32(128, 2048)
    t_a = A.f32(128, 2048)
    t_b = A.f32(128, 2048)
    t_c = A.f32(128, 2048)
    t_d = A.f32(128, 2048)
    t_e = A.f32(128, 2048)
    t_f = A.f32(128, 2048)
    t_g = A.f32(128, 2048)
    rw_o = A.b16(128, 2048)

    P.emit("vector", lambda v: v.memset(aa, 0.0))
    P.emit("vector", lambda v: v.memset(bb, 0.0))
    P.emit("vector", lambda v: v.memset(pp, -1e38))
    ppc, aac, bbc = pp, aa, bb
    for step in range(B):
        c0 = step * JPC
        P.wait("sync", "vector", P.cnt["vector"])
        P.wait("sync", "scalar", P.cnt["scalar"])
        wl = 0
        for dc in range(8):
            dma(P, "sync", kb_s[:, dc * 256:(dc + 1) * 256], kf[dc, :, c0:c0 + JPC])
            dma(P, "sync", vb_s[:, dc * 256:(dc + 1) * 256], vf[dc, :, c0:c0 + JPC])
            wl = dma(P, "sync", rb_s[:, dc * 256:(dc + 1) * 256],
                     rf[dc, :, c0:c0 + JPC])
        P.wait("vector", "sync", wl)
        P.wait("scalar", "sync", wl)
        for dc in range(8):
            P.emit("vector", lambda v, dc=dc: v.tensor_scalar(
                out=t_a[:, dc * 256:(dc + 1) * 256],
                in0=kb_s[:, dc * 256:(dc + 1) * 256],
                scalar1=tfu[:, dc:dc + 1], scalar2=None, op0=ALU.add))
        P.emit("vector", lambda v: v.tensor_tensor(out=t_b, in0=pp, in1=t_a,
                                                   op=ALU.max))
        P.emit("vector", lambda v: v.tensor_tensor(out=t_c, in0=pp, in1=t_b,
                                                   op=ALU.subtract))
        P.emit("vector", lambda v: v.tensor_tensor(out=t_d, in0=t_a, in1=t_b,
                                                   op=ALU.subtract))
        vv_ = P.cnt["vector"]
        P.wait("scalar", "vector", vv_)
        e1w = P.emit("scalar", lambda s: s.activation(out=t_c, in_=t_c, func=AF.Exp),
                     inc=1)
        e2w = P.emit("scalar", lambda s: s.activation(out=t_d, in_=t_d, func=AF.Exp),
                     inc=1)
        P.wait("vector", "scalar", e2w)
        P.emit("vector", lambda v: v.tensor_tensor(out=t_e, in0=t_c, in1=aa,
                                                   op=ALU.mult))
        P.emit("vector", lambda v: v.tensor_tensor(out=t_f, in0=t_d, in1=vb_s,
                                                   op=ALU.mult))
        P.emit("vector", lambda v: v.tensor_tensor(out=t_e, in0=t_e, in1=t_f,
                                                   op=ALU.add))
        P.emit("vector", lambda v: v.tensor_tensor(out=t_f, in0=t_c, in1=bb,
                                                   op=ALU.mult))
        P.emit("vector", lambda v: v.tensor_tensor(out=t_f, in0=t_f, in1=t_d,
                                                   op=ALU.add))
        P.emit("vector", lambda v: v.reciprocal(out=t_f, in_=t_f))
        P.emit("vector", lambda v: v.tensor_tensor(out=t_e, in0=t_e, in1=t_f,
                                                   op=ALU.mult))
        vv_ = P.cnt["vector"]
        P.wait("scalar", "vector", vv_)
        rsg = P.emit("scalar", lambda s: s.activation(out=t_f, in_=rb_s,
                                                      func=AF.Sigmoid), inc=1)
        P.wait("vector", "scalar", rsg)
        vrw = P.emit("vector", lambda v: v.tensor_tensor(out=rw_o, in0=t_f, in1=t_e,
                                                         op=ALU.mult), inc=1)
        P.wait("sync", "vector", vrw)
        for dc in range(8):
            dma(P, "sync", rwb[dc, :, c0:c0 + JPC], rw_o[:, dc * 256:(dc + 1) * 256])
        for dc in range(8):
            P.emit("vector", lambda v, dc=dc: v.tensor_scalar(
                out=t_a[:, dc * 256:(dc + 1) * 256],
                in0=pp[:, dc * 256:(dc + 1) * 256],
                scalar1=wneg[:, dc:dc + 1], scalar2=None, op0=ALU.add))
        P.emit("vector", lambda v: v.tensor_tensor(out=t_b, in0=t_a, in1=kb_s,
                                                   op=ALU.max))
        P.emit("vector", lambda v: v.tensor_tensor(out=t_c, in0=t_a, in1=t_b,
                                                   op=ALU.subtract))
        P.emit("vector", lambda v: v.tensor_tensor(out=t_d, in0=kb_s, in1=t_b,
                                                   op=ALU.subtract))
        vv_ = P.cnt["vector"]
        P.wait("scalar", "vector", vv_)
        e1w = P.emit("scalar", lambda s: s.activation(out=t_c, in_=t_c, func=AF.Exp),
                     inc=1)
        e2w = P.emit("scalar", lambda s: s.activation(out=t_d, in_=t_d, func=AF.Exp),
                     inc=1)
        P.wait("vector", "scalar", e2w)
        P.emit("vector", lambda v: v.tensor_tensor(out=t_e, in0=t_c, in1=aa,
                                                   op=ALU.mult))
        P.emit("vector", lambda v: v.tensor_tensor(out=t_g, in0=t_d, in1=vb_s,
                                                   op=ALU.mult))
        P.emit("vector", lambda v: v.tensor_tensor(out=aa, in0=t_e, in1=t_g,
                                                   op=ALU.add))
        P.emit("vector", lambda v: v.tensor_tensor(out=t_e, in0=t_c, in1=bb,
                                                   op=ALU.mult))
        P.emit("vector", lambda v: v.tensor_tensor(out=bb, in0=t_e, in1=t_d,
                                                   op=ALU.add))
        P.emit("vector", lambda v: v.tensor_copy(out=pp, in_=t_b))
    P.barrier()
    A.release()

    A.mark()
    wo_s = A.b16(128, 8 * D)
    rwt = A.b16(128, 8 * 512)
    xn1t = A.f32(128, 512)
    x2t = A.f32(128, 8 * 512)
    sqt = A.f32(128, 512)
    mrow = A.f32(1, 512)
    e2row = A.f32(1, 512)
    nvrow = A.f32(1, 512)
    sdrow = A.f32(1, 512)
    rsrow = A.f32(1, 512)
    yyt = A.f32(128, 512)
    u1t = A.f32(128, 512)

    wW = 0
    for dc in range(8):
        wW = dma(P, "gpsimd", wo_s[:, dc * D:(dc + 1) * D],
                 wot[dc * 128:(dc + 1) * 128, :])
    for TT in range(8):
        t0 = TT * 512
        P.wait("sync", "tensor", P.cnt["tensor"])
        wl = 0
        for dc in range(8):
            wl = dma(P, "sync", rwt[:, dc * 512:(dc + 1) * 512],
                     rwb[dc, :, t0:t0 + 512])
        P.wait("tensor", "sync", wl)
        if TT == 0:
            P.wait("tensor", "gpsimd", wW)
        for oc in range(8):
            psm = PST[oc % 2]
            for ic in range(8):
                mm(P, psm[:, :], wo_s[:, ic * D + oc * 128: ic * D + oc * 128 + 128],
                   rwt[:, ic * 512:(ic + 1) * 512], ic == 0, ic == 7,
                   inc=(1 if ic == 7 else 0))
            ow = P.cnt["tensor"]
            P.wait("sync", "vector", P.cnt["vector"])
            wx = P.dma_ch("sync", "ld0", xn1t, xn1f[oc, :, t0:t0 + 512])
            P.wait_ch("vector", "ld0", wx)
            P.wait("vector", "tensor", ow)
            vx2 = P.emit("vector", lambda v, oc=oc: v.tensor_tensor(
                out=x2t[:, oc * 512:(oc + 1) * 512], in0=xn1t, in1=PST[oc % 2][:, :],
                op=ALU.add), inc=1)
            P.wait("scalar", "vector", vx2)
            sq_w = P.emit("scalar", lambda s, oc=oc: s.activation(
                out=sqt, in_=x2t[:, oc * 512:(oc + 1) * 512], func=AF.Square), inc=1)
            P.wait("tensor", "vector", vx2)
            P.wait("tensor", "scalar", sq_w)
            mm(P, pbig[0:1, 0:512], onec, x2t[:, oc * 512:(oc + 1) * 512],
               oc == 0, oc == 7)
            mm(P, pbig[0:1, 512:1024], onec, sqt, oc == 0, oc == 7,
               inc=(1 if oc == 7 else 0))
        st_w = P.cnt["tensor"]
        P.wait("sync", "vector", P.cnt["vector"])
        for oc in range(8):
            dma(P, "sync", x2f[oc, :, t0:t0 + 512], x2t[:, oc * 512:(oc + 1) * 512])
        P.wait("vector", "tensor", st_w)
        P.emit("vector", lambda v: v.tensor_scalar(
            out=mrow, in0=pbig[0:1, 0:512], scalar1=INV_D, scalar2=None,
            op0=ALU.mult))
        P.emit("vector", lambda v: v.tensor_scalar(
            out=e2row, in0=pbig[0:1, 512:1024], scalar1=INV_D, scalar2=None,
            op0=ALU.mult))
        vst = P.cnt["vector"]
        P.wait("scalar", "vector", vst)
        am = P.emit("scalar", lambda s: s.activation(
            out=nvrow, in_=mrow, func=AF.Square), inc=1)
        P.wait("vector", "scalar", am)
        P.emit("vector", lambda v: v.tensor_tensor(
            out=nvrow, in0=e2row, in1=nvrow, op=ALU.subtract))
        vnv = P.cnt["vector"]
        P.wait("scalar", "vector", vnv)
        asd = P.emit("scalar", lambda s: s.activation(
            out=sdrow, in_=nvrow, func=AF.Sqrt, bias=eps_c[0:1, 0:1]), inc=1)
        P.wait("vector", "scalar", asd)
        P.emit("vector", lambda v: v.reciprocal(out=rsrow, in_=sdrow))
        vrs = P.cnt["vector"]
        P.wait("tensor", "vector", vrs)
        bm = mm(P, pbig[:, 1024:1536], onef[0:1, 0:128], mrow, True, True, inc=1)
        br = mm(P, pbig[:, 1536:2048], onef[0:1, 0:128], rsrow, True, True, inc=1)
        P.wait("vector", "tensor", br)
        for oc in range(8):
            P.emit("vector", lambda v, oc=oc: v.tensor_tensor(
                out=u1t, in0=x2t[:, oc * 512:(oc + 1) * 512],
                in1=pbig[:, 1024:1536], op=ALU.subtract))
            P.emit("vector", lambda v: v.tensor_tensor(
                out=u1t, in0=u1t, in1=pbig[:, 1536:2048], op=ALU.mult))
            vy = P.emit("vector", lambda v, oc=oc: v.tensor_scalar(
                out=yyt, in0=u1t, scalar1=ln2g[:, oc:oc + 1],
                scalar2=ln2b[:, oc:oc + 1], op0=ALU.mult, op1=ALU.add), inc=1)
            P.wait("sync", "vector", vy)
            dma(P, "sync", yyf[oc, :, t0:t0 + 512], yyt)
            P.wait("vector", "sync", P.cnt["sync"])
        P.wait("tensor", "sync", P.cnt["sync"])
    P.barrier()
    A.release()

    A.mark()
    wck_s = A.b16(128, 8 * FF2)
    ym_s = A.b16(128, 8 * 512)
    ywin = [A.f32(128, 768) for _ in range(2)]
    dtmp = A.f32(128, 512)
    rk_t = A.f32(128, 512)
    sq_b16 = A.b16(128, 512)

    wW = 0
    for dc in range(8):
        wW = dma(P, "gpsimd", wck_s[:, dc * FF2:(dc + 1) * FF2],
                 wckt[dc * 128:(dc + 1) * 128, :])
    for TT in range(8):
        t0 = TT * 512
        lo = max(0, t0 - 256)
        for ic in range(8):
            P.wait("sync", "vector", P.cnt["vector"])
            wx = P.dma_ch("sync", "ld%d" % (ic % 2), ywin[ic % 2][:, 0:t0 + 512 - lo],
                          yyf[ic, :, lo:t0 + 512])
            P.wait_ch("vector", "ld%d" % (ic % 2), wx)
            cur = ywin[ic % 2][:, 256:768] if TT > 0 else ywin[ic % 2][:, 0:512]
            dsl = ym_s[:, ic * 512:(ic + 1) * 512]
            if TT == 0:
                P.emit("vector", lambda v, dsl=dsl, cur=cur, ic=ic: v.tensor_scalar(
                    out=dsl[:, 0:256], in0=cur[:, 0:256], scalar1=cmk[:, ic:ic + 1],
                    scalar2=None, op0=ALU.mult))
                sh = ywin[ic % 2][:, 0:256]
                cu2 = ywin[ic % 2][:, 256:512]
                P.emit("vector", lambda v, cu2=cu2, sh=sh: v.tensor_tensor(
                    out=dtmp[:, 0:256], in0=cu2, in1=sh, op=ALU.subtract))
                P.emit("vector", lambda v, dsl=dsl, sh=sh, ic=ic:
                       v.scalar_tensor_tensor(
                           out=dsl[:, 256:512], in0=dtmp[:, 0:256],
                           scalar=cmk[:, ic:ic + 1], in1=sh,
                           op0=ALU.mult, op1=ALU.add))
            else:
                sh = ywin[ic % 2][:, 0:512]
                P.emit("vector", lambda v, cur=cur, sh=sh: v.tensor_tensor(
                    out=dtmp, in0=cur, in1=sh, op=ALU.subtract))
                P.emit("vector", lambda v, dsl=dsl, sh=sh, ic=ic:
                       v.scalar_tensor_tensor(
                           out=dsl, in0=dtmp, scalar=cmk[:, ic:ic + 1],
                           in1=sh, op0=ALU.mult, op1=ALU.add))
        vmix = P.cnt["vector"]
        P.wait("tensor", "vector", vmix)
        if TT == 0:
            P.wait("tensor", "gpsimd", wW)
        for fc in range(32):
            psm = PST[fc % 2]
            for ic in range(8):
                mm(P, psm[:, :],
                   wck_s[:, ic * FF2 + fc * 128: ic * FF2 + fc * 128 + 128],
                   ym_s[:, ic * 512:(ic + 1) * 512], ic == 0, ic == 7,
                   inc=(1 if ic == 7 else 0))
            kw_ = P.cnt["tensor"]
            P.wait("vector", "tensor", kw_)
            P.wait("vector", "scalar", P.cnt["scalar"])
            vrk = P.emit("vector", lambda v, fc=fc: v.tensor_scalar(
                out=rk_t, in0=PST[fc % 2][:, :], scalar1=0.0, scalar2=None,
                op0=ALU.max), inc=1)
            P.wait("scalar", "vector", vrk)
            asq = P.emit("scalar", lambda s: s.activation(
                out=sq_b16, in_=rk_t, func=AF.Square), inc=1)
            P.wait("sync", "scalar", asq)
            w_st = dma(P, "sync", sqb[fc, :, t0:t0 + 512], sq_b16)
            P.wait("scalar", "sync", w_st)
            P.wait("tensor", "vector", vrk)
    P.barrier()
    A.release()

    A.mark()
    wcv_s = A.b16(128, 32 * D)
    wcr_s = A.b16(128, 8 * D)
    rm2_s = A.b16(128, 8 * 512)
    sq_s = A.b16(128, 32 * 512)
    ywin2 = [A.f32(128, 768) for _ in range(2)]
    dtmp2 = A.f32(128, 512)
    rs_t = A.f32(128, 512)
    s1_t = A.f32(128, 512)
    x2t2 = A.f32(128, 512)

    wW = 0
    for fc in range(32):
        wW = dma(P, "gpsimd", wcv_s[:, fc * D:(fc + 1) * D],
                 wcvt[fc * 128:(fc + 1) * 128, :])
    for dc in range(8):
        wW = dma(P, "gpsimd", wcr_s[:, dc * D:(dc + 1) * D],
                 wcrt[dc * 128:(dc + 1) * 128, :])
    for TT in range(8):
        t0 = TT * 512
        lo = max(0, t0 - 256)
        for ic in range(8):
            P.wait("sync", "vector", P.cnt["vector"])
            wx = P.dma_ch("sync", "ld%d" % (ic % 2), ywin2[ic % 2][:, 0:t0 + 512 - lo],
                          yyf[ic, :, lo:t0 + 512])
            P.wait_ch("vector", "ld%d" % (ic % 2), wx)
            cur = ywin2[ic % 2][:, 256:768] if TT > 0 else ywin2[ic % 2][:, 0:512]
            dsl = rm2_s[:, ic * 512:(ic + 1) * 512]
            if TT == 0:
                P.emit("vector", lambda v, dsl=dsl, cur=cur, ic=ic: v.tensor_scalar(
                    out=dsl[:, 0:256], in0=cur[:, 0:256], scalar1=cmr[:, ic:ic + 1],
                    scalar2=None, op0=ALU.mult))
                sh = ywin2[ic % 2][:, 0:256]
                cu2 = ywin2[ic % 2][:, 256:512]
                P.emit("vector", lambda v, cu2=cu2, sh=sh: v.tensor_tensor(
                    out=dtmp2[:, 0:256], in0=cu2, in1=sh, op=ALU.subtract))
                P.emit("vector", lambda v, dsl=dsl, sh=sh, ic=ic:
                       v.scalar_tensor_tensor(
                           out=dsl[:, 256:512], in0=dtmp2[:, 0:256],
                           scalar=cmr[:, ic:ic + 1], in1=sh,
                           op0=ALU.mult, op1=ALU.add))
            else:
                sh = ywin2[ic % 2][:, 0:512]
                P.emit("vector", lambda v, cur=cur, sh=sh: v.tensor_tensor(
                    out=dtmp2, in0=cur, in1=sh, op=ALU.subtract))
                P.emit("vector", lambda v, dsl=dsl, sh=sh, ic=ic:
                       v.scalar_tensor_tensor(
                           out=dsl, in0=dtmp2, scalar=cmr[:, ic:ic + 1],
                           in1=sh, op0=ALU.mult, op1=ALU.add))
        P.wait("sync", "scalar", P.cnt["scalar"])
        wq = 0
        for fc in range(32):
            wq = dma(P, "sync", sq_s[:, fc * 512:(fc + 1) * 512],
                     sqb[fc, :, t0:t0 + 512])
        vmix = P.cnt["vector"]
        P.wait("tensor", "vector", vmix)
        P.wait("tensor", "sync", wq)
        if TT == 0:
            P.wait("tensor", "gpsimd", wW)
        for oc in range(8):
            for fc in range(32):
                mm(P, pst0[:, :],
                   wcv_s[:, fc * D + oc * 128: fc * D + oc * 128 + 128],
                   sq_s[:, fc * 512:(fc + 1) * 512], fc == 0, fc == 31,
                   inc=(1 if fc == 31 else 0))
            vvw = P.cnt["tensor"]
            for ic in range(8):
                mm(P, pst1[:, :],
                   wcr_s[:, ic * D + oc * 128: ic * D + oc * 128 + 128],
                   rm2_s[:, ic * 512:(ic + 1) * 512], ic == 0, ic == 7,
                   inc=(1 if ic == 7 else 0))
            rrw = P.cnt["tensor"]
            P.wait("scalar", "tensor", rrw)
            ars = P.emit("scalar", lambda s: s.activation(
                out=rs_t, in_=pst1[:, :], func=AF.Sigmoid), inc=1)
            P.wait("sync", "vector", P.cnt["vector"])
            wx2 = P.dma_ch("sync", "ld0", x2t2, x2f[oc, :, t0:t0 + 512])
            P.wait_ch("vector", "ld0", wx2)
            P.wait("vector", "scalar", ars)
            P.wait("vector", "tensor", vvw)
            P.emit("vector", lambda v: v.tensor_tensor(
                out=s1_t, in0=rs_t, in1=pst0[:, :], op=ALU.mult))
            vs1 = P.emit("vector", lambda v: v.tensor_tensor(
                out=s1_t, in0=s1_t, in1=x2t2, op=ALU.add), inc=1)
            P.wait("sync", "vector", vs1)
            w_st = dma(P, "sync", s1f[oc, :, t0:t0 + 512], s1_t)
            P.wait("vector", "sync", w_st)
            P.wait("tensor", "vector", vs1)
    P.barrier()
    A.release()

    A.mark()
    w1_s = A.b16(128, 8 * FF2)
    w2_s = A.b16(128, 32 * D)
    b1row = A.b16(1, FF2)
    oneb = A.b16(1, 512)
    b2row = A.f32(1, D)
    thc = A.f32(128, 32)
    a2c = A.f32(128, 32)
    qtmp1 = A.f32(128, 32)
    qtmp2 = A.f32(128, 32)
    qr_s = A.b16(128, 32 * 256)
    xn1t2 = A.b16(128, 256)
    h_t = A.f32(128, 256)
    a_t = A.f32(128, 256)
    s4_t = A.f32(128, 256)
    s2_t = A.f32(128, 256)
    c2_t = A.f32(128, 256)
    g1_t = A.f32(128, 256)
    s1t2 = A.f32(128, 256)
    tr_t = A.f32(128, 256)
    oasm = A.f32(128, 2 * D)
    x1t = A.f32(128, D)
    qv_s = A.f32(128, 32 * 3)

    wW = dma(P, "gpsimd", b1row, b1_r)
    wW = dma(P, "sync", b2row, b2_r)
    P.wait("vector", "gpsimd", P.cnt["gpsimd"])
    P.emit("vector", lambda v: v.memset(oneb, 1.0))
    wW = dma(P, "sync", qv_s[:, 0:32], qph_f)
    wW = dma(P, "sync", qv_s[:, 32:64], qwr_f)
    wW = dma(P, "sync", qv_s[:, 64:96], qwi_f)
    for dc in range(8):
        dma(P, "gpsimd", w1_s[:, dc * FF2:(dc + 1) * FF2],
            w1_d[dc * 128:(dc + 1) * 128, :])
    wW2 = 0
    for fc in range(32):
        wW2 = dma(P, "gpsimd", w2_s[:, fc * D:(fc + 1) * D],
                  w2_d[fc * 128:(fc + 1) * 128, :])
    PI = float(np.pi)
    P.wait("vector", "sync", wW)
    P.emit("vector", lambda v: v.tensor_copy(out=thc, in_=qv_s[:, 0:32]))
    for _ in range(5):
        P.emit("vector", lambda v: v.tensor_single_scalar(
            out=qtmp1, in_=thc, scalar=PI, op=ALU.is_gt))
        P.emit("vector", lambda v: v.scalar_tensor_tensor(
            out=thc, in0=qtmp1, scalar=-2 * PI, in1=thc, op0=ALU.mult, op1=ALU.add))
        P.emit("vector", lambda v: v.tensor_single_scalar(
            out=qtmp1, in_=thc, scalar=-PI, op=ALU.is_lt))
        P.emit("vector", lambda v: v.scalar_tensor_tensor(
            out=thc, in0=qtmp1, scalar=2 * PI, in1=thc, op0=ALU.mult, op1=ALU.add))
    vth = P.cnt["vector"]
    P.wait("scalar", "vector", vth)
    asn = P.emit("scalar", lambda s: s.activation(out=qtmp1, in_=thc, func=AF.Sin),
                 inc=1)
    ash = P.emit("scalar", lambda s: s.activation(out=qtmp2, in_=thc, func=AF.Sin,
                                                  scale=0.5), inc=1)
    P.wait("vector", "scalar", ash)
    P.emit("vector", lambda v: v.tensor_tensor(out=qtmp2, in0=qtmp2, in1=qtmp2,
                                               op=ALU.mult))
    P.emit("vector", lambda v: v.tensor_scalar(out=qtmp2, in0=qtmp2, scalar1=-2.0,
                                               scalar2=1.0, op0=ALU.mult,
                                               op1=ALU.add))
    P.emit("vector", lambda v: v.tensor_tensor(out=qtmp2, in0=qtmp2,
                                               in1=qv_s[:, 32:64], op=ALU.mult))
    P.emit("vector", lambda v: v.tensor_tensor(out=qtmp1, in0=qtmp1,
                                               in1=qv_s[:, 64:96], op=ALU.mult))
    P.emit("vector", lambda v: v.tensor_tensor(out=a2c, in0=qtmp2, in1=qtmp1,
                                               op=ALU.add))
    P.emit("vector", lambda v: v.tensor_tensor(out=a2c, in0=a2c,
                                               in1=qv_s[:, 64:96], op=ALU.mult))
    P.emit("vector", lambda v: v.tensor_scalar(out=a2c, in0=a2c, scalar1=0.2,
                                               scalar2=None, op0=ALU.mult))

    xnt_s = A.b16(128, 8 * 256)

    for T4 in range(16):
        t0 = T4 * 256
        P.wait("sync", "tensor", P.cnt["tensor"])
        wxt = 0
        for ic in range(8):
            wxt = dma(P, "sync", xnt_s[:, ic * 256:(ic + 1) * 256],
                      xn1b[ic, :, t0:t0 + 256])
        P.wait("tensor", "sync", wxt)
        if T4 == 0:
            P.wait("tensor", "gpsimd", wW2)
            P.wait("tensor", "vector", P.cnt["vector"])
        ahel = []
        for fc in range(32):
            psm = PST[fc % 2]
            if fc >= 2:
                P.wait("tensor", "scalar", ahel[fc - 2])
            for ic in range(8):
                mm(P, psm[:, 0:256],
                   w1_s[:, ic * FF2 + fc * 128: ic * FF2 + fc * 128 + 128],
                   xnt_s[:, ic * 256:(ic + 1) * 256], ic == 0, False)
            hw = mm(P, psm[:, 0:256], b1row[0:1, fc * 128:(fc + 1) * 128],
                    oneb[0:1, 0:256], False, True, inc=1)
            P.wait("scalar", "tensor", hw)
            if fc >= 1:
                P.wait("scalar", "vector", vqr)
            ahe = P.emit("scalar", lambda s, psm=psm: s.copy(
                out=h_t, in_=psm[:, 0:256]), inc=1)
            ahel.append(ahe)
            P.wait("vector", "scalar", ahe)
            P.emit("vector", lambda v, fc=fc: v.tensor_scalar(
                out=a_t, in0=h_t, scalar1=0.1, scalar2=thc[:, fc:fc + 1],
                op0=ALU.mult, op1=ALU.add))
            va = P.cnt["vector"]
            P.wait("scalar", "vector", va)
            as2 = P.emit("scalar", lambda s: s.activation(
                out=s2_t, in_=a_t, func=AF.Sin, scale=0.5), inc=1)
            as4 = P.emit("scalar", lambda s: s.activation(
                out=s4_t, in_=a_t, func=AF.Sin, scale=0.25), inc=1)
            P.wait("vector", "scalar", as4)
            P.emit("vector", lambda v: v.tensor_tensor(
                out=c2_t, in0=s4_t, in1=s4_t, op=ALU.mult))
            P.emit("vector", lambda v: v.tensor_scalar(
                out=c2_t, in0=c2_t, scalar1=-2.0, scalar2=1.0,
                op0=ALU.mult, op1=ALU.add))
            P.emit("vector", lambda v: v.tensor_tensor(
                out=c2_t, in0=c2_t, in1=s2_t, op=ALU.mult))
            P.emit("vector", lambda v, fc=fc: v.scalar_tensor_tensor(
                out=c2_t, in0=c2_t, scalar=a2c[:, fc:fc + 1], in1=h_t,
                op0=ALU.mult, op1=ALU.add))
            vqr = P.emit("vector", lambda v, fc=fc: v.tensor_scalar(
                out=qr_s[:, fc * 256:(fc + 1) * 256], in0=c2_t, scalar1=0.0,
                scalar2=None, op0=ALU.max), inc=1)
        P.wait("tensor", "vector", P.cnt["vector"])
        asll = []
        for oc in range(8):
            psm = PST[oc % 2]
            if oc >= 2:
                P.wait("tensor", "scalar", asll[oc - 2])
            for fc in range(32):
                mm(P, psm[:, 0:256],
                   w2_s[:, fc * D + oc * 128: fc * D + oc * 128 + 128],
                   qr_s[:, fc * 256:(fc + 1) * 256], fc == 0, False)
            ow = mm(P, psm[:, 0:256], b2row[0:1, oc * 128:(oc + 1) * 128],
                    onef[0:1, 0:256], False, True, inc=1)
            P.wait("scalar", "tensor", ow)
            if oc >= 1:
                P.wait("scalar", "vector", vtr[-1])
            asl = P.emit("scalar", lambda s, psm=psm: s.activation(
                out=g1_t, in_=psm[:, 0:256], func=AF.Silu), inc=1)
            asll.append(asl)
            P.wait("sync", "vector", P.cnt["vector"])
            ws1 = P.dma_ch("sync", "ld0", s1t2, s1f[oc, :, t0:t0 + 256])
            P.wait_ch("vector", "ld0", ws1)
            P.wait("vector", "scalar", asl)
            if oc == 0:
                vtr = []
            vtr.append(P.emit("vector", lambda v: v.tensor_tensor(
                out=tr_t, in0=g1_t, in1=s1t2, op=ALU.mult), inc=1))
            P.wait("tensor", "vector", vtr[-1])
            for ts_ in range(2):
                regt = pbig[:, (ts_ % 2) * 512:(ts_ % 2) * 512 + 128]
                tw = transp(P, regt, tr_t[:, ts_ * 128:(ts_ + 1) * 128], idf)
                P.wait("vector", "tensor", tw)
                P.emit("vector", lambda v, regt=regt, ts_=ts_, oc=oc: v.tensor_copy(
                    out=oasm[:, ts_ * D + oc * 128: ts_ * D + oc * 128 + 128],
                    in_=regt), inc=1)
            P.wait("tensor", "vector", P.cnt["vector"])
        for ts_ in range(2):
            tg = t0 + ts_ * 128
            bb_, jj = tg // JPC, tg % JPC
            P.wait("sync", "vector", P.cnt["vector"])
            wx1 = P.dma_ch("sync", "ld1", x1t, xs_in[bb_, jj:jj + 128, :])
            P.wait_ch("vector", "ld1", wx1)
            vo = P.emit("vector", lambda v, ts_=ts_: v.tensor_tensor(
                out=oasm[:, ts_ * D:(ts_ + 1) * D],
                in0=oasm[:, ts_ * D:(ts_ + 1) * D], in1=x1t, op=ALU.add), inc=1)
            P.wait("sync", "vector", vo)
            dma(P, "sync", out_d[bb_, jj:jj + 128, :], oasm[:, ts_ * D:(ts_ + 1) * D])
        P.wait("vector", "sync", P.cnt["sync"])
    A.release()
    A.release()


def build_l1(debug=False):
    env = _mk_env()
    P, nc, ctx = env["P"], env["nc"], env["ctx"]
    emit_l1(env, debug=debug)
    fin = P.cnt["sync"]
    P.wait("sync", "sync", fin)
    P.wait("gpsimd", "sync", fin)
    block = ctx.enter_context(nc.Block())
    P.finalize(block)
    ctx.close()
    return nc


def build_fused(debug=False):
    env = _mk_env()
    P, nc, ctx = env["P"], env["nc"], env["ctx"]
    x1loc = nc.dram_tensor("x1loc", [B, JPC, D], F32, kind="Internal").ap()
    shared_d = emit_l0(env, debug=debug, x1loc=x1loc)
    P.barrier()
    emit_l1(env, debug=debug, xs_src=x1loc, shared_d=shared_d)
    fin = P.cnt["sync"]
    P.wait("sync", "sync", fin)
    P.wait("gpsimd", "gpsimd", P.cnt["gpsimd"])
    P.wait("gpsimd", "sync", fin)
    block = ctx.enter_context(nc.Block())
    P.finalize(block)
    ctx.close()
    return nc


def _l0_maps(inputs):
    import ml_dtypes
    f = np.float32
    c = np.ascontiguousarray
    g = lambda k: np.asarray(inputs[k], dtype=f)
    shared = dict(
        wqt=c(g("L0_attn_Wq").T), wkt=c(g("L0_attn_Wk").T),
        wvt=c(g("L0_attn_Wv").T), wot=c(g("L0_attn_Wo").T),
        bq=g("L0_attn_bq").reshape(1, D), bk=g("L0_attn_bk").reshape(1, D),
        bv=g("L0_attn_bv").reshape(1, D), bo=g("L0_attn_bo").reshape(1, D),
        wqst=c(g("L0_attn_Wqs").T), wkst=c(g("L0_attn_Wks").T),
        wvst=c(g("L0_attn_Wvs").T),
        bqs=g("L0_attn_bqs").reshape(1, 3), bks=g("L0_attn_bks").reshape(1, 3),
        bvs=g("L0_attn_bvs").reshape(1, 3),
        temp=g("L0_attn_temp").reshape(1, 1),
        rw=c(g("L0_ffn_router_w")), rb=g("L0_ffn_router_b").reshape(1, E),
        w1=c(g("L0_ffn_w1")), b1r=c(g("L0_ffn_b1").reshape(E, 1, FF1)),
        w2=c(g("L0_ffn_w2")), b2r=c(g("L0_ffn_b2").reshape(E, 1, D)),
        iden=np.eye(128, dtype=ml_dtypes.bfloat16),
        idenf=np.eye(128, dtype=f),
        onesf=np.ones((1, 512), f),
        iota=np.broadcast_to(np.arange(E, dtype=f), (128, E)).copy(),
        iotam=np.broadcast_to(np.arange(E, dtype=f) - 999.0, (128, E)).copy(),
    )
    x = np.asarray(inputs["L0_attn_x"], dtype=f)
    return [dict(shared, x=c(np.roll(x, -i * JPC, axis=1)))
            for i in range(NCORES)]


def _l1_maps(inputs, x1=None):
    f = np.float32
    c = np.ascontiguousarray
    g = lambda k: np.asarray(inputs[k], dtype=f)
    fm8 = lambda k: c(g(k).reshape(8, 128).T)
    fm32 = lambda k: c(g(k).reshape(32, 128).T)
    shared = dict(
        ln1g=g("L1_attn_ln1_g").reshape(1, D), ln1b=g("L1_attn_ln1_b").reshape(1, D),
        ln2gf=fm8("L1_attn_ln2_g"), ln2bf=fm8("L1_attn_ln2_b"),
        tmkf=fm8("L1_attn_time_mix_k"), tmvf=fm8("L1_attn_time_mix_v"),
        tmrf=fm8("L1_attn_time_mix_r"), tdf=fm8("L1_attn_time_decay"),
        tff=fm8("L1_attn_time_first"),
        cmkf=fm8("L1_attn_cmix_k"), cmrf=fm8("L1_attn_cmix_r"),
        wkt1=c(g("L1_attn_Wk").T), wvt1=c(g("L1_attn_Wv").T),
        wrt1=c(g("L1_attn_Wr").T), wot1=c(g("L1_attn_Wo").T),
        wckt=c(g("L1_attn_Wck").T), wcvt=c(g("L1_attn_Wcv").T),
        wcrt=c(g("L1_attn_Wcr").T),
        w1f=c(g("L1_ffn_W1")), w2f=c(g("L1_ffn_W2")),
        b1f=g("L1_ffn_b1").reshape(1, FF2), b2f=g("L1_ffn_b2").reshape(1, D),
        qphf=fm32("L1_ffn_quantum_phases"),
        qwrf=fm32("L1_ffn_quantum_weights_real"),
        qwif=fm32("L1_ffn_quantum_weights_imag"),
        idenf=np.eye(128, dtype=f),
        onesf=np.ones((1, 512), f),
        onescol=np.ones((128, 1), f),
    )
    maps = []
    for core in range(NCORES):
        m = dict(shared)
        if x1 is not None:
            m["xs"] = c(x1[:, core * JPC:(core + 1) * JPC, :])
        maps.append(m)
    return maps


def kernel(**inputs):
    maps0 = _l0_maps(inputs)
    maps1 = _l1_maps(inputs)
    maps = [dict(m0, **m1) for m0, m1 in zip(maps0, maps1)]
    nc = build_fused()
    res = run_bass_kernel_spmd(nc, maps, core_ids=list(range(NCORES)))
    out = np.concatenate([r["out"] for r in res.results], axis=1)
    return out


def kernel_unfused(**inputs):
    maps0 = _l0_maps(inputs)
    nc0 = build_l0()
    res0 = run_bass_kernel_spmd(nc0, maps0, core_ids=list(range(NCORES)))
    maps1 = _l1_maps(inputs)
    for i in range(NCORES):
        maps1[i]["xs"] = np.ascontiguousarray(res0.results[i]["x1"])
    nc1 = build_l1()
    res1 = run_bass_kernel_spmd(nc1, maps1, core_ids=list(range(NCORES)))
    out = np.concatenate([r["out"] for r in res1.results], axis=1)
    return out


# version 04
# speedup vs baseline: 1.8462x; 1.8462x over previous
import numpy as np
from contextlib import ExitStack

import concourse.bass as bass
import concourse.mybir as mybir
from concourse.bass_utils import run_bass_kernel_spmd

F32 = mybir.dt.float32
BF16 = mybir.dt.bfloat16
AF = mybir.ActivationFunctionType
ALU = mybir.AluOpType
AX = mybir.AxisListType

NCORES = 8
B, T, D = 16, 2048, 1024
BPC = B // NCORES
JPC = T // NCORES
H, HD = 16, 64
E, FF1 = 16, 2048
FF2 = 4096
INV_D = 1.0 / D
EPS = 1e-5

ENGS = ("sync", "tensor", "vector", "scalar", "gpsimd")
ARENA_COLS = 53056


class Arena:
    def __init__(self, ap_f32):
        self.f = ap_f32
        self.bf = ap_f32.bitcast(BF16)
        self.off = 0
        self.marks = []

    def mark(self):
        self.marks.append(self.off)

    def release(self):
        self.off = self.marks.pop()

    def f32(self, parts, cols):
        off = (self.off + 3) & ~3
        self.off = off + 4 * cols
        assert self.off <= ARENA_COLS * 4, "arena overflow"
        return self.f[0:parts, off // 4: off // 4 + cols]

    def b16(self, parts, cols):
        off = (self.off + 3) & ~3
        self.off = off + 2 * cols
        self.off = (self.off + 3) & ~3
        assert self.off <= ARENA_COLS * 4, "arena overflow"
        return self.bf[0:parts, off // 2: off // 2 + cols]


class Plan:
    def __init__(self, nc, ctx):
        self.nc = nc
        self.ops = {e: [] for e in ENGS}
        self.sem = {e: ctx.enter_context(nc.semaphore("sem_" + e)) for e in ENGS}
        self.cnt = {e: 0 for e in ENGS}
        self.bsem = {e: ctx.enter_context(nc.semaphore("bar_" + e))
                     for e in ("sync", "gpsimd")}
        self.bcnt = {e: 0 for e in ("sync", "gpsimd")}
        self.ch = {c: ctx.enter_context(nc.semaphore("ch_" + c))
                   for c in ("ld0", "ld1")}
        self.chcnt = {c: 0 for c in self.ch}

    def dma_ch(self, queue, chan, out, in_, **kw):
        sem = self.ch[chan]
        prev = self.chcnt[chan]
        if prev:
            self.ops[queue].append(
                lambda e, sem=sem, prev=prev: e.wait_ge(sem, prev))
        self.chcnt[chan] += 16

        def fn(e, out=out, in_=in_, kw=kw, sem=sem):
            e.dma_start(out=out, in_=in_, **kw).then_inc(sem, 16)

        self.ops[queue].append(fn)
        return self.chcnt[chan]

    def wait_ch(self, eng, chan, value):
        if value <= 0:
            return
        sem = self.ch[chan]
        self.ops[eng].append(lambda e, sem=sem, value=value: e.wait_ge(sem, value))

    def emit(self, eng, fn, inc=0):
        if eng == "vector":
            sem = self.sem[eng]
            prev = self.cnt[eng]
            if prev:
                self.ops[eng].append(
                    lambda e, sem=sem, prev=prev: e.wait_ge(sem, prev))
            self.cnt[eng] += 1
            self.ops[eng].append(
                lambda e, fn=fn, sem=sem: fn(e).then_inc(sem, 1))
            return self.cnt[eng]
        if inc:
            sem = self.sem[eng]
            prev = self.cnt[eng]
            if eng in ("tensor", "scalar") and prev:
                self.ops[eng].append(
                    lambda e, sem=sem, prev=prev: e.wait_ge(sem, prev))
            self.cnt[eng] += inc

            def wrapped(e, fn=fn, sem=sem, inc=inc):
                fn(e).then_inc(sem, inc)

            self.ops[eng].append(wrapped)
            return self.cnt[eng]
        self.ops[eng].append(fn)
        return self.cnt[eng]

    def wait(self, eng, producer_eng, value):
        if value <= 0:
            return
        if producer_eng in ("sync", "gpsimd"):
            value = self.cnt[producer_eng]
        sem = self.sem[producer_eng]
        self.ops[eng].append(lambda e, sem=sem, value=value: e.wait_ge(sem, value))

    def barrier(self):
        for eng in ENGS:
            if eng in self.bsem:
                sem_d = self.sem[eng]
                val_d = self.cnt[eng]
                if val_d:
                    self.ops[eng].append(
                        lambda e, sem=sem_d, val=val_d: e.wait_ge(sem, val))
                self.bcnt[eng] += 1
                sem = self.bsem[eng]
                self.ops[eng].append(
                    lambda e, sem=sem: e.nop().then_inc(sem, 1))
            else:
                self.emit(eng, lambda e: e.nop(), inc=1)
        snap = dict(self.cnt)
        bsnap = dict(self.bcnt)
        for eng in ENGS:
            for other in ENGS:
                if other == eng:
                    continue
                if other in self.bsem:
                    sem = self.bsem[other]
                    val = bsnap[other]
                    self.ops[eng].append(
                        lambda e, sem=sem, val=val: e.wait_ge(sem, val))
                else:
                    self.wait(eng, other, snap[other])

    def finalize(self, block):
        for eng in ENGS:
            ops = self.ops[eng]
            getattr(block, eng)(lambda engobj, ops=ops: [fn(engobj) for fn in ops])


def dma(P, queue, out, in_, **kw):
    P.wait(queue, queue, P.cnt[queue])
    return P.emit(
        queue, lambda e, out=out, in_=in_, kw=kw: e.dma_start(out=out, in_=in_, **kw),
        inc=16)


def mm(P, out, lhsT, rhs, start, stop, inc=0):
    return P.emit("tensor", lambda te, out=out, l=lhsT, r=rhs, a=start, b=stop:
                  te.matmul(out, l, r, start=a, stop=b), inc=inc)


def transp(P, out, in_, iden, inc=1):
    return P.emit("tensor", lambda te, out=out, i=in_, d=iden:
                  te.transpose(out, i, d), inc=inc)


def _mk_env():
    nc = bass.Bass("TRN2", target_bir_lowering=False)
    ctx = ExitStack()
    arena_t = ctx.enter_context(nc.sbuf_tensor("arena", [128, ARENA_COLS], F32))
    A = Arena(arena_t[:, :])
    pbig = ctx.enter_context(nc.psum_tensor("pbig", [128, 2048], F32))
    pst0 = ctx.enter_context(nc.psum_tensor("pst0", [128, 512], F32))
    pst1 = ctx.enter_context(nc.psum_tensor("pst1", [128, 512], F32))
    ppv = ctx.enter_context(nc.psum_tensor("ppv", [128, 512], F32))
    pbf = ctx.enter_context(nc.psum_tensor("pbf", [128, 512], BF16))
    P = Plan(nc, ctx)
    return dict(nc=nc, ctx=ctx, A=A, P=P, pbig=pbig, pst0=pst0, pst1=pst1,
                ppv=ppv, pbf=pbf)


def emit_l0(env, debug=False, x1loc=None):
    nc, ctx, A, P = env["nc"], env["ctx"], env["A"], env["P"]
    pbig, pst0, pst1, ppv, pbf = (env["pbig"], env["pst0"], env["pst1"],
                                  env["ppv"], env["pbf"])
    di = lambda n, s, d=F32: nc.dram_tensor(n, s, d, kind="ExternalInput").ap()

    x_in = di("x", [B, T, D])
    wqt, wkt, wvt, wot = (di(n, [D, D]) for n in ("wqt", "wkt", "wvt", "wot"))
    bq, bk, bv, bo = (di(n, [1, D]) for n in ("bq", "bk", "bv", "bo"))
    wqst, wkst, wvst = (di(n, [HD, 3]) for n in ("wqst", "wkst", "wvst"))
    bqs, bks, bvs = (di(n, [1, 3]) for n in ("bqs", "bks", "bvs"))
    temp = di("temp", [1, 1])
    rw_d = di("rw", [D, E])
    rb_d = di("rb", [1, E])
    w1_d = di("w1", [E, D, FF1])
    b1_d = di("b1r", [E, 1, FF1])
    w2_d = di("w2", [E, FF1, D])
    b2_d = di("b2r", [E, 1, D])
    idb_d = di("iden", [128, 128], BF16)
    idf_d = di("idenf", [128, 128])
    onef_d = di("onesf", [1, 512])
    iota_d = di("iota", [128, E])
    iotam_d = di("iotam", [128, E])

    if x1loc is None:
        x1loc = nc.dram_tensor("x1", [B, JPC, D], F32, kind="ExternalOutput").ap()

    def scr(name, shape, dtyp=F32):
        kind = "ExternalOutput" if debug else "Internal"
        return nc.dram_tensor(name, shape, dtyp, kind=kind).ap()

    xnbf = scr("xnbf", [B, 8, 128, T], BF16)
    stm = scr("stm", [B, JPC, D])
    cdr = scr("cdr", [B, E, JPC])
    PST = (pst0, pst1)
    NTL = JPC // 128

    A.mark()
    idb = A.b16(128, 128)
    idf = A.f32(128, 128)
    onef = A.f32(1, 512)
    iota_s = A.f32(128, E)
    iotam_s = A.f32(128, E)
    tempv = A.f32(128, 1)
    rw_s = A.f32(128, 8 * E)
    rb_s = A.f32(1, E)
    bq_s = A.f32(1, D)
    bk_s = A.f32(1, D)
    bv_s = A.f32(1, D)
    bo_s = A.f32(1, D)
    wqs_s = A.b16(HD, 3)
    wks_s = A.b16(HD, 3)
    wvs_s = A.b16(HD, 3)
    bqs_s = A.f32(1, 3)
    bks_s = A.f32(1, 3)
    bvs_s = A.f32(1, 3)
    eps_c = A.f32(128, 1)

    dma(P, "sync", idb, idb_d)
    dma(P, "sync", idf, idf_d)
    dma(P, "sync", onef, onef_d)
    dma(P, "sync", iota_s, iota_d)
    dma(P, "sync", iotam_s, iotam_d)
    dma(P, "sync", tempv, temp.partition_broadcast(128))
    dma(P, "sync", rb_s, rb_d)
    for dc in range(8):
        dma(P, "sync", rw_s[:, dc * E:(dc + 1) * E], rw_d[dc * 128:(dc + 1) * 128, :])
    dma(P, "sync", bq_s, bq)
    dma(P, "sync", bk_s, bk)
    dma(P, "sync", bv_s, bv)
    dma(P, "sync", bo_s, bo)
    dma(P, "gpsimd", wqs_s, wqst)
    dma(P, "gpsimd", wks_s, wkst)
    dma(P, "gpsimd", wvs_s, wvst)
    dma(P, "sync", bqs_s, bqs)
    dma(P, "sync", bks_s, bks)
    w_pro = dma(P, "sync", bvs_s, bvs)
    g_pro = P.cnt["gpsimd"]
    P.wait("vector", "sync", w_pro)
    P.wait("vector", "gpsimd", g_pro)
    P.emit("vector", lambda v: v.tensor_scalar(
        out=bqs_s, in0=bqs_s, scalar1=tempv[0:1, 0:1], scalar2=None, op0=ALU.mult))
    P.emit("vector", lambda v: v.memset(eps_c, EPS))
    P.barrier()

    A.mark()
    xt_b = [A.f32(128, D) for _ in range(4)]
    xn_b = [A.f32(128, D) for _ in range(4)]
    xsq = A.f32(128, D)
    xnf = A.f32(128, D)
    stat = A.f32(128, 16)
    sd4 = A.f32(128, 4)
    rstd4 = A.f32(128, 4)
    msq4 = A.f32(128, 4)
    asm_b = A.b16(128, 8 * 512)
    Lb = A.f32(128, NTL * E)
    curb = A.f32(128, NTL * E)
    eqb = A.f32(128, NTL * E)
    ub = eqb
    umb = A.f32(128, NTL * E)
    maskb = A.f32(128, NTL * E)
    mx0 = A.f32(128, NTL)
    nmx0 = A.f32(128, NTL)
    mxr = A.f32(128, NTL)
    selb = A.f32(128, NTL)
    den16 = A.f32(128, NTL)
    rden = A.f32(128, NTL)
    ctmp = A.f32(16, 128)
    ctile = A.f32(128, E)
    Sl = lambda t_, ti: t_[:, ti * E:(ti + 1) * E]

    for b in range(B):
        prev_xn_reads = 0
        prev_sq = 0
        prev_lg = 0
        prev_vxn = 0
        prev_asm = 0
        for g in range(4):
            tis = [g * 4 + i for i in range(4)]
            P.wait("sync", "vector", prev_vxn)
            P.wait("sync", "scalar", prev_sq)
            w_x = 0
            for i, ti in enumerate(tis):
                w_x = dma(P, "sync", xt_b[i], x_in[b, ti * 128:(ti + 1) * 128, :])
            P.wait("vector", "sync", w_x)
            P.wait("scalar", "sync", w_x)
            for i in range(4):
                P.emit("vector", lambda v, i=i: v.reduce_sum(
                    out=stat[:, i:i + 1], in_=xt_b[i], axis=AX.X))
            vsq = []
            for i in range(4):
                if i >= 1:
                    P.wait("scalar", "vector", vsq[i - 1])
                sqw = P.emit("scalar", lambda s, i=i: s.activation(
                    out=xsq, in_=xt_b[i], func=AF.Square), inc=1)
                P.wait("vector", "scalar", sqw)
                vsq.append(P.emit("vector", lambda v, i=i: v.reduce_sum(
                    out=stat[:, 4 + i:5 + i], in_=xsq, axis=AX.X), inc=1))
            prev_sq = P.cnt["scalar"]
            vm = 0
            for i in range(4):
                vm = P.emit("vector", lambda v, i=i: v.tensor_scalar(
                    out=stat[:, 8 + i:9 + i], in0=stat[:, i:i + 1], scalar1=INV_D,
                    scalar2=None, op0=ALU.mult), inc=1)
            P.wait("scalar", "vector", vm)
            am = P.emit("scalar", lambda s: s.activation(
                out=msq4, in_=stat[:, 8:12], func=AF.Square), inc=1)
            P.wait("vector", "scalar", am)
            vnv = 0
            for i in range(4):
                vnv = P.emit("vector", lambda v, i=i: v.scalar_tensor_tensor(
                    out=stat[:, 12 + i:13 + i], in0=stat[:, 4 + i:5 + i],
                    scalar=INV_D, in1=msq4[:, i:i + 1],
                    op0=ALU.mult, op1=ALU.subtract), inc=1)
            P.wait("scalar", "vector", vnv)
            asd = P.emit("scalar", lambda s: s.activation(
                out=sd4, in_=stat[:, 12:16], func=AF.Sqrt, bias=eps_c), inc=1)
            P.wait("vector", "scalar", asd)
            P.wait("vector", "tensor", prev_xn_reads)
            P.wait("vector", "sync", prev_asm)
            P.emit("vector", lambda v: v.reciprocal(out=rstd4, in_=sd4))
            for i in range(4):
                P.emit("vector", lambda v, i=i: v.tensor_scalar(
                    out=xn_b[i], in0=xt_b[i], scalar1=stat[:, 8 + i:9 + i],
                    scalar2=None, op0=ALU.subtract))
            vxn = 0
            for i in range(4):
                vxn = P.emit("vector", lambda v, i=i: v.tensor_scalar(
                    out=xn_b[i], in0=xn_b[i], scalar1=rstd4[:, i:i + 1],
                    scalar2=None, op0=ALU.mult), inc=1)
            prev_vxn = vxn

            for i, ti in enumerate(tis):
                P.wait("tensor", "vector", vxn)
                P.wait("tensor", "scalar", P.cnt["scalar"])
                P.wait("tensor", "vector", P.cnt["vector"])
                is_loc = ti < NTL
                if is_loc:
                    P.wait("scalar", "tensor", prev_lg)
                evs = []
                for dc in range(8):
                    reg = pbig[:, (dc % 2) * 512:(dc % 2) * 512 + 128]
                    if dc >= 2:
                        P.wait("tensor", "scalar", evs[dc - 2])
                        P.wait("tensor", "vector", P.cnt["vector"])
                    tw = transp(P, reg, xn_b[i][:, dc * 128:(dc + 1) * 128], idf)
                    P.wait("scalar", "tensor", tw)
                    sf = P.emit("scalar", lambda s, dc=dc, reg=reg: s.copy(
                        out=xnf[:, dc * 128:(dc + 1) * 128], in_=reg), inc=1)
                    evs.append(sf)
                    P.wait("vector", "scalar", sf)
                    P.emit("vector", lambda v, dc=dc, i=i: v.tensor_copy(
                        out=asm_b[:, dc * 512 + i * 128: dc * 512 + i * 128 + 128],
                        in_=xnf[:, dc * 128:(dc + 1) * 128]))
                if is_loc:
                    lreg = pbig[:, 1920:1920 + E]
                    P.wait("tensor", "scalar", evs[-1])
                    for dc in range(8):
                        mm(P, lreg, xnf[:, dc * 128:(dc + 1) * 128],
                           rw_s[:, dc * E:(dc + 1) * E], dc == 0, False)
                    lg = mm(P, lreg, onef[0:1, 0:128], rb_s, False, True, inc=1)
                    prev_lg = lg
                    P.wait("vector", "tensor", lg)
                    P.emit("vector", lambda v, ti=ti, lreg=lreg: v.tensor_copy(
                        out=Sl(Lb, ti), in_=lreg), inc=1)
                prev_xn_reads = P.cnt["tensor"]
            P.wait("sync", "vector", P.cnt["vector"])
            for dc in range(8):
                prev_asm = dma(P, "sync", xnbf[b, dc, :, g * 512:(g + 1) * 512],
                               asm_b[:, dc * 512:(dc + 1) * 512])

        P.wait("vector", "vector", P.cnt["vector"])
        P.emit("vector", lambda v: v.tensor_copy(out=curb, in_=Lb))
        for ti in range(NTL):
            P.emit("vector", lambda v, ti=ti: v.reduce_max(
                out=nmx0[:, ti:ti + 1], in_=Sl(Lb, ti), axis=AX.X, negate=True))
        vtk = 0
        for r in range(4):
            for ti in range(NTL):
                P.emit("vector", lambda v, ti=ti: v.reduce_max(
                    out=mxr[:, ti:ti + 1], in_=Sl(curb, ti), axis=AX.X))
            if r == 0:
                P.emit("vector", lambda v: v.tensor_copy(out=mx0, in_=mxr))
            for ti in range(NTL):
                P.emit("vector", lambda v, ti=ti: v.tensor_scalar(
                    out=Sl(eqb, ti), in0=Sl(curb, ti), scalar1=mxr[:, ti:ti + 1],
                    scalar2=None, op0=ALU.is_equal))
            for ti in range(NTL):
                P.emit("vector", lambda v, ti=ti: v.tensor_tensor(
                    out=Sl(umb, ti), in0=Sl(eqb, ti), in1=iotam_s, op=ALU.mult))
            for ti in range(NTL):
                P.emit("vector", lambda v, ti=ti: v.tensor_scalar(
                    out=Sl(umb, ti), in0=Sl(umb, ti), scalar1=999.0, scalar2=None,
                    op0=ALU.add))
            for ti in range(NTL):
                P.emit("vector", lambda v, ti=ti: v.tensor_reduce(
                    out=selb[:, ti:ti + 1], in_=Sl(umb, ti), axis=AX.X, op=ALU.min))
            for ti in range(NTL):
                P.emit("vector", lambda v, ti=ti: v.tensor_scalar(
                    out=Sl(eqb, ti), in0=iota_s, scalar1=selb[:, ti:ti + 1],
                    scalar2=None, op0=ALU.is_equal))
            if r == 0:
                P.emit("vector", lambda v: v.tensor_copy(out=maskb, in_=eqb))
            else:
                vtk = P.emit("vector", lambda v: v.tensor_tensor(
                    out=maskb, in0=maskb, in1=eqb, op=ALU.add),
                    inc=(1 if r == 3 else 0))
            if r < 3:
                for ti in range(NTL):
                    P.emit("vector", lambda v, ti=ti: v.scalar_tensor_tensor(
                        out=Sl(curb, ti), in0=Sl(eqb, ti), scalar=-1e30,
                        in1=Sl(curb, ti), op0=ALU.mult, op1=ALU.add))
        P.wait("scalar", "vector", vtk)
        au = 0
        for ti in range(NTL):
            au = P.emit("scalar", lambda s, ti=ti: s.activation(
                out=Sl(ub, ti), in_=Sl(Lb, ti), func=AF.Exp,
                bias=nmx0[:, ti:ti + 1], scale=1.0), inc=1)
        P.wait("vector", "scalar", au)
        P.emit("vector", lambda v: v.tensor_tensor(out=ub, in0=ub, in1=maskb,
                                                   op=ALU.mult))
        for ti in range(NTL):
            P.emit("vector", lambda v, ti=ti: v.reduce_sum(
                out=den16[:, ti:ti + 1], in_=Sl(ub, ti), axis=AX.X))
        P.emit("vector", lambda v: v.reciprocal(out=rden, in_=den16))
        prev_ct = 0
        for ti in range(NTL):
            P.wait("vector", "gpsimd", prev_ct)
            vc = P.emit("vector", lambda v, ti=ti: v.tensor_scalar(
                out=ctile, in0=Sl(ub, ti), scalar1=rden[:, ti:ti + 1],
                scalar2=None, op0=ALU.mult), inc=1)
            P.wait("tensor", "vector", vc)
            tc = transp(P, pbig[0:E, 512:640], ctile, idf)
            P.wait("vector", "tensor", tc)
            vt2 = P.emit("vector", lambda v: v.tensor_copy(
                out=ctmp, in_=pbig[0:E, 512:640]), inc=1)
            P.wait("gpsimd", "vector", vt2)
            prev_ct = dma(P, "gpsimd", cdr[b, :, ti * 128:(ti + 1) * 128], ctmp)
        P.wait("vector", "gpsimd", prev_ct)
    P.barrier()
    A.release()

    A.mark()
    xnb = A.b16(128, 8 * T)
    ofm = A.b16(128, 8 * JPC)
    vt = A.b16(128, 16 * 65)
    pt = [A.b16(128, 256) for _ in range(2)]
    nmax = A.f32(128, 2)
    nm2 = A.f32(2, 128)
    nmrow = A.f32(1, JPC)
    wq_s = A.b16(128, 8 * D)
    wk_s = A.b16(128, 8 * D)
    wv_s = A.b16(128, 8 * D)
    wo_s = A.b16(128, 8 * D)
    q_sb = A.b16(64, 2 * JPC)
    k_sb = A.b16(64, 2 * T)
    v_sb = A.b16(64, 2 * T)
    k3 = A.f32(3, T)
    v3 = A.f32(3, T)
    r1_ = A.f32(3, T)
    r2_ = A.f32(3, T)
    r3_ = A.f32(3, T)
    r4_ = A.f32(3, T)
    cross_b = A.b16(3, T)
    qstp_b = A.b16(3, JPC)
    rc_row = A.f32(1, 256)
    ot_s = A.b16(64, 256)
    sf_s = A.f32(128, 256)
    stma = A.f32(128, 2 * D)

    P.emit("vector", lambda v: v.memset(
        vt.rearrange("p (k c) -> p k c", c=65)[:, :, 64:65], 1.0))
    wW = 0
    for dc in range(8):
        for (wsrc, wdst) in ((wqt, wq_s), (wkt, wk_s), (wvt, wv_s), (wot, wo_s)):
            wW = dma(P, "gpsimd", wdst[:, dc * D:(dc + 1) * D],
                     wsrc[dc * 128:(dc + 1) * 128, :])

    for b in range(B):
        P.wait("sync", "tensor", P.cnt["tensor"])
        P.wait("sync", "vector", P.cnt["vector"])
        wxb = 0
        for dc in range(8):
            wxb = dma(P, "sync", xnb[:, dc * T:(dc + 1) * T], xnbf[b, dc, :, :])
        for fc in range(8):
            if b == 0 and fc == 0:
                P.wait("tensor", "gpsimd", wW)
            P.wait("tensor", "sync", wxb)
            P.wait("tensor", "vector", P.cnt["vector"])
            for (wfull, dst, brow) in ((wk_s, k_sb, bk_s), (wv_s, v_sb, bv_s)):
                evl = 0
                for tg in range(4):
                    reg = pbig[:, tg * 512:(tg + 1) * 512]
                    for dc in range(8):
                        mm(P, reg, wfull[:, dc * D + fc * 128: dc * D + fc * 128 + 128],
                           xnb[:, dc * T + tg * 512: dc * T + tg * 512 + 512],
                           dc == 0, False)
                    w = mm(P, reg, brow[0:1, fc * 128:(fc + 1) * 128],
                           onef[0:1, 0:512], False, True, inc=1)
                    P.wait("vector", "tensor", w)
                    for hh2 in range(2):
                        evl = P.emit("vector",
                                     lambda v, reg=reg, dst=dst, tg=tg, hh2=hh2:
                                     v.tensor_copy(
                                         out=dst[0:64, hh2 * T + tg * 512:
                                                 hh2 * T + (tg + 1) * 512],
                                         in_=reg[64 * hh2:64 * hh2 + 64, :]), inc=1)
                P.wait("tensor", "vector", evl)
            reg = pbig[:, 0:256]
            for dc in range(8):
                mm(P, reg, wq_s[:, dc * D + fc * 128: dc * D + fc * 128 + 128],
                   xnb[:, dc * T: dc * T + 256], dc == 0, False)
            w = mm(P, reg, bq_s[0:1, fc * 128:(fc + 1) * 128],
                   onef[0:1, 0:256], False, True, inc=1)
            P.wait("vector", "tensor", w)
            for hh2 in range(2):
                P.emit("vector", lambda v, reg=reg, hh2=hh2: v.tensor_copy(
                    out=q_sb[0:64, hh2 * JPC: hh2 * JPC + 256],
                    in_=reg[64 * hh2:64 * hh2 + 64, :]))
            evq = P.emit("vector", lambda v: v.tensor_scalar(
                out=q_sb[0:64, :], in0=q_sb[0:64, :], scalar1=tempv[0:64, 0:1],
                scalar2=None, op0=ALU.mult), inc=1)
            P.wait("tensor", "vector", evq)
            for hh in range(2):
                h, ho, hq = 2 * fc + hh, hh * T, hh * JPC
                for (wst, bst, src_, soff, swid, dst3) in (
                        (wqs_s, bqs_s, q_sb, hq, JPC, qstp_b),
                        (wks_s, bks_s, k_sb, ho, T, k3),
                        (wvs_s, bvs_s, v_sb, ho, T, v3)):
                    stw = 0
                    for tg in range(swid // 512 if swid >= 512 else 1):
                        wdt = min(512, swid)
                        reg3 = pbig[0:3, tg * 512:tg * 512 + wdt]
                        mm(P, reg3, wst, src_[0:64, soff + tg * 512:
                                              soff + tg * 512 + wdt], True, False)
                        stw = mm(P, reg3, bst, onef[0:1, 0:wdt], False, True,
                                 inc=(1 if tg == max(0, swid // 512 - 1) else 0))
                    P.wait("vector", "tensor", stw)
                    svw = P.emit("vector", lambda v, dst3=dst3, swid=swid:
                                 v.tensor_copy(out=dst3, in_=pbig[0:3, 0:swid]),
                                 inc=1)
                    P.wait("tensor", "vector", svw)
                vk = P.cnt["vector"]
                P.wait("gpsimd", "vector", vk)
                rw_ = 0
                for (srt, dstp, sh) in ((k3, r1_, 1), (k3, r2_, 2),
                                        (v3, r3_, 1), (v3, r4_, 2)):
                    rw_ = dma(P, "gpsimd", dstp[0:3 - sh, :], srt[sh:3, :])
                    rw_ = dma(P, "gpsimd", dstp[3 - sh:3, :], srt[0:sh, :])
                P.wait("vector", "gpsimd", rw_)
                P.emit("vector", lambda v: v.tensor_tensor(
                    out=r1_, in0=r1_, in1=r4_, op=ALU.mult))
                P.emit("vector", lambda v: v.tensor_tensor(
                    out=r2_, in0=r2_, in1=r3_, op=ALU.mult))
                vcr = P.emit("vector", lambda v: v.tensor_tensor(
                    out=cross_b, in0=r1_, in1=r2_, op=ALU.subtract), inc=1)
                P.wait("gpsimd", "vector", vcr)
                gq = P.cnt["gpsimd"]
                P.wait("tensor", "vector", P.cnt["vector"])
                P.wait("tensor", "gpsimd", gq)
                for qt in range(2):
                    sw = 0
                    for kg in range(4):
                        reg = pbig[:, kg * 512:(kg + 1) * 512]
                        mm(P, reg, q_sb[0:64, hq + qt * 128: hq + (qt + 1) * 128],
                           k_sb[0:64, ho + kg * 512: ho + (kg + 1) * 512],
                           True, False)
                        sw = mm(P, reg, qstp_b[:, qt * 128:(qt + 1) * 128],
                                cross_b[:, kg * 512:(kg + 1) * 512], False, True,
                                inc=(1 if kg == 3 else 0))
                    P.wait("vector", "tensor", sw)
                    rw2 = P.emit("vector", lambda v, qt=qt: v.reduce_max(
                        out=nmax[:, qt:qt + 1], in_=pbig[:, 0:T], axis=AX.X,
                        negate=True), inc=1)
                    P.wait("tensor", "vector", rw2)
                tn = transp(P, pbig[0:2, 0:128], nmax, idf)
                P.wait("vector", "tensor", tn)
                vn = P.emit("vector", lambda v: v.tensor_copy(
                    out=nm2, in_=pbig[0:2, 0:128]), inc=1)
                P.wait("gpsimd", "vector", vn)
                gn = dma(P, "gpsimd", nmrow, nm2)
                P.wait("tensor", "vector", vn)
                vvt = []
                for kc in range(16):
                    reg = pbf[:, (kc % 8) * 64:(kc % 8) * 64 + 64]
                    if kc >= 1:
                        P.wait("tensor", "vector", vvt[kc - 1])
                    tw = transp(P, reg, v_sb[0:64, ho + kc * 128: ho + (kc + 1) * 128],
                                idb[0:64, 0:64])
                    P.wait("vector", "tensor", tw)
                    vvt.append(P.emit("vector", lambda v, reg=reg, kc=kc:
                                      v.tensor_copy(out=vt[:, kc * 65:kc * 65 + 64],
                                                    in_=reg), inc=1))
                P.wait("tensor", "gpsimd", gn)
                P.wait("tensor", "vector", vvt[-1])
                aexp, pvw = [], []
                for kc in range(16):
                    reg = PST[kc % 2][:, 0:256]
                    if kc >= 2:
                        P.wait("tensor", "scalar", aexp[kc - 2])
                    mm(P, reg, k_sb[0:64, ho + kc * 128: ho + (kc + 1) * 128],
                       q_sb[0:64, hq: hq + 256], True, False)
                    mm(P, reg, cross_b[:, kc * 128:(kc + 1) * 128],
                       qstp_b[:, 0:256], False, False)
                    stw = mm(P, reg, onef[0:1, 0:128], nmrow[0:1, 0:256],
                             False, True, inc=1)
                    P.wait("scalar", "tensor", stw)
                    if kc >= 2:
                        P.wait("scalar", "tensor", pvw[kc - 2])
                    aexp.append(P.emit("scalar", lambda s, reg=reg, kc=kc:
                                       s.activation(out=pt[kc % 2], in_=reg,
                                                    func=AF.Exp), inc=1))
                    P.wait("tensor", "scalar", aexp[kc])
                    pvw.append(mm(P, ppv[0:65, 0:256], vt[:, kc * 65:(kc + 1) * 65],
                                  pt[kc % 2], kc == 0, kc == 15, inc=1))
                P.wait("vector", "tensor", pvw[15])
                vrc = P.emit("vector", lambda v: v.reciprocal(
                    out=rc_row, in_=ppv[64:65, 0:256]), inc=1)
                P.wait("tensor", "vector", vrc)
                bc = mm(P, pbig[0:64, 0:256], onef[0:1, 0:64], rc_row,
                        True, True, inc=1)
                P.wait("scalar", "tensor", pvw[15])
                P.wait("scalar", "vector", vrc)
                aev = P.emit("scalar", lambda s: s.copy(
                    out=ot_s, in_=ppv[0:64, 0:256]), inc=1)
                P.wait("vector", "tensor", bc)
                P.wait("vector", "scalar", aev)
                hs2 = 64 * hh
                vo = P.emit("vector", lambda v, fc=fc, hs2=hs2: v.tensor_tensor(
                    out=ofm[hs2:hs2 + 64, fc * JPC: fc * JPC + 256],
                    in0=ot_s, in1=pbig[0:64, 0:256], op=ALU.mult), inc=1)
                P.wait("tensor", "vector", vo)
        P.wait("tensor", "vector", P.cnt["vector"])
        vt2l = []
        for fo in range(8):
            reg = pst0[:, 0:256]
            for dc in range(8):
                mm(P, reg, wo_s[:, dc * D + fo * 128: dc * D + fo * 128 + 128],
                   ofm[:, dc * JPC: dc * JPC + 256], dc == 0, False)
            mw = mm(P, reg, bo_s[0:1, fo * 128:(fo + 1) * 128], onef[0:1, 0:256],
                    False, True, inc=1)
            P.wait("scalar", "tensor", mw)
            aev = P.emit("scalar", lambda s, reg=reg: s.copy(out=sf_s, in_=reg),
                         inc=1)
            P.wait("tensor", "scalar", aev)
            for ts_ in range(2):
                regt = pbig[:, (ts_ % 2) * 512:(ts_ % 2) * 512 + 128]
                if len(vt2l) >= 2:
                    P.wait("tensor", "vector", vt2l[-2])
                tw = transp(P, regt, sf_s[:, ts_ * 128:(ts_ + 1) * 128], idf)
                P.wait("vector", "tensor", tw)
                vt2l.append(P.emit("vector",
                                   lambda v, regt=regt, ts_=ts_, fo=fo: v.tensor_copy(
                                       out=stma[:, ts_ * D + fo * 128:
                                                ts_ * D + fo * 128 + 128],
                                       in_=regt), inc=1))
            P.wait("scalar", "tensor", P.cnt["tensor"])
        P.wait("sync", "vector", P.cnt["vector"])
        for ts_ in range(2):
            dma(P, "sync", stm[b, ts_ * 128:(ts_ + 1) * 128, :],
                stma[:, ts_ * D:(ts_ + 1) * D])
        P.wait("vector", "sync", P.cnt["sync"])
    P.barrier()
    A.release()

    A.mark()
    xn_sg = A.b16(128, 8 * 1024)
    acc = [A.f32(128, 1024) for _ in range(8)]
    w1_s = [A.b16(128, 8 * FF1), A.b16(128, 8 * FF1)]
    w2_s = A.b16(128, 16 * D)
    hm_s = A.b16(128, 16 * 512)
    b1_row = A.b16(1, FF1)
    oneb0 = A.b16(1, 512)
    b2_row = A.f32(1, D)
    ce_row = A.f32(1, 1024)
    ce_s = A.f32(128, 512)
    gt_s = A.f32(128, 128)
    gtm = A.f32(128, 1024)
    xst = A.f32(128, 1024)
    sst = A.f32(128, 1024)

    for sg in range(4):
        bs = [sg * 4 + i for i in range(4)]
        P.wait("sync", "tensor", P.cnt["tensor"])
        wx = 0
        for dc in range(8):
            for i, bb_ in enumerate(bs):
                wx = dma(P, "sync", xn_sg[:, dc * 1024 + i * 256: dc * 1024 + (i + 1) * 256],
                         xnbf[bb_, dc, :, 0:JPC])
        P.wait("vector", "tensor", P.cnt["tensor"])
        P.wait("vector", "scalar", P.cnt["scalar"])
        for oc in range(8):
            P.emit("vector", lambda v, oc=oc: v.memset(acc[oc], 0.0))
        P.wait("tensor", "sync", wx)
        if sg == 0:
            P.wait("vector", "tensor", P.cnt["tensor"])
            P.emit("vector", lambda v: v.memset(oneb0, 1.0))
            P.wait("tensor", "vector", P.cnt["vector"])
            w1w = 0
            for dc in range(8):
                w1w = dma(P, "gpsimd", w1_s[0][:, dc * FF1:(dc + 1) * FF1],
                          w1_d[0, dc * 128:(dc + 1) * 128, :])
            w2w = 0
            for fcc in range(16):
                w2w = dma(P, "gpsimd", w2_s[:, fcc * D:(fcc + 1) * D],
                          w2_d[0, fcc * 128:(fcc + 1) * 128, :])
            w1ld, w2ld = w1w, w2w
        for e in range(E):
            wbuf = w1_s[e % 2]
            w1w, w2w = w1ld, w2ld
            P.wait("sync", "tensor", P.cnt["tensor"])
            P.wait("gpsimd", "tensor", P.cnt["tensor"])
            bw1 = dma(P, "gpsimd", b1_row, b1_d[e])
            bw2 = dma(P, "sync", b2_row, b2_d[e])
            for i, bb_ in enumerate(bs):
                dma(P, "sync", ce_row[0:1, i * 256:(i + 1) * 256],
                    cdr[bb_, e:e + 1, :])
            cw = P.cnt["sync"]
            for tt in range(2):
                P.wait("tensor", "sync", cw)
                bcw = mm(P, pbig[:, 0:512], onef[0:1, 0:128],
                         ce_row[0:1, tt * 512:(tt + 1) * 512], True, True, inc=1)
                P.wait("vector", "tensor", bcw)
                vce = P.emit("vector", lambda v: v.tensor_copy(
                    out=ce_s, in_=pbig[:, 0:512]), inc=1)
                if tt == 0:
                    P.wait("tensor", "gpsimd", P.cnt["gpsimd"])
                ag = []
                for fcc in range(16):
                    reg = PST[fcc % 2][:, :]
                    if fcc >= 2:
                        P.wait("tensor", "scalar", ag[fcc - 2])
                    for dc in range(8):
                        mm(P, reg,
                           wbuf[:, dc * FF1 + fcc * 128: dc * FF1 + fcc * 128 + 128],
                           xn_sg[:, dc * 1024 + tt * 512: dc * 1024 + tt * 512 + 512],
                           dc == 0, False)
                    hw = mm(P, reg, b1_row[0:1, fcc * 128:(fcc + 1) * 128],
                            oneb0[0:1, 0:512], False, True, inc=1)
                    P.wait("scalar", "tensor", hw)
                    ag.append(P.emit("scalar", lambda s, reg=reg, fcc=fcc:
                                     s.activation(
                                         out=hm_s[:, fcc * 512:(fcc + 1) * 512],
                                         in_=reg, func=AF.Gelu), inc=1))
                if tt == 1:
                    en, bn = (e + 1) % E, (e + 1) % 2
                    srcidx = e + 1 if e + 1 < E else 0
                    P.wait("gpsimd", "tensor", P.cnt["tensor"])
                    w1n = 0
                    for dc in range(8):
                        w1n = dma(P, "gpsimd", w1_s[bn][:, dc * FF1:(dc + 1) * FF1],
                                  w1_d[srcidx, dc * 128:(dc + 1) * 128, :])
                    w1ld = w1n
                P.wait("vector", "scalar", ag[-1])
                P.wait("vector", "vector", vce)
                vhm = 0
                for fcc in range(16):
                    vhm = P.emit("vector", lambda v, fcc=fcc: v.tensor_tensor(
                        out=hm_s[:, fcc * 512:(fcc + 1) * 512],
                        in0=hm_s[:, fcc * 512:(fcc + 1) * 512], in1=ce_s,
                        op=ALU.mult), inc=(1 if fcc == 15 else 0))
                P.wait("tensor", "vector", vhm)
                if tt == 0:
                    P.wait("tensor", "gpsimd", w2w)
                va = []
                for oc in range(8):
                    reg = PST[oc % 2][:, :]
                    if oc >= 2:
                        P.wait("tensor", "vector", va[oc - 2])
                    for fcc in range(16):
                        mm(P, reg, w2_s[:, fcc * D + oc * 128: fcc * D + oc * 128 + 128],
                           hm_s[:, fcc * 512:(fcc + 1) * 512], fcc == 0, False)
                    ow = mm(P, reg, b2_row[0:1, oc * 128:(oc + 1) * 128],
                            ce_row[0:1, tt * 512:(tt + 1) * 512], False, True, inc=1)
                    P.wait("vector", "tensor", ow)
                    va.append(P.emit("vector", lambda v, reg=reg, oc=oc, tt=tt:
                                     v.tensor_tensor(
                                         out=acc[oc][:, tt * 512:(tt + 1) * 512],
                                         in0=acc[oc][:, tt * 512:(tt + 1) * 512],
                                         in1=reg, op=ALU.add), inc=1))
                P.wait("tensor", "vector", va[-1])
                P.wait("scalar", "tensor", P.cnt["tensor"])
                if tt == 1:
                    srcidx = e + 1 if e + 1 < E else 0
                    P.wait("gpsimd", "tensor", P.cnt["tensor"])
                    w2n = 0
                    for fcc in range(16):
                        w2n = dma(P, "gpsimd", w2_s[:, fcc * D:(fcc + 1) * D],
                                  w2_d[srcidx, fcc * 128:(fcc + 1) * 128, :])
                    w2ld = w2n
        prev_out = 0
        for t8 in range(8):
            bb_ = sg * 4 + t8 // 2
            j0 = (t8 % 2) * 128
            P.wait("sync", "vector", P.cnt["vector"])
            wxx = dma(P, "sync", xst, x_in[bb_, j0:j0 + 128, :])
            wss = dma(P, "sync", sst, stm[bb_, j0:j0 + 128, :])
            P.wait("vector", "sync", prev_out)
            P.wait("tensor", "vector", P.cnt["vector"])
            vgt = []
            for oc in range(8):
                asil = P.emit("scalar", lambda s, oc=oc, t8=t8: s.activation(
                    out=gt_s, in_=acc[oc][:, t8 * 128:(t8 + 1) * 128],
                    func=AF.Silu), inc=1)
                P.wait("tensor", "scalar", asil)
                regt = pbig[:, (oc % 2) * 512:(oc % 2) * 512 + 128]
                if oc >= 2:
                    P.wait("tensor", "vector", vgt[oc - 2])
                tw = transp(P, regt, gt_s, idf)
                P.wait("scalar", "tensor", tw)
                P.wait("vector", "tensor", tw)
                vgt.append(P.emit("vector", lambda v, regt=regt, oc=oc:
                                  v.tensor_copy(
                                      out=gtm[:, oc * 128:(oc + 1) * 128],
                                      in_=regt), inc=1))
            P.wait("vector", "sync", wxx)
            P.wait("vector", "sync", wss)
            P.wait("vector", "vector", vgt[-1])
            vmu = P.emit("vector", lambda v: v.tensor_tensor(
                out=gtm, in0=gtm, in1=sst, op=ALU.mult), inc=1)
            P.wait("vector", "vector", vmu)
            vf = P.emit("vector", lambda v: v.tensor_tensor(
                out=gtm, in0=gtm, in1=xst, op=ALU.add), inc=1)
            P.wait("sync", "vector", vf)
            prev_out = dma(P, "sync", x1loc[bb_, j0:j0 + 128, :], gtm)
    A.release()
    A.release()
    return dict(idf_d=idf_d, onef_d=onef_d)


def build_l0(debug=False):
    env = _mk_env()
    P, nc, ctx = env["P"], env["nc"], env["ctx"]
    emit_l0(env, debug=debug)
    fin = P.cnt["sync"]
    P.wait("sync", "sync", fin)
    P.wait("gpsimd", "sync", fin)
    block = ctx.enter_context(nc.Block())
    P.finalize(block)
    ctx.close()
    return nc


NTOK = B * JPC


def emit_l1(env, debug=False, xs_src=None, shared_d=None):
    nc, ctx, A, P = env["nc"], env["ctx"], env["A"], env["P"]
    pbig, pst0, pst1, ppv = env["pbig"], env["pst0"], env["pst1"], env["ppv"]
    di = lambda n, s, d=F32: nc.dram_tensor(n, s, d, kind="ExternalInput").ap()

    xs_in = di("xs", [B, JPC, D]) if xs_src is None else xs_src
    ln1g_r = di("ln1g", [1, D])
    ln1b_r = di("ln1b", [1, D])
    ln2g_f = di("ln2gf", [128, 8])
    ln2b_f = di("ln2bf", [128, 8])
    tmk_f = di("tmkf", [128, 8])
    tmv_f = di("tmvf", [128, 8])
    tmr_f = di("tmrf", [128, 8])
    td_f = di("tdf", [128, 8])
    tf_f = di("tff", [128, 8])
    cmk_f = di("cmkf", [128, 8])
    cmr_f = di("cmrf", [128, 8])
    wkt = di("wkt1", [D, D])
    wvt = di("wvt1", [D, D])
    wrt = di("wrt1", [D, D])
    wot = di("wot1", [D, D])
    wckt = di("wckt", [D, FF2])
    wcvt = di("wcvt", [FF2, D])
    wcrt = di("wcrt", [D, D])
    w1_d = di("w1f", [D, FF2])
    w2_d = di("w2f", [FF2, D])
    b1_r = di("b1f", [1, FF2])
    b2_r = di("b2f", [1, D])
    qph_f = di("qphf", [128, 32])
    qwr_f = di("qwrf", [128, 32])
    qwi_f = di("qwif", [128, 32])
    if shared_d is None:
        idf_d = di("idenf", [128, 128])
        onef_d = di("onesf", [1, 512])
    else:
        idf_d = shared_d["idf_d"]
        onef_d = shared_d["onef_d"]
    onec_d = di("onescol", [128, 1])

    out_d = nc.dram_tensor("out", [B, JPC, D], F32, kind="ExternalOutput").ap()

    def scr(name, shape, dtyp=F32):
        kind = "ExternalOutput" if debug else "Internal"
        return nc.dram_tensor(name, shape, dtyp, kind=kind).ap()

    xn1f = scr("xn1f", [8, 128, NTOK])
    xn1b = scr("xn1b", [8, 128, NTOK], BF16)
    xxf = scr("xxf", [8, 128, NTOK])
    kf = scr("kf", [8, 128, NTOK])
    vf = scr("vf", [8, 128, NTOK])
    rf = scr("rf", [8, 128, NTOK])
    rwb = scr("rwb", [8, 128, NTOK], BF16)
    x2f = scr("x2f", [8, 128, NTOK])
    yyf = scr("yyf", [8, 128, NTOK])
    sqb = scr("sqb", [32, 128, NTOK], BF16)
    s1f = scr("s1f", [8, 128, NTOK])
    PST = (pst0, pst1)

    A.mark()
    idf = A.f32(128, 128)
    onef = A.f32(1, 512)
    onec = A.f32(128, 1)
    eps_c = A.f32(128, 1)
    ln1g = A.f32(1, D)
    ln1b = A.f32(1, D)
    ln2g = A.f32(128, 8)
    ln2b = A.f32(128, 8)
    tmk = A.f32(128, 8)
    tmv = A.f32(128, 8)
    tmr = A.f32(128, 8)
    tfu = A.f32(128, 8)
    wneg = A.f32(128, 8)
    cmk = A.f32(128, 8)
    cmr = A.f32(128, 8)
    g1bc = A.f32(128, D)
    b1bc = A.f32(128, D)
    tdtmp = A.f32(128, 8)

    for (dst, srcd) in ((idf, idf_d), (onef, onef_d), (onec, onec_d),
                        (ln1g, ln1g_r), (ln1b, ln1b_r), (ln2g, ln2g_f),
                        (ln2b, ln2b_f), (tmk, tmk_f), (tmv, tmv_f),
                        (tmr, tmr_f), (tfu, tf_f), (tdtmp, td_f),
                        (cmk, cmk_f), (cmr, cmr_f)):
        dma(P, "sync", dst, srcd)
    w_pro = P.cnt["sync"]
    P.wait("vector", "sync", w_pro)
    P.emit("vector", lambda v: v.memset(eps_c, EPS))
    P.wait("scalar", "sync", w_pro)
    aexp = P.emit("scalar", lambda s: s.activation(
        out=wneg, in_=tdtmp, func=AF.Exp), inc=1)
    P.wait("vector", "scalar", aexp)
    P.emit("vector", lambda v: v.tensor_scalar(
        out=wneg, in0=wneg, scalar1=-1.0, scalar2=None, op0=ALU.mult))
    P.wait("tensor", "sync", w_pro)
    for (row, bc, bank) in ((ln1g, g1bc, 0), (ln1b, b1bc, 1)):
        for hlf in range(2):
            P.wait("tensor", "vector", P.cnt["vector"])
            w = mm(P, pbig[:, bank * 512: bank * 512 + 512],
                   onef[0:1, 0:128], row[0:1, hlf * 512:(hlf + 1) * 512],
                   True, True, inc=1)
            P.wait("vector", "tensor", w)
            P.emit("vector", lambda v, bc=bc, hlf=hlf, bank=bank: v.tensor_copy(
                out=bc[:, hlf * 512:(hlf + 1) * 512],
                in_=pbig[:, bank * 512: bank * 512 + 512]))
    P.barrier()

    NT2 = NTOK // 128

    A.mark()
    xt_b = [A.f32(128, D) for _ in range(4)]
    xn_b = [A.f32(128, D) for _ in range(4)]
    xx_b = [A.f32(128, D) for _ in range(4)]
    xsq = A.f32(128, D)
    stat = A.f32(128, 16)
    stat2 = A.f32(128, 16)
    sd4 = A.f32(128, 4)
    rstd4 = A.f32(128, 4)
    msq4 = A.f32(128, 4)
    asm_f = A.f32(128, 8 * 512)
    asm_b = A.b16(128, 8 * 512)
    asm_x = A.f32(128, 8 * 512)

    def tm_stats(src_tiles, scol, n=4):
        for i in range(n):
            P.emit("vector", lambda v, i=i: v.reduce_sum(
                out=stat[:, scol + i:scol + i + 1], in_=src_tiles[i], axis=AX.X))
        vsq = []
        for i in range(n):
            if i >= 1:
                P.wait("scalar", "vector", vsq[i - 1])
            sqw = P.emit("scalar", lambda s, i=i: s.activation(
                out=xsq, in_=src_tiles[i], func=AF.Square), inc=1)
            P.wait("vector", "scalar", sqw)
            vsq.append(P.emit("vector", lambda v, i=i: v.reduce_sum(
                out=stat[:, scol + 4 + i:scol + 5 + i], in_=xsq, axis=AX.X), inc=1))
        vm = 0
        for i in range(n):
            vm = P.emit("vector", lambda v, i=i: v.tensor_scalar(
                out=stat[:, scol + 8 + i:scol + 9 + i],
                in0=stat[:, scol + i:scol + i + 1], scalar1=INV_D,
                scalar2=None, op0=ALU.mult), inc=1)
        P.wait("scalar", "vector", vm)
        am = P.emit("scalar", lambda s: s.activation(
            out=msq4, in_=stat[:, scol + 8:scol + 12], func=AF.Square), inc=1)
        P.wait("vector", "scalar", am)
        vnv = 0
        for i in range(n):
            vnv = P.emit("vector", lambda v, i=i: v.scalar_tensor_tensor(
                out=stat[:, scol + 12 + i:scol + 13 + i],
                in0=stat[:, scol + 4 + i:scol + 5 + i], scalar=INV_D,
                in1=msq4[:, i:i + 1], op0=ALU.mult, op1=ALU.subtract), inc=1)
        P.wait("scalar", "vector", vnv)
        asd = P.emit("scalar", lambda s: s.activation(
            out=sd4, in_=stat[:, scol + 12:scol + 16], func=AF.Sqrt,
            bias=eps_c), inc=1)
        P.wait("vector", "scalar", asd)
        P.emit("vector", lambda v: v.reciprocal(out=rstd4, in_=sd4))

    prev_rd = 0
    prev_asm = 0
    for g in range(NT2 // 4):
        tis = [g * 4 + i for i in range(4)]
        P.wait("sync", "vector", P.cnt["vector"])
        P.wait("sync", "scalar", P.cnt["scalar"])
        w_x = 0
        for i, ti in enumerate(tis):
            bb_, jj = ti // 2, (ti % 2) * 128
            w_x = dma(P, "sync", xt_b[i], xs_in[bb_, jj:jj + 128, :])
        P.wait("vector", "sync", w_x)
        P.wait("scalar", "sync", w_x)
        tm_stats(xt_b, 0)
        for i in range(4):
            P.emit("vector", lambda v, i=i: v.tensor_scalar(
                out=xn_b[i], in0=xt_b[i], scalar1=stat[:, 8 + i:9 + i],
                scalar2=None, op0=ALU.subtract))
        for i in range(4):
            P.emit("vector", lambda v, i=i: v.tensor_scalar(
                out=xn_b[i], in0=xn_b[i], scalar1=rstd4[:, i:i + 1],
                scalar2=None, op0=ALU.mult))
        P.wait("scalar", "vector", P.cnt["vector"])
        tm_stats(xn_b, 0)
        for i in range(4):
            P.emit("vector", lambda v, i=i: v.tensor_scalar(
                out=xx_b[i], in0=xn_b[i], scalar1=stat[:, 8 + i:9 + i],
                scalar2=None, op0=ALU.subtract))
        for i in range(4):
            P.emit("vector", lambda v, i=i: v.tensor_scalar(
                out=xx_b[i], in0=xx_b[i], scalar1=rstd4[:, i:i + 1],
                scalar2=None, op0=ALU.mult))
        for i in range(4):
            P.emit("vector", lambda v, i=i: v.tensor_tensor(
                out=xx_b[i], in0=xx_b[i], in1=g1bc, op=ALU.mult))
        for i in range(4):
            P.emit("vector", lambda v, i=i: v.tensor_tensor(
                out=xx_b[i], in0=xx_b[i], in1=b1bc, op=ALU.add))
        vready = P.cnt["vector"]
        P.wait("tensor", "vector", vready)
        P.wait("vector", "sync", prev_asm)
        P.wait("scalar", "sync", prev_asm)
        evs = []
        for i in range(4):
            for dc in range(8):
                reg = pbig[:, (dc % 2) * 512:(dc % 2) * 512 + 128]
                reg2 = pbig[:, 1024 + (dc % 2) * 512:1024 + (dc % 2) * 512 + 128]
                k = i * 8 + dc
                if k >= 2:
                    P.wait("tensor", "scalar", evs[k - 2])
                tw = transp(P, reg, xn_b[i][:, dc * 128:(dc + 1) * 128], idf)
                tw2 = transp(P, reg2, xx_b[i][:, dc * 128:(dc + 1) * 128], idf)
                P.wait("scalar", "tensor", tw2)
                sf = P.emit("scalar", lambda s, reg=reg, dc=dc, i=i: s.copy(
                    out=asm_f[:, dc * 512 + i * 128: dc * 512 + i * 128 + 128],
                    in_=reg), inc=1)
                sx = P.emit("scalar", lambda s, reg2=reg2, dc=dc, i=i: s.copy(
                    out=asm_x[:, dc * 512 + i * 128: dc * 512 + i * 128 + 128],
                    in_=reg2), inc=1)
                evs.append(sx)
                P.wait("vector", "scalar", sf)
                P.emit("vector", lambda v, dc=dc, i=i: v.tensor_copy(
                    out=asm_b[:, dc * 512 + i * 128: dc * 512 + i * 128 + 128],
                    in_=asm_f[:, dc * 512 + i * 128: dc * 512 + i * 128 + 128]))
        P.wait("sync", "scalar", P.cnt["scalar"])
        P.wait("sync", "vector", P.cnt["vector"])
        for dc in range(8):
            dma(P, "sync", xn1f[dc, :, g * 512:(g + 1) * 512],
                asm_f[:, dc * 512:(dc + 1) * 512])
            dma(P, "sync", xn1b[dc, :, g * 512:(g + 1) * 512],
                asm_b[:, dc * 512:(dc + 1) * 512])
            prev_asm = dma(P, "sync", xxf[dc, :, g * 512:(g + 1) * 512],
                           asm_x[:, dc * 512:(dc + 1) * 512])
    P.barrier()
    A.release()

    A.mark()
    wk_s = A.b16(128, 8 * D)
    wv_s = A.b16(128, 8 * D)
    wr_s = A.b16(128, 8 * D)
    km_s = A.b16(128, 8 * 512)
    vm_s = A.b16(128, 8 * 512)
    rm_s = A.b16(128, 8 * 512)
    xwin = [A.f32(128, 768) for _ in range(2)]
    kev = A.f32(128, 512)
    vev = A.f32(128, 512)
    rev = A.f32(128, 512)

    wW = 0
    for dc in range(8):
        for (wsrc, wdst) in ((wkt, wk_s), (wvt, wv_s), (wrt, wr_s)):
            wW = dma(P, "gpsimd", wdst[:, dc * D:(dc + 1) * D],
                     wsrc[dc * 128:(dc + 1) * 128, :])
    for TT in range(8):
        t0 = TT * 512
        lo = max(0, t0 - 256)
        for ic in range(8):
            P.wait("sync", "vector", P.cnt["vector"])
            wx = P.dma_ch("sync", "ld%d" % (ic % 2), xwin[ic % 2][:, 0:t0 + 512 - lo],
                          xxf[ic, :, lo:t0 + 512])
            P.wait_ch("vector", "ld%d" % (ic % 2), wx)
            cur = xwin[ic % 2][:, 256:768] if TT > 0 else xwin[ic % 2][:, 0:512]
            for (mix, msb, mvec) in ((0, km_s, tmk), (1, vm_s, tmv), (2, rm_s, tmr)):
                dsl = msb[:, ic * 512:(ic + 1) * 512]
                if TT == 0:
                    P.emit("vector", lambda v, dsl=dsl, cur=cur, mvec=mvec, ic=ic:
                           v.tensor_scalar(out=dsl[:, 0:256], in0=cur[:, 0:256],
                                           scalar1=mvec[:, ic:ic + 1], scalar2=None,
                                           op0=ALU.mult))
                    sh = xwin[ic % 2][:, 0:256]
                    cu2 = xwin[ic % 2][:, 256:512]
                    P.emit("vector", lambda v, cu2=cu2, sh=sh: v.tensor_tensor(
                        out=kev[:, 0:256], in0=cu2, in1=sh, op=ALU.subtract))
                    P.emit("vector", lambda v, dsl=dsl, sh=sh, mvec=mvec, ic=ic:
                           v.scalar_tensor_tensor(
                               out=dsl[:, 256:512], in0=kev[:, 0:256],
                               scalar=mvec[:, ic:ic + 1], in1=sh,
                               op0=ALU.mult, op1=ALU.add))
                else:
                    sh = xwin[ic % 2][:, 0:512]
                    P.emit("vector", lambda v, cur=cur, sh=sh: v.tensor_tensor(
                        out=kev, in0=cur, in1=sh, op=ALU.subtract))
                    P.emit("vector", lambda v, dsl=dsl, sh=sh, mvec=mvec, ic=ic:
                           v.scalar_tensor_tensor(
                               out=dsl, in0=kev, scalar=mvec[:, ic:ic + 1],
                               in1=sh, op0=ALU.mult, op1=ALU.add))
        vmix = P.cnt["vector"]
        P.wait("tensor", "vector", vmix)
        if TT == 0:
            P.wait("tensor", "gpsimd", wW)
        for oc in range(8):
            for (wsb, msb, psm) in ((wk_s, km_s, pst0), (wv_s, vm_s, pst1),
                                    (wr_s, rm_s, ppv)):
                for ic in range(8):
                    mm(P, psm[:, :], wsb[:, ic * D + oc * 128: ic * D + oc * 128 + 128],
                       msb[:, ic * 512:(ic + 1) * 512], ic == 0, ic == 7,
                       inc=(1 if ic == 7 else 0))
            kw_ = P.cnt["tensor"]
            P.wait("scalar", "tensor", kw_)
            ke = P.emit("scalar", lambda s: s.copy(out=kev, in_=pst0[:, :]), inc=1)
            re_ = P.emit("scalar", lambda s: s.copy(out=rev, in_=ppv[:, :]), inc=1)
            P.wait("vector", "tensor", kw_)
            ve = P.emit("vector", lambda v: v.tensor_copy(out=vev, in_=pst1[:, :]),
                        inc=1)
            P.wait("sync", "scalar", re_)
            P.wait("sync", "vector", ve)
            dma(P, "sync", kf[oc, :, t0:t0 + 512], kev)
            dma(P, "sync", vf[oc, :, t0:t0 + 512], vev)
            w_st = dma(P, "sync", rf[oc, :, t0:t0 + 512], rev)
            P.wait("tensor", "sync", w_st)
    P.barrier()
    A.release()

    A.mark()
    aa = A.f32(128, 2048)
    bb = A.f32(128, 2048)
    pp = A.f32(128, 2048)
    kb_s = A.f32(128, 2048)
    vb_s = A.f32(128, 2048)
    rb_s = A.f32(128, 2048)
    t_a = A.f32(128, 2048)
    t_b = A.f32(128, 2048)
    t_c = A.f32(128, 2048)
    t_d = A.f32(128, 2048)
    t_e = A.f32(128, 2048)
    t_f = A.f32(128, 2048)
    t_g = A.f32(128, 2048)
    rw_o = A.b16(128, 2048)

    P.emit("vector", lambda v: v.memset(aa, 0.0))
    P.emit("vector", lambda v: v.memset(bb, 0.0))
    P.emit("vector", lambda v: v.memset(pp, -1e38))
    ppc, aac, bbc = pp, aa, bb
    for step in range(B):
        c0 = step * JPC
        P.wait("sync", "vector", P.cnt["vector"])
        P.wait("sync", "scalar", P.cnt["scalar"])
        wl = 0
        for dc in range(8):
            dma(P, "sync", kb_s[:, dc * 256:(dc + 1) * 256], kf[dc, :, c0:c0 + JPC])
            dma(P, "sync", vb_s[:, dc * 256:(dc + 1) * 256], vf[dc, :, c0:c0 + JPC])
            wl = dma(P, "sync", rb_s[:, dc * 256:(dc + 1) * 256],
                     rf[dc, :, c0:c0 + JPC])
        P.wait("vector", "sync", wl)
        P.wait("scalar", "sync", wl)
        for dc in range(8):
            P.emit("vector", lambda v, dc=dc: v.tensor_scalar(
                out=t_a[:, dc * 256:(dc + 1) * 256],
                in0=kb_s[:, dc * 256:(dc + 1) * 256],
                scalar1=tfu[:, dc:dc + 1], scalar2=None, op0=ALU.add))
        P.emit("vector", lambda v: v.tensor_tensor(out=t_b, in0=pp, in1=t_a,
                                                   op=ALU.max))
        P.emit("vector", lambda v: v.tensor_tensor(out=t_c, in0=pp, in1=t_b,
                                                   op=ALU.subtract))
        P.emit("vector", lambda v: v.tensor_tensor(out=t_d, in0=t_a, in1=t_b,
                                                   op=ALU.subtract))
        vv_ = P.cnt["vector"]
        P.wait("scalar", "vector", vv_)
        e1w = P.emit("scalar", lambda s: s.activation(out=t_c, in_=t_c, func=AF.Exp),
                     inc=1)
        e2w = P.emit("scalar", lambda s: s.activation(out=t_d, in_=t_d, func=AF.Exp),
                     inc=1)
        P.wait("vector", "scalar", e2w)
        P.emit("vector", lambda v: v.tensor_tensor(out=t_e, in0=t_c, in1=aa,
                                                   op=ALU.mult))
        P.emit("vector", lambda v: v.tensor_tensor(out=t_f, in0=t_d, in1=vb_s,
                                                   op=ALU.mult))
        P.emit("vector", lambda v: v.tensor_tensor(out=t_e, in0=t_e, in1=t_f,
                                                   op=ALU.add))
        P.emit("vector", lambda v: v.tensor_tensor(out=t_f, in0=t_c, in1=bb,
                                                   op=ALU.mult))
        P.emit("vector", lambda v: v.tensor_tensor(out=t_f, in0=t_f, in1=t_d,
                                                   op=ALU.add))
        P.emit("vector", lambda v: v.reciprocal(out=t_f, in_=t_f))
        P.emit("vector", lambda v: v.tensor_tensor(out=t_e, in0=t_e, in1=t_f,
                                                   op=ALU.mult))
        vv_ = P.cnt["vector"]
        P.wait("scalar", "vector", vv_)
        rsg = P.emit("scalar", lambda s: s.activation(out=t_f, in_=rb_s,
                                                      func=AF.Sigmoid), inc=1)
        P.wait("vector", "scalar", rsg)
        vrw = P.emit("vector", lambda v: v.tensor_tensor(out=rw_o, in0=t_f, in1=t_e,
                                                         op=ALU.mult), inc=1)
        P.wait("sync", "vector", vrw)
        for dc in range(8):
            dma(P, "sync", rwb[dc, :, c0:c0 + JPC], rw_o[:, dc * 256:(dc + 1) * 256])
        for dc in range(8):
            P.emit("vector", lambda v, dc=dc: v.tensor_scalar(
                out=t_a[:, dc * 256:(dc + 1) * 256],
                in0=pp[:, dc * 256:(dc + 1) * 256],
                scalar1=wneg[:, dc:dc + 1], scalar2=None, op0=ALU.add))
        P.emit("vector", lambda v: v.tensor_tensor(out=t_b, in0=t_a, in1=kb_s,
                                                   op=ALU.max))
        P.emit("vector", lambda v: v.tensor_tensor(out=t_c, in0=t_a, in1=t_b,
                                                   op=ALU.subtract))
        P.emit("vector", lambda v: v.tensor_tensor(out=t_d, in0=kb_s, in1=t_b,
                                                   op=ALU.subtract))
        vv_ = P.cnt["vector"]
        P.wait("scalar", "vector", vv_)
        e1w = P.emit("scalar", lambda s: s.activation(out=t_c, in_=t_c, func=AF.Exp),
                     inc=1)
        e2w = P.emit("scalar", lambda s: s.activation(out=t_d, in_=t_d, func=AF.Exp),
                     inc=1)
        P.wait("vector", "scalar", e2w)
        P.emit("vector", lambda v: v.tensor_tensor(out=t_e, in0=t_c, in1=aa,
                                                   op=ALU.mult))
        P.emit("vector", lambda v: v.tensor_tensor(out=t_g, in0=t_d, in1=vb_s,
                                                   op=ALU.mult))
        P.emit("vector", lambda v: v.tensor_tensor(out=aa, in0=t_e, in1=t_g,
                                                   op=ALU.add))
        P.emit("vector", lambda v: v.tensor_tensor(out=t_e, in0=t_c, in1=bb,
                                                   op=ALU.mult))
        P.emit("vector", lambda v: v.tensor_tensor(out=bb, in0=t_e, in1=t_d,
                                                   op=ALU.add))
        P.emit("vector", lambda v: v.tensor_copy(out=pp, in_=t_b))
    P.barrier()
    A.release()

    A.mark()
    wo_s = A.b16(128, 8 * D)
    rwt = A.b16(128, 8 * 512)
    xn1t = A.f32(128, 512)
    x2t = A.f32(128, 8 * 512)
    sqt = A.f32(128, 512)
    mrow = A.f32(1, 512)
    e2row = A.f32(1, 512)
    nvrow = A.f32(1, 512)
    sdrow = A.f32(1, 512)
    rsrow = A.f32(1, 512)
    yyt = A.f32(128, 512)
    u1t = A.f32(128, 512)

    wW = 0
    for dc in range(8):
        wW = dma(P, "gpsimd", wo_s[:, dc * D:(dc + 1) * D],
                 wot[dc * 128:(dc + 1) * 128, :])
    for TT in range(8):
        t0 = TT * 512
        P.wait("sync", "tensor", P.cnt["tensor"])
        wl = 0
        for dc in range(8):
            wl = dma(P, "sync", rwt[:, dc * 512:(dc + 1) * 512],
                     rwb[dc, :, t0:t0 + 512])
        P.wait("tensor", "sync", wl)
        if TT == 0:
            P.wait("tensor", "gpsimd", wW)
        for oc in range(8):
            psm = PST[oc % 2]
            for ic in range(8):
                mm(P, psm[:, :], wo_s[:, ic * D + oc * 128: ic * D + oc * 128 + 128],
                   rwt[:, ic * 512:(ic + 1) * 512], ic == 0, ic == 7,
                   inc=(1 if ic == 7 else 0))
            ow = P.cnt["tensor"]
            P.wait("sync", "vector", P.cnt["vector"])
            wx = P.dma_ch("sync", "ld0", xn1t, xn1f[oc, :, t0:t0 + 512])
            P.wait_ch("vector", "ld0", wx)
            P.wait("vector", "tensor", ow)
            vx2 = P.emit("vector", lambda v, oc=oc: v.tensor_tensor(
                out=x2t[:, oc * 512:(oc + 1) * 512], in0=xn1t, in1=PST[oc % 2][:, :],
                op=ALU.add), inc=1)
            P.wait("scalar", "vector", vx2)
            sq_w = P.emit("scalar", lambda s, oc=oc: s.activation(
                out=sqt, in_=x2t[:, oc * 512:(oc + 1) * 512], func=AF.Square), inc=1)
            P.wait("tensor", "vector", vx2)
            P.wait("tensor", "scalar", sq_w)
            mm(P, pbig[0:1, 0:512], onec, x2t[:, oc * 512:(oc + 1) * 512],
               oc == 0, oc == 7)
            mm(P, pbig[0:1, 512:1024], onec, sqt, oc == 0, oc == 7,
               inc=(1 if oc == 7 else 0))
        st_w = P.cnt["tensor"]
        P.wait("sync", "vector", P.cnt["vector"])
        for oc in range(8):
            dma(P, "sync", x2f[oc, :, t0:t0 + 512], x2t[:, oc * 512:(oc + 1) * 512])
        P.wait("vector", "tensor", st_w)
        P.emit("vector", lambda v: v.tensor_scalar(
            out=mrow, in0=pbig[0:1, 0:512], scalar1=INV_D, scalar2=None,
            op0=ALU.mult))
        P.emit("vector", lambda v: v.tensor_scalar(
            out=e2row, in0=pbig[0:1, 512:1024], scalar1=INV_D, scalar2=None,
            op0=ALU.mult))
        vst = P.cnt["vector"]
        P.wait("scalar", "vector", vst)
        am = P.emit("scalar", lambda s: s.activation(
            out=nvrow, in_=mrow, func=AF.Square), inc=1)
        P.wait("vector", "scalar", am)
        P.emit("vector", lambda v: v.tensor_tensor(
            out=nvrow, in0=e2row, in1=nvrow, op=ALU.subtract))
        vnv = P.cnt["vector"]
        P.wait("scalar", "vector", vnv)
        asd = P.emit("scalar", lambda s: s.activation(
            out=sdrow, in_=nvrow, func=AF.Sqrt, bias=eps_c[0:1, 0:1]), inc=1)
        P.wait("vector", "scalar", asd)
        P.emit("vector", lambda v: v.reciprocal(out=rsrow, in_=sdrow))
        vrs = P.cnt["vector"]
        P.wait("tensor", "vector", vrs)
        bm = mm(P, pbig[:, 1024:1536], onef[0:1, 0:128], mrow, True, True, inc=1)
        br = mm(P, pbig[:, 1536:2048], onef[0:1, 0:128], rsrow, True, True, inc=1)
        P.wait("vector", "tensor", br)
        for oc in range(8):
            P.emit("vector", lambda v, oc=oc: v.tensor_tensor(
                out=u1t, in0=x2t[:, oc * 512:(oc + 1) * 512],
                in1=pbig[:, 1024:1536], op=ALU.subtract))
            P.emit("vector", lambda v: v.tensor_tensor(
                out=u1t, in0=u1t, in1=pbig[:, 1536:2048], op=ALU.mult))
            vy = P.emit("vector", lambda v, oc=oc: v.tensor_scalar(
                out=yyt, in0=u1t, scalar1=ln2g[:, oc:oc + 1],
                scalar2=ln2b[:, oc:oc + 1], op0=ALU.mult, op1=ALU.add), inc=1)
            P.wait("sync", "vector", vy)
            dma(P, "sync", yyf[oc, :, t0:t0 + 512], yyt)
            P.wait("vector", "sync", P.cnt["sync"])
        P.wait("tensor", "sync", P.cnt["sync"])
    P.barrier()
    A.release()

    A.mark()
    wck_s = A.b16(128, 8 * FF2)
    ym_s = A.b16(128, 8 * 512)
    ywin = [A.f32(128, 768) for _ in range(2)]
    dtmp = A.f32(128, 512)
    rk_t = A.f32(128, 512)
    sq_b16 = A.b16(128, 512)

    wW = 0
    for dc in range(8):
        wW = dma(P, "gpsimd", wck_s[:, dc * FF2:(dc + 1) * FF2],
                 wckt[dc * 128:(dc + 1) * 128, :])
    for TT in range(8):
        t0 = TT * 512
        lo = max(0, t0 - 256)
        for ic in range(8):
            P.wait("sync", "vector", P.cnt["vector"])
            wx = P.dma_ch("sync", "ld%d" % (ic % 2), ywin[ic % 2][:, 0:t0 + 512 - lo],
                          yyf[ic, :, lo:t0 + 512])
            P.wait_ch("vector", "ld%d" % (ic % 2), wx)
            cur = ywin[ic % 2][:, 256:768] if TT > 0 else ywin[ic % 2][:, 0:512]
            dsl = ym_s[:, ic * 512:(ic + 1) * 512]
            if TT == 0:
                P.emit("vector", lambda v, dsl=dsl, cur=cur, ic=ic: v.tensor_scalar(
                    out=dsl[:, 0:256], in0=cur[:, 0:256], scalar1=cmk[:, ic:ic + 1],
                    scalar2=None, op0=ALU.mult))
                sh = ywin[ic % 2][:, 0:256]
                cu2 = ywin[ic % 2][:, 256:512]
                P.emit("vector", lambda v, cu2=cu2, sh=sh: v.tensor_tensor(
                    out=dtmp[:, 0:256], in0=cu2, in1=sh, op=ALU.subtract))
                P.emit("vector", lambda v, dsl=dsl, sh=sh, ic=ic:
                       v.scalar_tensor_tensor(
                           out=dsl[:, 256:512], in0=dtmp[:, 0:256],
                           scalar=cmk[:, ic:ic + 1], in1=sh,
                           op0=ALU.mult, op1=ALU.add))
            else:
                sh = ywin[ic % 2][:, 0:512]
                P.emit("vector", lambda v, cur=cur, sh=sh: v.tensor_tensor(
                    out=dtmp, in0=cur, in1=sh, op=ALU.subtract))
                P.emit("vector", lambda v, dsl=dsl, sh=sh, ic=ic:
                       v.scalar_tensor_tensor(
                           out=dsl, in0=dtmp, scalar=cmk[:, ic:ic + 1],
                           in1=sh, op0=ALU.mult, op1=ALU.add))
        vmix = P.cnt["vector"]
        P.wait("tensor", "vector", vmix)
        if TT == 0:
            P.wait("tensor", "gpsimd", wW)
        for fc in range(32):
            psm = PST[fc % 2]
            for ic in range(8):
                mm(P, psm[:, :],
                   wck_s[:, ic * FF2 + fc * 128: ic * FF2 + fc * 128 + 128],
                   ym_s[:, ic * 512:(ic + 1) * 512], ic == 0, ic == 7,
                   inc=(1 if ic == 7 else 0))
            kw_ = P.cnt["tensor"]
            P.wait("vector", "tensor", kw_)
            P.wait("vector", "scalar", P.cnt["scalar"])
            vrk = P.emit("vector", lambda v, fc=fc: v.tensor_scalar(
                out=rk_t, in0=PST[fc % 2][:, :], scalar1=0.0, scalar2=None,
                op0=ALU.max), inc=1)
            P.wait("scalar", "vector", vrk)
            asq = P.emit("scalar", lambda s: s.activation(
                out=sq_b16, in_=rk_t, func=AF.Square), inc=1)
            P.wait("sync", "scalar", asq)
            w_st = dma(P, "sync", sqb[fc, :, t0:t0 + 512], sq_b16)
            P.wait("scalar", "sync", w_st)
            P.wait("tensor", "vector", vrk)
    P.barrier()
    A.release()

    A.mark()
    wcv_s = A.b16(128, 32 * D)
    wcr_s = A.b16(128, 8 * D)
    rm2_s = A.b16(128, 8 * 512)
    sq_s = A.b16(128, 32 * 512)
    ywin2 = [A.f32(128, 768) for _ in range(2)]
    dtmp2 = A.f32(128, 512)
    rs_t = A.f32(128, 512)
    s1_t = A.f32(128, 512)
    x2t2 = A.f32(128, 512)

    wW = 0
    for fc in range(32):
        wW = dma(P, "gpsimd", wcv_s[:, fc * D:(fc + 1) * D],
                 wcvt[fc * 128:(fc + 1) * 128, :])
    for dc in range(8):
        wW = dma(P, "gpsimd", wcr_s[:, dc * D:(dc + 1) * D],
                 wcrt[dc * 128:(dc + 1) * 128, :])
    for TT in range(8):
        t0 = TT * 512
        lo = max(0, t0 - 256)
        for ic in range(8):
            P.wait("sync", "vector", P.cnt["vector"])
            wx = P.dma_ch("sync", "ld%d" % (ic % 2), ywin2[ic % 2][:, 0:t0 + 512 - lo],
                          yyf[ic, :, lo:t0 + 512])
            P.wait_ch("vector", "ld%d" % (ic % 2), wx)
            cur = ywin2[ic % 2][:, 256:768] if TT > 0 else ywin2[ic % 2][:, 0:512]
            dsl = rm2_s[:, ic * 512:(ic + 1) * 512]
            if TT == 0:
                P.emit("vector", lambda v, dsl=dsl, cur=cur, ic=ic: v.tensor_scalar(
                    out=dsl[:, 0:256], in0=cur[:, 0:256], scalar1=cmr[:, ic:ic + 1],
                    scalar2=None, op0=ALU.mult))
                sh = ywin2[ic % 2][:, 0:256]
                cu2 = ywin2[ic % 2][:, 256:512]
                P.emit("vector", lambda v, cu2=cu2, sh=sh: v.tensor_tensor(
                    out=dtmp2[:, 0:256], in0=cu2, in1=sh, op=ALU.subtract))
                P.emit("vector", lambda v, dsl=dsl, sh=sh, ic=ic:
                       v.scalar_tensor_tensor(
                           out=dsl[:, 256:512], in0=dtmp2[:, 0:256],
                           scalar=cmr[:, ic:ic + 1], in1=sh,
                           op0=ALU.mult, op1=ALU.add))
            else:
                sh = ywin2[ic % 2][:, 0:512]
                P.emit("vector", lambda v, cur=cur, sh=sh: v.tensor_tensor(
                    out=dtmp2, in0=cur, in1=sh, op=ALU.subtract))
                P.emit("vector", lambda v, dsl=dsl, sh=sh, ic=ic:
                       v.scalar_tensor_tensor(
                           out=dsl, in0=dtmp2, scalar=cmr[:, ic:ic + 1],
                           in1=sh, op0=ALU.mult, op1=ALU.add))
        P.wait("sync", "scalar", P.cnt["scalar"])
        wq = 0
        for fc in range(32):
            wq = dma(P, "sync", sq_s[:, fc * 512:(fc + 1) * 512],
                     sqb[fc, :, t0:t0 + 512])
        vmix = P.cnt["vector"]
        P.wait("tensor", "vector", vmix)
        P.wait("tensor", "sync", wq)
        if TT == 0:
            P.wait("tensor", "gpsimd", wW)
        for oc in range(8):
            for fc in range(32):
                mm(P, pst0[:, :],
                   wcv_s[:, fc * D + oc * 128: fc * D + oc * 128 + 128],
                   sq_s[:, fc * 512:(fc + 1) * 512], fc == 0, fc == 31,
                   inc=(1 if fc == 31 else 0))
            vvw = P.cnt["tensor"]
            for ic in range(8):
                mm(P, pst1[:, :],
                   wcr_s[:, ic * D + oc * 128: ic * D + oc * 128 + 128],
                   rm2_s[:, ic * 512:(ic + 1) * 512], ic == 0, ic == 7,
                   inc=(1 if ic == 7 else 0))
            rrw = P.cnt["tensor"]
            P.wait("scalar", "tensor", rrw)
            ars = P.emit("scalar", lambda s: s.activation(
                out=rs_t, in_=pst1[:, :], func=AF.Sigmoid), inc=1)
            P.wait("sync", "vector", P.cnt["vector"])
            wx2 = P.dma_ch("sync", "ld0", x2t2, x2f[oc, :, t0:t0 + 512])
            P.wait_ch("vector", "ld0", wx2)
            P.wait("vector", "scalar", ars)
            P.wait("vector", "tensor", vvw)
            P.emit("vector", lambda v: v.tensor_tensor(
                out=s1_t, in0=rs_t, in1=pst0[:, :], op=ALU.mult))
            vs1 = P.emit("vector", lambda v: v.tensor_tensor(
                out=s1_t, in0=s1_t, in1=x2t2, op=ALU.add), inc=1)
            P.wait("sync", "vector", vs1)
            w_st = dma(P, "sync", s1f[oc, :, t0:t0 + 512], s1_t)
            P.wait("vector", "sync", w_st)
            P.wait("tensor", "vector", vs1)
    P.barrier()
    A.release()

    A.mark()
    w1_s = A.b16(128, 8 * FF2)
    w2_s = A.b16(128, 32 * D)
    b1row = A.b16(1, FF2)
    oneb = A.b16(1, 512)
    b2row = A.f32(1, D)
    thc = A.f32(128, 32)
    a2c = A.f32(128, 32)
    qtmp1 = A.f32(128, 32)
    qtmp2 = A.f32(128, 32)
    qr_s = A.b16(128, 32 * 256)
    xn1t2 = A.b16(128, 256)
    h_t = A.f32(128, 256)
    a_t = A.f32(128, 256)
    s4_t = A.f32(128, 256)
    s2_t = A.f32(128, 256)
    c2_t = A.f32(128, 256)
    g1_t = A.f32(128, 256)
    s1t2 = A.f32(128, 256)
    tr_t = A.f32(128, 256)
    oasm = A.f32(128, 2 * D)
    x1t = A.f32(128, D)
    qv_s = A.f32(128, 32 * 3)

    wW = dma(P, "gpsimd", b1row, b1_r)
    wW = dma(P, "sync", b2row, b2_r)
    P.wait("vector", "gpsimd", P.cnt["gpsimd"])
    P.emit("vector", lambda v: v.memset(oneb, 1.0))
    wW = dma(P, "sync", qv_s[:, 0:32], qph_f)
    wW = dma(P, "sync", qv_s[:, 32:64], qwr_f)
    wW = dma(P, "sync", qv_s[:, 64:96], qwi_f)
    for dc in range(8):
        dma(P, "gpsimd", w1_s[:, dc * FF2:(dc + 1) * FF2],
            w1_d[dc * 128:(dc + 1) * 128, :])
    wW2 = 0
    for fc in range(32):
        wW2 = dma(P, "gpsimd", w2_s[:, fc * D:(fc + 1) * D],
                  w2_d[fc * 128:(fc + 1) * 128, :])
    PI = float(np.pi)
    P.wait("vector", "sync", wW)
    P.emit("vector", lambda v: v.tensor_copy(out=thc, in_=qv_s[:, 0:32]))
    for _ in range(5):
        P.emit("vector", lambda v: v.tensor_single_scalar(
            out=qtmp1, in_=thc, scalar=PI, op=ALU.is_gt))
        P.emit("vector", lambda v: v.scalar_tensor_tensor(
            out=thc, in0=qtmp1, scalar=-2 * PI, in1=thc, op0=ALU.mult, op1=ALU.add))
        P.emit("vector", lambda v: v.tensor_single_scalar(
            out=qtmp1, in_=thc, scalar=-PI, op=ALU.is_lt))
        P.emit("vector", lambda v: v.scalar_tensor_tensor(
            out=thc, in0=qtmp1, scalar=2 * PI, in1=thc, op0=ALU.mult, op1=ALU.add))
    vth = P.cnt["vector"]
    P.wait("scalar", "vector", vth)
    asn = P.emit("scalar", lambda s: s.activation(out=qtmp1, in_=thc, func=AF.Sin),
                 inc=1)
    ash = P.emit("scalar", lambda s: s.activation(out=qtmp2, in_=thc, func=AF.Sin,
                                                  scale=0.5), inc=1)
    P.wait("vector", "scalar", ash)
    P.emit("vector", lambda v: v.tensor_tensor(out=qtmp2, in0=qtmp2, in1=qtmp2,
                                               op=ALU.mult))
    P.emit("vector", lambda v: v.tensor_scalar(out=qtmp2, in0=qtmp2, scalar1=-2.0,
                                               scalar2=1.0, op0=ALU.mult,
                                               op1=ALU.add))
    P.emit("vector", lambda v: v.tensor_tensor(out=qtmp2, in0=qtmp2,
                                               in1=qv_s[:, 32:64], op=ALU.mult))
    P.emit("vector", lambda v: v.tensor_tensor(out=qtmp1, in0=qtmp1,
                                               in1=qv_s[:, 64:96], op=ALU.mult))
    P.emit("vector", lambda v: v.tensor_tensor(out=a2c, in0=qtmp2, in1=qtmp1,
                                               op=ALU.add))
    P.emit("vector", lambda v: v.tensor_tensor(out=a2c, in0=a2c,
                                               in1=qv_s[:, 64:96], op=ALU.mult))
    P.emit("vector", lambda v: v.tensor_scalar(out=a2c, in0=a2c, scalar1=0.2,
                                               scalar2=None, op0=ALU.mult))

    xnt_s = A.b16(128, 8 * 256)

    for T4 in range(16):
        t0 = T4 * 256
        P.wait("sync", "tensor", P.cnt["tensor"])
        wxt = 0
        for ic in range(8):
            wxt = dma(P, "sync", xnt_s[:, ic * 256:(ic + 1) * 256],
                      xn1b[ic, :, t0:t0 + 256])
        P.wait("tensor", "sync", wxt)
        if T4 == 0:
            P.wait("tensor", "gpsimd", wW2)
            P.wait("tensor", "vector", P.cnt["vector"])
        ahel = []
        for fc in range(32):
            psm = PST[fc % 2]
            if fc >= 2:
                P.wait("tensor", "scalar", ahel[fc - 2])
            for ic in range(8):
                mm(P, psm[:, 0:256],
                   w1_s[:, ic * FF2 + fc * 128: ic * FF2 + fc * 128 + 128],
                   xnt_s[:, ic * 256:(ic + 1) * 256], ic == 0, False)
            hw = mm(P, psm[:, 0:256], b1row[0:1, fc * 128:(fc + 1) * 128],
                    oneb[0:1, 0:256], False, True, inc=1)
            P.wait("scalar", "tensor", hw)
            if fc >= 1:
                P.wait("scalar", "vector", vqr)
            ahe = P.emit("scalar", lambda s, psm=psm: s.copy(
                out=h_t, in_=psm[:, 0:256]), inc=1)
            ahel.append(ahe)
            P.wait("vector", "scalar", ahe)
            P.emit("vector", lambda v, fc=fc: v.tensor_scalar(
                out=a_t, in0=h_t, scalar1=0.1, scalar2=thc[:, fc:fc + 1],
                op0=ALU.mult, op1=ALU.add))
            va = P.cnt["vector"]
            P.wait("scalar", "vector", va)
            as2 = P.emit("scalar", lambda s: s.activation(
                out=s2_t, in_=a_t, func=AF.Sin, scale=0.5), inc=1)
            as4 = P.emit("scalar", lambda s: s.activation(
                out=s4_t, in_=a_t, func=AF.Sin, scale=0.25), inc=1)
            P.wait("vector", "scalar", as4)
            P.emit("vector", lambda v: v.tensor_tensor(
                out=c2_t, in0=s4_t, in1=s4_t, op=ALU.mult))
            P.emit("vector", lambda v: v.tensor_scalar(
                out=c2_t, in0=c2_t, scalar1=-2.0, scalar2=1.0,
                op0=ALU.mult, op1=ALU.add))
            P.emit("vector", lambda v: v.tensor_tensor(
                out=c2_t, in0=c2_t, in1=s2_t, op=ALU.mult))
            P.emit("vector", lambda v, fc=fc: v.scalar_tensor_tensor(
                out=c2_t, in0=c2_t, scalar=a2c[:, fc:fc + 1], in1=h_t,
                op0=ALU.mult, op1=ALU.add))
            vqr = P.emit("vector", lambda v, fc=fc: v.tensor_scalar(
                out=qr_s[:, fc * 256:(fc + 1) * 256], in0=c2_t, scalar1=0.0,
                scalar2=None, op0=ALU.max), inc=1)
        P.wait("tensor", "vector", P.cnt["vector"])
        asll = []
        for oc in range(8):
            psm = PST[oc % 2]
            if oc >= 2:
                P.wait("tensor", "scalar", asll[oc - 2])
            for fc in range(32):
                mm(P, psm[:, 0:256],
                   w2_s[:, fc * D + oc * 128: fc * D + oc * 128 + 128],
                   qr_s[:, fc * 256:(fc + 1) * 256], fc == 0, False)
            ow = mm(P, psm[:, 0:256], b2row[0:1, oc * 128:(oc + 1) * 128],
                    onef[0:1, 0:256], False, True, inc=1)
            P.wait("scalar", "tensor", ow)
            if oc >= 1:
                P.wait("scalar", "vector", vtr[-1])
            asl = P.emit("scalar", lambda s, psm=psm: s.activation(
                out=g1_t, in_=psm[:, 0:256], func=AF.Silu), inc=1)
            asll.append(asl)
            P.wait("sync", "vector", P.cnt["vector"])
            ws1 = P.dma_ch("sync", "ld0", s1t2, s1f[oc, :, t0:t0 + 256])
            P.wait_ch("vector", "ld0", ws1)
            P.wait("vector", "scalar", asl)
            if oc == 0:
                vtr = []
            vtr.append(P.emit("vector", lambda v: v.tensor_tensor(
                out=tr_t, in0=g1_t, in1=s1t2, op=ALU.mult), inc=1))
            P.wait("tensor", "vector", vtr[-1])
            for ts_ in range(2):
                regt = pbig[:, (ts_ % 2) * 512:(ts_ % 2) * 512 + 128]
                tw = transp(P, regt, tr_t[:, ts_ * 128:(ts_ + 1) * 128], idf)
                P.wait("vector", "tensor", tw)
                P.emit("vector", lambda v, regt=regt, ts_=ts_, oc=oc: v.tensor_copy(
                    out=oasm[:, ts_ * D + oc * 128: ts_ * D + oc * 128 + 128],
                    in_=regt), inc=1)
            P.wait("tensor", "vector", P.cnt["vector"])
        for ts_ in range(2):
            tg = t0 + ts_ * 128
            bb_, jj = tg // JPC, tg % JPC
            P.wait("sync", "vector", P.cnt["vector"])
            wx1 = P.dma_ch("sync", "ld1", x1t, xs_in[bb_, jj:jj + 128, :])
            P.wait_ch("vector", "ld1", wx1)
            vo = P.emit("vector", lambda v, ts_=ts_: v.tensor_tensor(
                out=oasm[:, ts_ * D:(ts_ + 1) * D],
                in0=oasm[:, ts_ * D:(ts_ + 1) * D], in1=x1t, op=ALU.add), inc=1)
            P.wait("sync", "vector", vo)
            dma(P, "sync", out_d[bb_, jj:jj + 128, :], oasm[:, ts_ * D:(ts_ + 1) * D])
        P.wait("vector", "sync", P.cnt["sync"])
    A.release()
    A.release()


def build_l1(debug=False):
    env = _mk_env()
    P, nc, ctx = env["P"], env["nc"], env["ctx"]
    emit_l1(env, debug=debug)
    fin = P.cnt["sync"]
    P.wait("sync", "sync", fin)
    P.wait("gpsimd", "sync", fin)
    block = ctx.enter_context(nc.Block())
    P.finalize(block)
    ctx.close()
    return nc


def build_fused(debug=False):
    env = _mk_env()
    P, nc, ctx = env["P"], env["nc"], env["ctx"]
    x1loc = nc.dram_tensor("x1loc", [B, JPC, D], F32, kind="Internal").ap()
    shared_d = emit_l0(env, debug=debug, x1loc=x1loc)
    P.barrier()
    emit_l1(env, debug=debug, xs_src=x1loc, shared_d=shared_d)
    fin = P.cnt["sync"]
    P.wait("sync", "sync", fin)
    P.wait("gpsimd", "gpsimd", P.cnt["gpsimd"])
    P.wait("gpsimd", "sync", fin)
    block = ctx.enter_context(nc.Block())
    P.finalize(block)
    ctx.close()
    return nc


def _l0_maps(inputs):
    import ml_dtypes
    f = np.float32
    c = np.ascontiguousarray
    g = lambda k: np.asarray(inputs[k], dtype=f)
    shared = dict(
        wqt=c(g("L0_attn_Wq").T), wkt=c(g("L0_attn_Wk").T),
        wvt=c(g("L0_attn_Wv").T), wot=c(g("L0_attn_Wo").T),
        bq=g("L0_attn_bq").reshape(1, D), bk=g("L0_attn_bk").reshape(1, D),
        bv=g("L0_attn_bv").reshape(1, D), bo=g("L0_attn_bo").reshape(1, D),
        wqst=c(g("L0_attn_Wqs").T), wkst=c(g("L0_attn_Wks").T),
        wvst=c(g("L0_attn_Wvs").T),
        bqs=g("L0_attn_bqs").reshape(1, 3), bks=g("L0_attn_bks").reshape(1, 3),
        bvs=g("L0_attn_bvs").reshape(1, 3),
        temp=g("L0_attn_temp").reshape(1, 1),
        rw=c(g("L0_ffn_router_w")), rb=g("L0_ffn_router_b").reshape(1, E),
        w1=c(g("L0_ffn_w1")), b1r=c(g("L0_ffn_b1").reshape(E, 1, FF1)),
        w2=c(g("L0_ffn_w2")), b2r=c(g("L0_ffn_b2").reshape(E, 1, D)),
        iden=np.eye(128, dtype=ml_dtypes.bfloat16),
        idenf=np.eye(128, dtype=f),
        onesf=np.ones((1, 512), f),
        iota=np.broadcast_to(np.arange(E, dtype=f), (128, E)).copy(),
        iotam=np.broadcast_to(np.arange(E, dtype=f) - 999.0, (128, E)).copy(),
    )
    x = np.asarray(inputs["L0_attn_x"], dtype=f)
    return [dict(shared, x=c(np.roll(x, -i * JPC, axis=1)))
            for i in range(NCORES)]


def _l1_maps(inputs, x1=None):
    f = np.float32
    c = np.ascontiguousarray
    g = lambda k: np.asarray(inputs[k], dtype=f)
    fm8 = lambda k: c(g(k).reshape(8, 128).T)
    fm32 = lambda k: c(g(k).reshape(32, 128).T)
    shared = dict(
        ln1g=g("L1_attn_ln1_g").reshape(1, D), ln1b=g("L1_attn_ln1_b").reshape(1, D),
        ln2gf=fm8("L1_attn_ln2_g"), ln2bf=fm8("L1_attn_ln2_b"),
        tmkf=fm8("L1_attn_time_mix_k"), tmvf=fm8("L1_attn_time_mix_v"),
        tmrf=fm8("L1_attn_time_mix_r"), tdf=fm8("L1_attn_time_decay"),
        tff=fm8("L1_attn_time_first"),
        cmkf=fm8("L1_attn_cmix_k"), cmrf=fm8("L1_attn_cmix_r"),
        wkt1=c(g("L1_attn_Wk").T), wvt1=c(g("L1_attn_Wv").T),
        wrt1=c(g("L1_attn_Wr").T), wot1=c(g("L1_attn_Wo").T),
        wckt=c(g("L1_attn_Wck").T), wcvt=c(g("L1_attn_Wcv").T),
        wcrt=c(g("L1_attn_Wcr").T),
        w1f=c(g("L1_ffn_W1")), w2f=c(g("L1_ffn_W2")),
        b1f=g("L1_ffn_b1").reshape(1, FF2), b2f=g("L1_ffn_b2").reshape(1, D),
        qphf=fm32("L1_ffn_quantum_phases"),
        qwrf=fm32("L1_ffn_quantum_weights_real"),
        qwif=fm32("L1_ffn_quantum_weights_imag"),
        idenf=np.eye(128, dtype=f),
        onesf=np.ones((1, 512), f),
        onescol=np.ones((128, 1), f),
    )
    maps = []
    for core in range(NCORES):
        m = dict(shared)
        if x1 is not None:
            m["xs"] = c(x1[:, core * JPC:(core + 1) * JPC, :])
        maps.append(m)
    return maps


def kernel(**inputs):
    maps0 = _l0_maps(inputs)
    maps1 = _l1_maps(inputs)
    maps = [dict(m0, **m1) for m0, m1 in zip(maps0, maps1)]
    nc = build_fused()
    res = run_bass_kernel_spmd(nc, maps, core_ids=list(range(NCORES)))
    out = np.concatenate([r["out"] for r in res.results], axis=1)
    return out


def kernel_unfused(**inputs):
    maps0 = _l0_maps(inputs)
    nc0 = build_l0()
    res0 = run_bass_kernel_spmd(nc0, maps0, core_ids=list(range(NCORES)))
    maps1 = _l1_maps(inputs)
    for i in range(NCORES):
        maps1[i]["xs"] = np.ascontiguousarray(res0.results[i]["x1"])
    nc1 = build_l1()
    res1 = run_bass_kernel_spmd(nc1, maps1, core_ids=list(range(NCORES)))
    out = np.concatenate([r["out"] for r in res1.results], axis=1)
    return out


# version 05
# speedup vs baseline: 2.2202x; 1.2026x over previous
import numpy as np
from contextlib import ExitStack

import concourse.bass as bass
import concourse.mybir as mybir
from concourse.bass_utils import run_bass_kernel_spmd

F32 = mybir.dt.float32
BF16 = mybir.dt.bfloat16
AF = mybir.ActivationFunctionType
ALU = mybir.AluOpType
AX = mybir.AxisListType

NCORES = 8
B, T, D = 16, 2048, 1024
BPC = B // NCORES
JPC = T // NCORES
H, HD = 16, 64
E, FF1 = 16, 2048
FF2 = 4096
INV_D = 1.0 / D
EPS = 1e-5

ENGS = ("sync", "tensor", "vector", "scalar", "gpsimd")
ARENA_COLS = 53056


class Arena:
    def __init__(self, ap_f32):
        self.f = ap_f32
        self.bf = ap_f32.bitcast(BF16)
        self.off = 0
        self.marks = []

    def mark(self):
        self.marks.append(self.off)

    def release(self):
        self.off = self.marks.pop()

    def f32(self, parts, cols):
        off = (self.off + 3) & ~3
        self.off = off + 4 * cols
        assert self.off <= ARENA_COLS * 4, "arena overflow"
        return self.f[0:parts, off // 4: off // 4 + cols]

    def b16(self, parts, cols):
        off = (self.off + 3) & ~3
        self.off = off + 2 * cols
        self.off = (self.off + 3) & ~3
        assert self.off <= ARENA_COLS * 4, "arena overflow"
        return self.bf[0:parts, off // 2: off // 2 + cols]


class Plan:
    def __init__(self, nc, ctx):
        self.nc = nc
        self.ops = {e: [] for e in ENGS}
        self.sem = {e: ctx.enter_context(nc.semaphore("sem_" + e)) for e in ENGS}
        self.cnt = {e: 0 for e in ENGS}
        self.bsem = {e: ctx.enter_context(nc.semaphore("bar_" + e))
                     for e in ("sync", "gpsimd")}
        self.bcnt = {e: 0 for e in ("sync", "gpsimd")}
        self.ch = {c: ctx.enter_context(nc.semaphore("ch_" + c))
                   for c in ("ld0", "ld1")}
        self.chcnt = {c: 0 for c in self.ch}

    def dma_ch(self, queue, chan, out, in_, **kw):
        sem = self.ch[chan]
        prev = self.chcnt[chan]
        if prev:
            self.ops[queue].append(
                lambda e, sem=sem, prev=prev: e.wait_ge(sem, prev))
        self.chcnt[chan] += 16

        def fn(e, out=out, in_=in_, kw=kw, sem=sem):
            e.dma_start(out=out, in_=in_, **kw).then_inc(sem, 16)

        self.ops[queue].append(fn)
        return self.chcnt[chan]

    def wait_ch(self, eng, chan, value):
        if value <= 0:
            return
        sem = self.ch[chan]
        self.ops[eng].append(lambda e, sem=sem, value=value: e.wait_ge(sem, value))

    def emit(self, eng, fn, inc=0):
        if eng == "vector":
            sem = self.sem[eng]
            prev = self.cnt[eng]
            if prev:
                self.ops[eng].append(
                    lambda e, sem=sem, prev=prev: e.wait_ge(sem, prev))
            self.cnt[eng] += 1
            self.ops[eng].append(
                lambda e, fn=fn, sem=sem: fn(e).then_inc(sem, 1))
            return self.cnt[eng]
        if inc:
            sem = self.sem[eng]
            prev = self.cnt[eng]
            if eng in ("tensor", "scalar") and prev:
                self.ops[eng].append(
                    lambda e, sem=sem, prev=prev: e.wait_ge(sem, prev))
            self.cnt[eng] += inc

            def wrapped(e, fn=fn, sem=sem, inc=inc):
                fn(e).then_inc(sem, inc)

            self.ops[eng].append(wrapped)
            return self.cnt[eng]
        self.ops[eng].append(fn)
        return self.cnt[eng]

    def wait(self, eng, producer_eng, value):
        if value <= 0:
            return
        if producer_eng in ("sync", "gpsimd"):
            value = self.cnt[producer_eng]
        sem = self.sem[producer_eng]
        self.ops[eng].append(lambda e, sem=sem, value=value: e.wait_ge(sem, value))

    def barrier(self):
        for eng in ENGS:
            if eng in self.bsem:
                sem_d = self.sem[eng]
                val_d = self.cnt[eng]
                if val_d:
                    self.ops[eng].append(
                        lambda e, sem=sem_d, val=val_d: e.wait_ge(sem, val))
                self.bcnt[eng] += 1
                sem = self.bsem[eng]
                self.ops[eng].append(
                    lambda e, sem=sem: e.nop().then_inc(sem, 1))
            else:
                self.emit(eng, lambda e: e.nop(), inc=1)
        snap = dict(self.cnt)
        bsnap = dict(self.bcnt)
        for eng in ENGS:
            for other in ENGS:
                if other == eng:
                    continue
                if other in self.bsem:
                    sem = self.bsem[other]
                    val = bsnap[other]
                    self.ops[eng].append(
                        lambda e, sem=sem, val=val: e.wait_ge(sem, val))
                else:
                    self.wait(eng, other, snap[other])

    def finalize(self, block):
        for eng in ENGS:
            ops = self.ops[eng]
            getattr(block, eng)(lambda engobj, ops=ops: [fn(engobj) for fn in ops])


def dma(P, queue, out, in_, **kw):
    P.wait(queue, queue, P.cnt[queue])
    return P.emit(
        queue, lambda e, out=out, in_=in_, kw=kw: e.dma_start(out=out, in_=in_, **kw),
        inc=16)


def mm(P, out, lhsT, rhs, start, stop, inc=0):
    return P.emit("tensor", lambda te, out=out, l=lhsT, r=rhs, a=start, b=stop:
                  te.matmul(out, l, r, start=a, stop=b), inc=inc)


def transp(P, out, in_, iden, inc=1):
    return P.emit("tensor", lambda te, out=out, i=in_, d=iden:
                  te.transpose(out, i, d), inc=inc)


def _mk_env():
    nc = bass.Bass("TRN2", target_bir_lowering=False)
    ctx = ExitStack()
    arena_t = ctx.enter_context(nc.sbuf_tensor("arena", [128, ARENA_COLS], F32))
    A = Arena(arena_t[:, :])
    pbig = ctx.enter_context(nc.psum_tensor("pbig", [128, 2048], F32))
    pst0 = ctx.enter_context(nc.psum_tensor("pst0", [128, 512], F32))
    pst1 = ctx.enter_context(nc.psum_tensor("pst1", [128, 512], F32))
    ppv = ctx.enter_context(nc.psum_tensor("ppv", [128, 512], F32))
    pbf = ctx.enter_context(nc.psum_tensor("pbf", [128, 512], BF16))
    P = Plan(nc, ctx)
    return dict(nc=nc, ctx=ctx, A=A, P=P, pbig=pbig, pst0=pst0, pst1=pst1,
                ppv=ppv, pbf=pbf)


def emit_l0(env, debug=False, x1loc=None):
    nc, ctx, A, P = env["nc"], env["ctx"], env["A"], env["P"]
    pbig, pst0, pst1, ppv, pbf = (env["pbig"], env["pst0"], env["pst1"],
                                  env["ppv"], env["pbf"])
    di = lambda n, s, d=F32: nc.dram_tensor(n, s, d, kind="ExternalInput").ap()

    x_in = di("x", [B, T, D])
    wqt, wkt, wvt, wot = (di(n, [D, D]) for n in ("wqt", "wkt", "wvt", "wot"))
    bq, bk, bv, bo = (di(n, [1, D]) for n in ("bq", "bk", "bv", "bo"))
    wqst, wkst, wvst = (di(n, [HD, 3]) for n in ("wqst", "wkst", "wvst"))
    bqs, bks, bvs = (di(n, [1, 3]) for n in ("bqs", "bks", "bvs"))
    wkst_r1, wkst_r2, wvst_r1, wvst_r2 = (
        di(n, [HD, 3]) for n in ("wkst_r1", "wkst_r2", "wvst_r1", "wvst_r2"))
    bks_r1, bks_r2, bvs_r1, bvs_r2 = (
        di(n, [1, 3]) for n in ("bks_r1", "bks_r2", "bvs_r1", "bvs_r2"))
    bqc_d, bkc_d, bvc_d = (di(n, [128, 8]) for n in ("bqc", "bkc", "bvc"))
    temp = di("temp", [1, 1])
    rw_d = di("rw", [D, E])
    rb_d = di("rb", [1, E])
    w1_d = di("w1", [E, D, FF1])
    b1_d = di("b1r", [E, 1, FF1])
    w2_d = di("w2", [E, FF1, D])
    b2_d = di("b2r", [E, 1, D])
    idb_d = di("iden", [128, 128], BF16)
    idf_d = di("idenf", [128, 128])
    onef_d = di("onesf", [1, 512])
    iota_d = di("iota", [128, E])
    iotam_d = di("iotam", [128, E])

    if x1loc is None:
        x1loc = nc.dram_tensor("x1", [B, JPC, D], F32, kind="ExternalOutput").ap()

    def scr(name, shape, dtyp=F32):
        kind = "ExternalOutput" if debug else "Internal"
        return nc.dram_tensor(name, shape, dtyp, kind=kind).ap()

    xnbf = scr("xnbf", [B, 8, 128, T], BF16)
    stm = scr("stm", [B, JPC, D])
    cdr = scr("cdr", [B, E, JPC])
    PST = (pst0, pst1)
    NTL = JPC // 128

    A.mark()
    idb = A.b16(128, 128)
    idf = A.f32(128, 128)
    onef = A.f32(1, 512)
    iota_s = A.f32(128, E)
    iotam_s = A.f32(128, E)
    tempv = A.f32(128, 1)
    rw_s = A.f32(128, 8 * E)
    rb_s = A.f32(1, E)
    bo_s = A.f32(1, D)
    bqc = A.f32(128, 8)
    bkc = A.f32(128, 8)
    bvc = A.f32(128, 8)
    wqs6 = A.b16(65, 6)
    wkab = A.b16(65, 6)
    wvba = A.b16(65, 6)
    bqs_s = A.f32(1, 3)
    eps_c = A.f32(128, 1)

    dma(P, "sync", idb, idb_d)
    dma(P, "sync", idf, idf_d)
    dma(P, "sync", onef, onef_d)
    dma(P, "sync", iota_s, iota_d)
    dma(P, "sync", iotam_s, iotam_d)
    dma(P, "sync", tempv, temp.partition_broadcast(128))
    dma(P, "sync", rb_s, rb_d)
    for dc in range(8):
        dma(P, "sync", rw_s[:, dc * E:(dc + 1) * E], rw_d[dc * 128:(dc + 1) * 128, :])
    dma(P, "sync", bo_s, bo)
    dma(P, "sync", bqc, bqc_d)
    dma(P, "sync", bkc, bkc_d)
    dma(P, "sync", bvc, bvc_d)
    dma(P, "gpsimd", wqs6[0:64, 0:3], wqst)
    dma(P, "gpsimd", wqs6[0:64, 3:6], wqst)
    dma(P, "gpsimd", wkab[0:64, 0:3], wkst_r1)
    dma(P, "gpsimd", wkab[0:64, 3:6], wkst_r2)
    dma(P, "gpsimd", wkab[64:65, 0:3], bks_r1)
    dma(P, "gpsimd", wkab[64:65, 3:6], bks_r2)
    dma(P, "gpsimd", wvba[0:64, 0:3], wvst_r2)
    dma(P, "gpsimd", wvba[0:64, 3:6], wvst_r1)
    dma(P, "gpsimd", wvba[64:65, 0:3], bvs_r2)
    dma(P, "gpsimd", wvba[64:65, 3:6], bvs_r1)
    w_pro = dma(P, "sync", bqs_s, bqs)
    g_pro = P.cnt["gpsimd"]
    P.wait("vector", "sync", w_pro)
    P.wait("vector", "gpsimd", g_pro)
    P.emit("vector", lambda v: v.tensor_scalar(
        out=wqs6[64:65, 0:3], in0=bqs_s, scalar1=tempv[0:1, 0:1],
        scalar2=None, op0=ALU.mult))
    P.emit("vector", lambda v: v.tensor_scalar(
        out=wqs6[64:65, 3:6], in0=bqs_s, scalar1=tempv[0:1, 0:1],
        scalar2=None, op0=ALU.mult))
    P.emit("vector", lambda v: v.tensor_scalar(
        out=wvba[0:65, 3:6], in0=wvba[0:65, 3:6], scalar1=-1.0,
        scalar2=None, op0=ALU.mult))
    P.emit("vector", lambda v: v.memset(eps_c, EPS))
    P.barrier()

    A.mark()
    xt_b = [A.f32(128, D) for _ in range(4)]
    xn_b = [A.f32(128, D) for _ in range(4)]
    xsq = A.f32(128, D)
    xnf = A.f32(128, D)
    stat = A.f32(128, 16)
    sd4 = A.f32(128, 4)
    rstd4 = A.f32(128, 4)
    msq4 = A.f32(128, 4)
    asm_b = A.b16(128, 8 * 512)
    Lb = A.f32(128, NTL * E)
    curb = A.f32(128, NTL * E)
    eqb = A.f32(128, NTL * E)
    ub = eqb
    umb = A.f32(128, NTL * E)
    maskb = A.f32(128, NTL * E)
    mx0 = A.f32(128, NTL)
    nmx0 = A.f32(128, NTL)
    mxr = A.f32(128, NTL)
    selb = A.f32(128, NTL)
    den16 = A.f32(128, NTL)
    rden = A.f32(128, NTL)
    ctmp = A.f32(16, 128)
    ctile = A.f32(128, E)
    Sl = lambda t_, ti: t_[:, ti * E:(ti + 1) * E]

    for b in range(B):
        prev_xn_reads = 0
        prev_sq = 0
        prev_lg = 0
        prev_vxn = 0
        prev_asm = 0
        for g in range(4):
            tis = [g * 4 + i for i in range(4)]
            P.wait("sync", "vector", prev_vxn)
            P.wait("sync", "scalar", prev_sq)
            w_x = 0
            for i, ti in enumerate(tis):
                w_x = dma(P, "sync", xt_b[i], x_in[b, ti * 128:(ti + 1) * 128, :])
            P.wait("vector", "sync", w_x)
            P.wait("scalar", "sync", w_x)
            for i in range(4):
                P.emit("vector", lambda v, i=i: v.reduce_sum(
                    out=stat[:, i:i + 1], in_=xt_b[i], axis=AX.X))
            vsq = []
            for i in range(4):
                if i >= 1:
                    P.wait("scalar", "vector", vsq[i - 1])
                sqw = P.emit("scalar", lambda s, i=i: s.activation(
                    out=xsq, in_=xt_b[i], func=AF.Square), inc=1)
                P.wait("vector", "scalar", sqw)
                vsq.append(P.emit("vector", lambda v, i=i: v.reduce_sum(
                    out=stat[:, 4 + i:5 + i], in_=xsq, axis=AX.X), inc=1))
            prev_sq = P.cnt["scalar"]
            vm = 0
            for i in range(4):
                vm = P.emit("vector", lambda v, i=i: v.tensor_scalar(
                    out=stat[:, 8 + i:9 + i], in0=stat[:, i:i + 1], scalar1=INV_D,
                    scalar2=None, op0=ALU.mult), inc=1)
            P.wait("scalar", "vector", vm)
            am = P.emit("scalar", lambda s: s.activation(
                out=msq4, in_=stat[:, 8:12], func=AF.Square), inc=1)
            P.wait("vector", "scalar", am)
            vnv = 0
            for i in range(4):
                vnv = P.emit("vector", lambda v, i=i: v.scalar_tensor_tensor(
                    out=stat[:, 12 + i:13 + i], in0=stat[:, 4 + i:5 + i],
                    scalar=INV_D, in1=msq4[:, i:i + 1],
                    op0=ALU.mult, op1=ALU.subtract), inc=1)
            P.wait("scalar", "vector", vnv)
            asd = P.emit("scalar", lambda s: s.activation(
                out=sd4, in_=stat[:, 12:16], func=AF.Sqrt, bias=eps_c), inc=1)
            P.wait("vector", "scalar", asd)
            P.wait("vector", "tensor", prev_xn_reads)
            P.wait("vector", "sync", prev_asm)
            P.emit("vector", lambda v: v.reciprocal(out=rstd4, in_=sd4))
            for i in range(4):
                P.emit("vector", lambda v, i=i: v.tensor_scalar(
                    out=xn_b[i], in0=xt_b[i], scalar1=stat[:, 8 + i:9 + i],
                    scalar2=None, op0=ALU.subtract))
            vxn = 0
            for i in range(4):
                vxn = P.emit("vector", lambda v, i=i: v.tensor_scalar(
                    out=xn_b[i], in0=xn_b[i], scalar1=rstd4[:, i:i + 1],
                    scalar2=None, op0=ALU.mult), inc=1)
            prev_vxn = vxn

            for i, ti in enumerate(tis):
                P.wait("tensor", "vector", vxn)
                P.wait("tensor", "scalar", P.cnt["scalar"])
                P.wait("tensor", "vector", P.cnt["vector"])
                is_loc = ti < NTL
                if is_loc:
                    P.wait("scalar", "tensor", prev_lg)
                evs = []
                for dc in range(8):
                    reg = pbig[:, (dc % 2) * 512:(dc % 2) * 512 + 128]
                    if dc >= 2:
                        P.wait("tensor", "scalar", evs[dc - 2])
                        P.wait("tensor", "vector", P.cnt["vector"])
                    tw = transp(P, reg, xn_b[i][:, dc * 128:(dc + 1) * 128], idf)
                    P.wait("scalar", "tensor", tw)
                    sf = P.emit("scalar", lambda s, dc=dc, reg=reg: s.copy(
                        out=xnf[:, dc * 128:(dc + 1) * 128], in_=reg), inc=1)
                    evs.append(sf)
                    P.wait("vector", "scalar", sf)
                    P.emit("vector", lambda v, dc=dc, i=i: v.tensor_copy(
                        out=asm_b[:, dc * 512 + i * 128: dc * 512 + i * 128 + 128],
                        in_=xnf[:, dc * 128:(dc + 1) * 128]))
                if is_loc:
                    lreg = pbig[:, 1920:1920 + E]
                    P.wait("tensor", "scalar", evs[-1])
                    for dc in range(8):
                        mm(P, lreg, xnf[:, dc * 128:(dc + 1) * 128],
                           rw_s[:, dc * E:(dc + 1) * E], dc == 0, False)
                    lg = mm(P, lreg, onef[0:1, 0:128], rb_s, False, True, inc=1)
                    prev_lg = lg
                    P.wait("vector", "tensor", lg)
                    P.emit("vector", lambda v, ti=ti, lreg=lreg: v.tensor_copy(
                        out=Sl(Lb, ti), in_=lreg), inc=1)
                prev_xn_reads = P.cnt["tensor"]
            P.wait("sync", "vector", P.cnt["vector"])
            for dc in range(8):
                prev_asm = dma(P, "sync", xnbf[b, dc, :, g * 512:(g + 1) * 512],
                               asm_b[:, dc * 512:(dc + 1) * 512])

        P.wait("vector", "vector", P.cnt["vector"])
        P.emit("vector", lambda v: v.tensor_copy(out=curb, in_=Lb))
        for ti in range(NTL):
            P.emit("vector", lambda v, ti=ti: v.reduce_max(
                out=nmx0[:, ti:ti + 1], in_=Sl(Lb, ti), axis=AX.X, negate=True))
        vtk = 0
        for r in range(4):
            for ti in range(NTL):
                P.emit("vector", lambda v, ti=ti: v.reduce_max(
                    out=mxr[:, ti:ti + 1], in_=Sl(curb, ti), axis=AX.X))
            if r == 0:
                P.emit("vector", lambda v: v.tensor_copy(out=mx0, in_=mxr))
            for ti in range(NTL):
                P.emit("vector", lambda v, ti=ti: v.tensor_scalar(
                    out=Sl(eqb, ti), in0=Sl(curb, ti), scalar1=mxr[:, ti:ti + 1],
                    scalar2=None, op0=ALU.is_equal))
            for ti in range(NTL):
                P.emit("vector", lambda v, ti=ti: v.tensor_tensor(
                    out=Sl(umb, ti), in0=Sl(eqb, ti), in1=iotam_s, op=ALU.mult))
            for ti in range(NTL):
                P.emit("vector", lambda v, ti=ti: v.tensor_scalar(
                    out=Sl(umb, ti), in0=Sl(umb, ti), scalar1=999.0, scalar2=None,
                    op0=ALU.add))
            for ti in range(NTL):
                P.emit("vector", lambda v, ti=ti: v.tensor_reduce(
                    out=selb[:, ti:ti + 1], in_=Sl(umb, ti), axis=AX.X, op=ALU.min))
            for ti in range(NTL):
                P.emit("vector", lambda v, ti=ti: v.tensor_scalar(
                    out=Sl(eqb, ti), in0=iota_s, scalar1=selb[:, ti:ti + 1],
                    scalar2=None, op0=ALU.is_equal))
            if r == 0:
                P.emit("vector", lambda v: v.tensor_copy(out=maskb, in_=eqb))
            else:
                vtk = P.emit("vector", lambda v: v.tensor_tensor(
                    out=maskb, in0=maskb, in1=eqb, op=ALU.add),
                    inc=(1 if r == 3 else 0))
            if r < 3:
                for ti in range(NTL):
                    P.emit("vector", lambda v, ti=ti: v.scalar_tensor_tensor(
                        out=Sl(curb, ti), in0=Sl(eqb, ti), scalar=-1e30,
                        in1=Sl(curb, ti), op0=ALU.mult, op1=ALU.add))
        P.wait("scalar", "vector", vtk)
        au = 0
        for ti in range(NTL):
            au = P.emit("scalar", lambda s, ti=ti: s.activation(
                out=Sl(ub, ti), in_=Sl(Lb, ti), func=AF.Exp,
                bias=nmx0[:, ti:ti + 1], scale=1.0), inc=1)
        P.wait("vector", "scalar", au)
        P.emit("vector", lambda v: v.tensor_tensor(out=ub, in0=ub, in1=maskb,
                                                   op=ALU.mult))
        for ti in range(NTL):
            P.emit("vector", lambda v, ti=ti: v.reduce_sum(
                out=den16[:, ti:ti + 1], in_=Sl(ub, ti), axis=AX.X))
        P.emit("vector", lambda v: v.reciprocal(out=rden, in_=den16))
        prev_ct = 0
        for ti in range(NTL):
            P.wait("vector", "gpsimd", prev_ct)
            vc = P.emit("vector", lambda v, ti=ti: v.tensor_scalar(
                out=ctile, in0=Sl(ub, ti), scalar1=rden[:, ti:ti + 1],
                scalar2=None, op0=ALU.mult), inc=1)
            P.wait("tensor", "vector", vc)
            tc = transp(P, pbig[0:E, 512:640], ctile, idf)
            P.wait("vector", "tensor", tc)
            vt2 = P.emit("vector", lambda v: v.tensor_copy(
                out=ctmp, in_=pbig[0:E, 512:640]), inc=1)
            P.wait("gpsimd", "vector", vt2)
            prev_ct = dma(P, "gpsimd", cdr[b, :, ti * 128:(ti + 1) * 128], ctmp)
        P.wait("vector", "gpsimd", prev_ct)
    P.barrier()
    A.release()

    A.mark()
    xnb = A.b16(128, 8 * T)
    ofm = A.b16(128, 8 * JPC)
    vt = A.b16(128, 16 * 65)
    pt = [A.b16(128, 256) for _ in range(2)]
    nmax = A.f32(128, 2)
    nmax8 = A.f32(128, 8)
    nm2 = A.f32(2, 128)
    wq_s = A.b16(128, 8 * D)
    wk_s = A.b16(128, 8 * D)
    wv_s = A.b16(128, 8 * D)
    wo_s = A.b16(128, 8 * D)
    q_sb = A.b16(65, 2 * JPC)
    k_sb = A.b16(65, 2 * T)
    v_sb = A.b16(65, 2 * T)
    skb_s = A.f32(6, T)
    pp_b = A.b16(6, T)
    qstp_b = A.b16(6, 2 * JPC)
    rc_row = A.f32(1, 256)
    ot_s = A.b16(64, 256)
    sf_s = A.f32(128, 256)
    stma = A.f32(128, 2 * D)

    P.emit("vector", lambda v: v.memset(
        vt.rearrange("p (k c) -> p k c", c=65)[:, :, 64:65], 1.0))
    P.emit("vector", lambda v: v.memset(k_sb[64:65, :], 1.0))
    P.emit("vector", lambda v: v.memset(v_sb[64:65, :], 1.0))
    wW = 0
    for dc in range(8):
        for (wsrc, wdst) in ((wqt, wq_s), (wkt, wk_s), (wvt, wv_s), (wot, wo_s)):
            wW = dma(P, "gpsimd", wdst[:, dc * D:(dc + 1) * D],
                     wsrc[dc * 128:(dc + 1) * 128, :])

    v_pbig = P.cnt["vector"]
    s_pst = 0
    v_pst = P.cnt["vector"]
    s_ppv = 0
    for b in range(B):
        P.wait("sync", "tensor", P.cnt["tensor"])
        P.wait("sync", "vector", P.cnt["vector"])
        wxb = 0
        for dc in range(8):
            wxb = dma(P, "sync", xnb[:, dc * T:(dc + 1) * T], xnbf[b, dc, :, :])
        for fc in range(8):
            if b == 0 and fc == 0:
                P.wait("tensor", "gpsimd", wW)
            P.wait("tensor", "sync", wxb)
            P.wait("tensor", "vector", v_pbig)
            ev_prev = P.cnt["vector"]
            for wave in range(2):
                w1 = 0
                for (pj, (wfull, dst, bcol)) in enumerate(
                        ((wk_s, k_sb, bkc), (wv_s, v_sb, bvc))):
                    for tgi in range(2):
                        tg = wave * 2 + tgi
                        reg = pbig[:, (pj * 2 + tgi) * 512:(pj * 2 + tgi + 1) * 512]
                        if wave == 1 and pj == 0 and tgi == 0:
                            P.wait("tensor", "vector", ev_prev)
                        for dc in range(8):
                            mm(P, reg,
                               wfull[:, dc * D + fc * 128: dc * D + fc * 128 + 128],
                               xnb[:, dc * T + tg * 512: dc * T + tg * 512 + 512],
                               dc == 0, dc == 7, inc=(1 if dc == 7 else 0))
                w1 = P.cnt["tensor"]
                P.wait("vector", "tensor", w1)
                for (pj, (wfull, dst, bcol)) in enumerate(
                        ((wk_s, k_sb, bkc), (wv_s, v_sb, bvc))):
                    for tgi in range(2):
                        tg = wave * 2 + tgi
                        reg = pbig[:, (pj * 2 + tgi) * 512:(pj * 2 + tgi + 1) * 512]
                        for hh2 in range(2):
                            ev_prev = P.emit(
                                "vector",
                                lambda v, reg=reg, dst=dst, tg=tg, hh2=hh2,
                                bcol=bcol, fc=fc:
                                v.tensor_scalar(
                                    out=dst[0:64, hh2 * T + tg * 512:
                                            hh2 * T + (tg + 1) * 512],
                                    in0=reg[64 * hh2:64 * hh2 + 64, :],
                                    scalar1=bcol[64 * hh2:64 * hh2 + 64, fc:fc + 1],
                                    scalar2=None, op0=ALU.add), inc=1)
            P.wait("tensor", "scalar", s_pst)
            P.wait("tensor", "vector", v_pst)
            reg = pst0[:, 0:256]
            wq3 = 0
            for dc in range(8):
                wq3 = mm(P, reg, wq_s[:, dc * D + fc * 128: dc * D + fc * 128 + 128],
                         xnb[:, dc * T: dc * T + 256], dc == 0, dc == 7,
                         inc=(1 if dc == 7 else 0))
            P.wait("vector", "tensor", wq3)
            for hh2 in range(2):
                P.emit("vector", lambda v, reg=reg, hh2=hh2, fc=fc: v.tensor_scalar(
                    out=q_sb[0:64, hh2 * JPC: hh2 * JPC + 256],
                    in0=reg[64 * hh2:64 * hh2 + 64, :],
                    scalar1=bqc[64 * hh2:64 * hh2 + 64, fc:fc + 1],
                    scalar2=tempv[0:64, 0:1], op0=ALU.add, op1=ALU.mult), inc=1)
            P.wait("vector", "gpsimd", P.cnt["gpsimd"])
            evq = P.emit("vector", lambda v: v.memset(q_sb[64:65, :], 1.0), inc=1)
            v_pst = evq
            P.wait("tensor", "vector", evq)
            wqs_mm = mm(P, pst1[0:6, 0:512], wqs6, q_sb[0:65, 0:512],
                        True, True, inc=1)
            P.wait("vector", "tensor", wqs_mm)
            evqs = P.emit("vector", lambda v: v.tensor_copy(
                out=qstp_b, in_=pst1[0:6, 0:512]), inc=1)
            v_pst = evqs
            for hh in range(2):
                ho, hq = hh * T, hh * JPC
                P.wait("tensor", "vector", v_pbig)
                stpw = 0
                for tg in range(4):
                    mm(P, pbig[0:6, tg * 512:(tg + 1) * 512], wkab,
                       k_sb[0:65, ho + tg * 512: ho + (tg + 1) * 512], True, True)
                    stpw = mm(P, pbig[32:38, tg * 512:(tg + 1) * 512], wvba,
                              v_sb[0:65, ho + tg * 512: ho + (tg + 1) * 512],
                              True, True, inc=1)
                P.wait("vector", "tensor", stpw)
                P.emit("vector", lambda v: v.tensor_copy(
                    out=skb_s, in_=pbig[0:6, 0:T]), inc=1)
                vcr = P.emit("vector", lambda v: v.tensor_tensor(
                    out=pp_b, in0=skb_s, in1=pbig[32:38, 0:T], op=ALU.mult),
                    inc=1)
                P.wait("tensor", "vector", vcr)
                rid = {}
                for qt in range(2):
                    for kg in range(4):
                        bank = pbig[:, kg * 512:(kg + 1) * 512]
                        if qt == 1:
                            P.wait("tensor", "vector", rid[(0, kg)])
                        mm(P, bank, q_sb[0:64, hq + qt * 128: hq + (qt + 1) * 128],
                           k_sb[0:64, ho + kg * 512: ho + (kg + 1) * 512],
                           True, False)
                        mw = mm(P, bank, qstp_b[:, hq + qt * 128: hq + (qt + 1) * 128],
                                pp_b[:, kg * 512:(kg + 1) * 512], False, True, inc=1)
                        P.wait("vector", "tensor", mw)
                        rid[(qt, kg)] = P.emit(
                            "vector", lambda v, qt=qt, kg=kg, bank=bank: v.reduce_max(
                                out=nmax8[:, 4 * qt + kg:4 * qt + kg + 1], in_=bank,
                                axis=AX.X, negate=True), inc=1)
                    P.emit("vector", lambda v, qt=qt: v.tensor_reduce(
                        out=nmax[:, qt:qt + 1], in_=nmax8[:, 4 * qt:4 * qt + 4],
                        op=ALU.min, axis=AX.X), inc=1)
                rmin1 = P.cnt["vector"]
                P.wait("tensor", "vector", rmin1)
                P.wait("tensor", "scalar", s_ppv)
                tn = transp(P, ppv[0:2, 0:128], nmax, idf)
                P.wait("vector", "tensor", tn)
                vnm = P.emit("vector", lambda v: v.tensor_copy(
                    out=nm2, in_=ppv[0:2, 0:128]), inc=1)
                P.wait("gpsimd", "vector", vnm)
                gn = dma(P, "gpsimd", q_sb[64:65, hq:hq + 256], nm2)
                P.wait("tensor", "vector", vnm)
                P.wait("tensor", "scalar", s_pst)
                vte = []
                vbanks = (pbf[:, :], pst0[:, :].bitcast(BF16),
                          pst1[:, :].bitcast(BF16), ppv[:, :].bitcast(BF16))
                for g in range(4):
                    bank = vbanks[g]
                    tpw = 0
                    for j in range(4):
                        kc = 4 * g + j
                        tpw = transp(P, bank[:, j * 64:(j + 1) * 64],
                                     v_sb[0:64, ho + kc * 128: ho + (kc + 1) * 128],
                                     idb[0:64, 0:64], inc=1)
                    P.wait("vector", "tensor", tpw)
                    vte.append(P.emit(
                        "vector", lambda v, bank=bank, g=g: v.tensor_copy(
                            out=vt.rearrange("p (k c) -> p k c", c=65)[
                                :, 4 * g:4 * g + 4, 0:64],
                            in_=bank[:, 0:256].rearrange("p (k c) -> p k c", c=64)),
                        inc=1))
                v_pst = vte[2]
                aexp, pvw = [], []
                for kc in range(16):
                    reg = PST[kc % 2][:, 0:256]
                    if kc >= 2:
                        P.wait("tensor", "scalar", aexp[kc - 2])
                    elif kc == 0:
                        P.wait("tensor", "vector", vte[1])
                        P.wait("tensor", "gpsimd", gn)
                    else:
                        P.wait("tensor", "vector", vte[2])
                    mm(P, reg, k_sb[0:65, ho + kc * 128: ho + (kc + 1) * 128],
                       q_sb[0:65, hq: hq + 256], True, False)
                    stw = mm(P, reg, pp_b[:, kc * 128:(kc + 1) * 128],
                             qstp_b[:, hq: hq + 256], False, True, inc=1)
                    P.wait("scalar", "tensor", stw)
                    if kc >= 2:
                        P.wait("scalar", "tensor", pvw[kc - 2])
                    aexp.append(P.emit("scalar", lambda s, reg=reg, kc=kc:
                                       s.activation(out=pt[kc % 2], in_=reg,
                                                    func=AF.Exp), inc=1))
                    if kc >= 1:
                        P.wait("tensor", "scalar", aexp[kc - 1])
                        if kc == 1:
                            P.wait("tensor", "vector", vte[3])
                        pvw.append(mm(P, ppv[0:65, 0:256],
                                      vt[:, (kc - 1) * 65:kc * 65],
                                      pt[(kc - 1) % 2], kc == 1, False, inc=1))
                P.wait("tensor", "scalar", aexp[15])
                pvw.append(mm(P, ppv[0:65, 0:256], vt[:, 15 * 65:16 * 65],
                              pt[15 % 2], False, True, inc=1))
                pv_last = pvw[-1]
                s_pst = aexp[15]
                P.wait("vector", "tensor", pv_last)
                vrc = P.emit("vector", lambda v: v.reciprocal(
                    out=rc_row, in_=ppv[64:65, 0:256]), inc=1)
                P.wait("tensor", "vector", vrc)
                bc = mm(P, pbig[0:64, 0:256], onef[0:1, 0:64], rc_row,
                        True, True, inc=1)
                P.wait("scalar", "tensor", pv_last)
                P.wait("scalar", "vector", vrc)
                aev = P.emit("scalar", lambda s: s.copy(
                    out=ot_s, in_=ppv[0:64, 0:256]), inc=1)
                s_ppv = aev
                P.wait("vector", "tensor", bc)
                P.wait("vector", "scalar", aev)
                hs2 = 64 * hh
                vo = P.emit("vector", lambda v, fc=fc, hs2=hs2: v.tensor_tensor(
                    out=ofm[hs2:hs2 + 64, fc * JPC: fc * JPC + 256],
                    in0=ot_s, in1=pbig[0:64, 0:256], op=ALU.mult), inc=1)
                v_pbig = vo
        P.wait("tensor", "vector", P.cnt["vector"])
        P.wait("tensor", "scalar", P.cnt["scalar"])
        vt2l = []
        for fo in range(8):
            reg = pst0[:, 0:256]
            for dc in range(8):
                mm(P, reg, wo_s[:, dc * D + fo * 128: dc * D + fo * 128 + 128],
                   ofm[:, dc * JPC: dc * JPC + 256], dc == 0, False)
            mw = mm(P, reg, bo_s[0:1, fo * 128:(fo + 1) * 128], onef[0:1, 0:256],
                    False, True, inc=1)
            P.wait("scalar", "tensor", mw)
            aev = P.emit("scalar", lambda s, reg=reg: s.copy(out=sf_s, in_=reg),
                         inc=1)
            P.wait("tensor", "scalar", aev)
            for ts_ in range(2):
                regt = pbig[:, (ts_ % 2) * 512:(ts_ % 2) * 512 + 128]
                if len(vt2l) >= 2:
                    P.wait("tensor", "vector", vt2l[-2])
                tw = transp(P, regt, sf_s[:, ts_ * 128:(ts_ + 1) * 128], idf)
                P.wait("vector", "tensor", tw)
                vt2l.append(P.emit("vector",
                                   lambda v, regt=regt, ts_=ts_, fo=fo: v.tensor_copy(
                                       out=stma[:, ts_ * D + fo * 128:
                                                ts_ * D + fo * 128 + 128],
                                       in_=regt), inc=1))
            P.wait("scalar", "tensor", P.cnt["tensor"])
        P.wait("sync", "vector", P.cnt["vector"])
        for ts_ in range(2):
            dma(P, "sync", stm[b, ts_ * 128:(ts_ + 1) * 128, :],
                stma[:, ts_ * D:(ts_ + 1) * D])
        P.wait("vector", "sync", P.cnt["sync"])
        v_pbig = P.cnt["vector"]
        s_pst = P.cnt["scalar"]
    P.barrier()
    A.release()

    A.mark()
    xn_sg = A.b16(128, 8 * 1024)
    acc = [A.f32(128, 1024) for _ in range(8)]
    w1_s = [A.b16(128, 8 * FF1), A.b16(128, 8 * FF1)]
    w2_s = A.b16(128, 16 * D)
    hm_s = A.b16(128, 16 * 512)
    b1_row = A.b16(1, FF1)
    oneb0 = A.b16(1, 512)
    b2_row = A.f32(1, D)
    ce_row = A.f32(1, 1024)
    ce_s = A.f32(128, 512)
    gt_s = A.f32(128, 128)
    gtm = A.f32(128, 1024)
    xst = A.f32(128, 1024)
    sst = A.f32(128, 1024)

    for sg in range(4):
        bs = [sg * 4 + i for i in range(4)]
        P.wait("sync", "tensor", P.cnt["tensor"])
        wx = 0
        for dc in range(8):
            for i, bb_ in enumerate(bs):
                wx = dma(P, "sync", xn_sg[:, dc * 1024 + i * 256: dc * 1024 + (i + 1) * 256],
                         xnbf[bb_, dc, :, 0:JPC])
        P.wait("vector", "tensor", P.cnt["tensor"])
        P.wait("vector", "scalar", P.cnt["scalar"])
        for oc in range(8):
            P.emit("vector", lambda v, oc=oc: v.memset(acc[oc], 0.0))
        P.wait("tensor", "sync", wx)
        if sg == 0:
            P.wait("vector", "tensor", P.cnt["tensor"])
            P.emit("vector", lambda v: v.memset(oneb0, 1.0))
            P.wait("tensor", "vector", P.cnt["vector"])
            w1w = 0
            for dc in range(8):
                w1w = dma(P, "gpsimd", w1_s[0][:, dc * FF1:(dc + 1) * FF1],
                          w1_d[0, dc * 128:(dc + 1) * 128, :])
            w2w = 0
            for fcc in range(16):
                w2w = dma(P, "gpsimd", w2_s[:, fcc * D:(fcc + 1) * D],
                          w2_d[0, fcc * 128:(fcc + 1) * 128, :])
            w1ld, w2ld = w1w, w2w
        for e in range(E):
            wbuf = w1_s[e % 2]
            w1w, w2w = w1ld, w2ld
            P.wait("sync", "tensor", P.cnt["tensor"])
            P.wait("gpsimd", "tensor", P.cnt["tensor"])
            bw1 = dma(P, "gpsimd", b1_row, b1_d[e])
            bw2 = dma(P, "sync", b2_row, b2_d[e])
            for i, bb_ in enumerate(bs):
                dma(P, "sync", ce_row[0:1, i * 256:(i + 1) * 256],
                    cdr[bb_, e:e + 1, :])
            cw = P.cnt["sync"]
            for tt in range(2):
                P.wait("tensor", "sync", cw)
                bcw = mm(P, pbig[:, 0:512], onef[0:1, 0:128],
                         ce_row[0:1, tt * 512:(tt + 1) * 512], True, True, inc=1)
                P.wait("vector", "tensor", bcw)
                vce = P.emit("vector", lambda v: v.tensor_copy(
                    out=ce_s, in_=pbig[:, 0:512]), inc=1)
                if tt == 0:
                    P.wait("tensor", "gpsimd", P.cnt["gpsimd"])
                ag = []
                for fcc in range(16):
                    reg = PST[fcc % 2][:, :]
                    if fcc >= 2:
                        P.wait("tensor", "scalar", ag[fcc - 2])
                    for dc in range(8):
                        mm(P, reg,
                           wbuf[:, dc * FF1 + fcc * 128: dc * FF1 + fcc * 128 + 128],
                           xn_sg[:, dc * 1024 + tt * 512: dc * 1024 + tt * 512 + 512],
                           dc == 0, False)
                    hw = mm(P, reg, b1_row[0:1, fcc * 128:(fcc + 1) * 128],
                            oneb0[0:1, 0:512], False, True, inc=1)
                    P.wait("scalar", "tensor", hw)
                    ag.append(P.emit("scalar", lambda s, reg=reg, fcc=fcc:
                                     s.activation(
                                         out=hm_s[:, fcc * 512:(fcc + 1) * 512],
                                         in_=reg, func=AF.Gelu), inc=1))
                if tt == 1:
                    en, bn = (e + 1) % E, (e + 1) % 2
                    srcidx = e + 1 if e + 1 < E else 0
                    P.wait("gpsimd", "tensor", P.cnt["tensor"])
                    w1n = 0
                    for dc in range(8):
                        w1n = dma(P, "gpsimd", w1_s[bn][:, dc * FF1:(dc + 1) * FF1],
                                  w1_d[srcidx, dc * 128:(dc + 1) * 128, :])
                    w1ld = w1n
                P.wait("vector", "scalar", ag[-1])
                P.wait("vector", "vector", vce)
                vhm = 0
                for fcc in range(16):
                    vhm = P.emit("vector", lambda v, fcc=fcc: v.tensor_tensor(
                        out=hm_s[:, fcc * 512:(fcc + 1) * 512],
                        in0=hm_s[:, fcc * 512:(fcc + 1) * 512], in1=ce_s,
                        op=ALU.mult), inc=(1 if fcc == 15 else 0))
                P.wait("tensor", "vector", vhm)
                if tt == 0:
                    P.wait("tensor", "gpsimd", w2w)
                va = []
                for oc in range(8):
                    reg = PST[oc % 2][:, :]
                    if oc >= 2:
                        P.wait("tensor", "vector", va[oc - 2])
                    for fcc in range(16):
                        mm(P, reg, w2_s[:, fcc * D + oc * 128: fcc * D + oc * 128 + 128],
                           hm_s[:, fcc * 512:(fcc + 1) * 512], fcc == 0, False)
                    ow = mm(P, reg, b2_row[0:1, oc * 128:(oc + 1) * 128],
                            ce_row[0:1, tt * 512:(tt + 1) * 512], False, True, inc=1)
                    P.wait("vector", "tensor", ow)
                    va.append(P.emit("vector", lambda v, reg=reg, oc=oc, tt=tt:
                                     v.tensor_tensor(
                                         out=acc[oc][:, tt * 512:(tt + 1) * 512],
                                         in0=acc[oc][:, tt * 512:(tt + 1) * 512],
                                         in1=reg, op=ALU.add), inc=1))
                P.wait("tensor", "vector", va[-1])
                P.wait("scalar", "tensor", P.cnt["tensor"])
                if tt == 1:
                    srcidx = e + 1 if e + 1 < E else 0
                    P.wait("gpsimd", "tensor", P.cnt["tensor"])
                    w2n = 0
                    for fcc in range(16):
                        w2n = dma(P, "gpsimd", w2_s[:, fcc * D:(fcc + 1) * D],
                                  w2_d[srcidx, fcc * 128:(fcc + 1) * 128, :])
                    w2ld = w2n
        prev_out = 0
        for t8 in range(8):
            bb_ = sg * 4 + t8 // 2
            j0 = (t8 % 2) * 128
            P.wait("sync", "vector", P.cnt["vector"])
            wxx = dma(P, "sync", xst, x_in[bb_, j0:j0 + 128, :])
            wss = dma(P, "sync", sst, stm[bb_, j0:j0 + 128, :])
            P.wait("vector", "sync", prev_out)
            P.wait("tensor", "vector", P.cnt["vector"])
            vgt = []
            for oc in range(8):
                asil = P.emit("scalar", lambda s, oc=oc, t8=t8: s.activation(
                    out=gt_s, in_=acc[oc][:, t8 * 128:(t8 + 1) * 128],
                    func=AF.Silu), inc=1)
                P.wait("tensor", "scalar", asil)
                regt = pbig[:, (oc % 2) * 512:(oc % 2) * 512 + 128]
                if oc >= 2:
                    P.wait("tensor", "vector", vgt[oc - 2])
                tw = transp(P, regt, gt_s, idf)
                P.wait("scalar", "tensor", tw)
                P.wait("vector", "tensor", tw)
                vgt.append(P.emit("vector", lambda v, regt=regt, oc=oc:
                                  v.tensor_copy(
                                      out=gtm[:, oc * 128:(oc + 1) * 128],
                                      in_=regt), inc=1))
            P.wait("vector", "sync", wxx)
            P.wait("vector", "sync", wss)
            P.wait("vector", "vector", vgt[-1])
            vmu = P.emit("vector", lambda v: v.tensor_tensor(
                out=gtm, in0=gtm, in1=sst, op=ALU.mult), inc=1)
            P.wait("vector", "vector", vmu)
            vf = P.emit("vector", lambda v: v.tensor_tensor(
                out=gtm, in0=gtm, in1=xst, op=ALU.add), inc=1)
            P.wait("sync", "vector", vf)
            prev_out = dma(P, "sync", x1loc[bb_, j0:j0 + 128, :], gtm)
    A.release()
    A.release()
    return dict(idf_d=idf_d, onef_d=onef_d)


def build_l0(debug=False):
    env = _mk_env()
    P, nc, ctx = env["P"], env["nc"], env["ctx"]
    emit_l0(env, debug=debug)
    fin = P.cnt["sync"]
    P.wait("sync", "sync", fin)
    P.wait("gpsimd", "sync", fin)
    block = ctx.enter_context(nc.Block())
    P.finalize(block)
    ctx.close()
    return nc


NTOK = B * JPC


def emit_l1(env, debug=False, xs_src=None, shared_d=None):
    nc, ctx, A, P = env["nc"], env["ctx"], env["A"], env["P"]
    pbig, pst0, pst1, ppv = env["pbig"], env["pst0"], env["pst1"], env["ppv"]
    di = lambda n, s, d=F32: nc.dram_tensor(n, s, d, kind="ExternalInput").ap()

    xs_in = di("xs", [B, JPC, D]) if xs_src is None else xs_src
    ln1g_r = di("ln1g", [1, D])
    ln1b_r = di("ln1b", [1, D])
    ln2g_f = di("ln2gf", [128, 8])
    ln2b_f = di("ln2bf", [128, 8])
    tmk_f = di("tmkf", [128, 8])
    tmv_f = di("tmvf", [128, 8])
    tmr_f = di("tmrf", [128, 8])
    td_f = di("tdf", [128, 8])
    tf_f = di("tff", [128, 8])
    cmk_f = di("cmkf", [128, 8])
    cmr_f = di("cmrf", [128, 8])
    wkt = di("wkt1", [D, D])
    wvt = di("wvt1", [D, D])
    wrt = di("wrt1", [D, D])
    wot = di("wot1", [D, D])
    wckt = di("wckt", [D, FF2])
    wcvt = di("wcvt", [FF2, D])
    wcrt = di("wcrt", [D, D])
    w1_d = di("w1f", [D, FF2])
    w2_d = di("w2f", [FF2, D])
    b1_r = di("b1f", [1, FF2])
    b2_r = di("b2f", [1, D])
    qph_f = di("qphf", [128, 32])
    qwr_f = di("qwrf", [128, 32])
    qwi_f = di("qwif", [128, 32])
    if shared_d is None:
        idf_d = di("idenf", [128, 128])
        onef_d = di("onesf", [1, 512])
    else:
        idf_d = shared_d["idf_d"]
        onef_d = shared_d["onef_d"]
    onec_d = di("onescol", [128, 1])

    out_d = nc.dram_tensor("out", [B, JPC, D], F32, kind="ExternalOutput").ap()

    def scr(name, shape, dtyp=F32):
        kind = "ExternalOutput" if debug else "Internal"
        return nc.dram_tensor(name, shape, dtyp, kind=kind).ap()

    xn1f = scr("xn1f", [8, 128, NTOK])
    xn1b = scr("xn1b", [8, 128, NTOK], BF16)
    xxf = scr("xxf", [8, 128, NTOK])
    kf = scr("kf", [8, 128, NTOK])
    vf = scr("vf", [8, 128, NTOK])
    rf = scr("rf", [8, 128, NTOK])
    rwb = scr("rwb", [8, 128, NTOK], BF16)
    x2f = scr("x2f", [8, 128, NTOK])
    yyf = scr("yyf", [8, 128, NTOK])
    sqb = scr("sqb", [32, 128, NTOK], BF16)
    s1f = scr("s1f", [8, 128, NTOK])
    PST = (pst0, pst1)

    A.mark()
    idf = A.f32(128, 128)
    onef = A.f32(1, 512)
    onec = A.f32(128, 1)
    eps_c = A.f32(128, 1)
    ln1g = A.f32(1, D)
    ln1b = A.f32(1, D)
    ln2g = A.f32(128, 8)
    ln2b = A.f32(128, 8)
    tmk = A.f32(128, 8)
    tmv = A.f32(128, 8)
    tmr = A.f32(128, 8)
    tfu = A.f32(128, 8)
    wneg = A.f32(128, 8)
    cmk = A.f32(128, 8)
    cmr = A.f32(128, 8)
    g1bc = A.f32(128, D)
    b1bc = A.f32(128, D)
    tdtmp = A.f32(128, 8)

    for (dst, srcd) in ((idf, idf_d), (onef, onef_d), (onec, onec_d),
                        (ln1g, ln1g_r), (ln1b, ln1b_r), (ln2g, ln2g_f),
                        (ln2b, ln2b_f), (tmk, tmk_f), (tmv, tmv_f),
                        (tmr, tmr_f), (tfu, tf_f), (tdtmp, td_f),
                        (cmk, cmk_f), (cmr, cmr_f)):
        dma(P, "sync", dst, srcd)
    w_pro = P.cnt["sync"]
    P.wait("vector", "sync", w_pro)
    P.emit("vector", lambda v: v.memset(eps_c, EPS))
    P.wait("scalar", "sync", w_pro)
    aexp = P.emit("scalar", lambda s: s.activation(
        out=wneg, in_=tdtmp, func=AF.Exp), inc=1)
    P.wait("vector", "scalar", aexp)
    P.emit("vector", lambda v: v.tensor_scalar(
        out=wneg, in0=wneg, scalar1=-1.0, scalar2=None, op0=ALU.mult))
    P.wait("tensor", "sync", w_pro)
    for (row, bc, bank) in ((ln1g, g1bc, 0), (ln1b, b1bc, 1)):
        for hlf in range(2):
            P.wait("tensor", "vector", P.cnt["vector"])
            w = mm(P, pbig[:, bank * 512: bank * 512 + 512],
                   onef[0:1, 0:128], row[0:1, hlf * 512:(hlf + 1) * 512],
                   True, True, inc=1)
            P.wait("vector", "tensor", w)
            P.emit("vector", lambda v, bc=bc, hlf=hlf, bank=bank: v.tensor_copy(
                out=bc[:, hlf * 512:(hlf + 1) * 512],
                in_=pbig[:, bank * 512: bank * 512 + 512]))
    P.barrier()

    NT2 = NTOK // 128

    A.mark()
    xt_b = [A.f32(128, D) for _ in range(4)]
    xn_b = [A.f32(128, D) for _ in range(4)]
    xx_b = [A.f32(128, D) for _ in range(4)]
    xsq = A.f32(128, D)
    stat = A.f32(128, 16)
    stat2 = A.f32(128, 16)
    sd4 = A.f32(128, 4)
    rstd4 = A.f32(128, 4)
    msq4 = A.f32(128, 4)
    asm_f = A.f32(128, 8 * 512)
    asm_b = A.b16(128, 8 * 512)
    asm_x = A.f32(128, 8 * 512)

    def tm_stats(src_tiles, scol, n=4):
        for i in range(n):
            P.emit("vector", lambda v, i=i: v.reduce_sum(
                out=stat[:, scol + i:scol + i + 1], in_=src_tiles[i], axis=AX.X))
        vsq = []
        for i in range(n):
            if i >= 1:
                P.wait("scalar", "vector", vsq[i - 1])
            sqw = P.emit("scalar", lambda s, i=i: s.activation(
                out=xsq, in_=src_tiles[i], func=AF.Square), inc=1)
            P.wait("vector", "scalar", sqw)
            vsq.append(P.emit("vector", lambda v, i=i: v.reduce_sum(
                out=stat[:, scol + 4 + i:scol + 5 + i], in_=xsq, axis=AX.X), inc=1))
        vm = 0
        for i in range(n):
            vm = P.emit("vector", lambda v, i=i: v.tensor_scalar(
                out=stat[:, scol + 8 + i:scol + 9 + i],
                in0=stat[:, scol + i:scol + i + 1], scalar1=INV_D,
                scalar2=None, op0=ALU.mult), inc=1)
        P.wait("scalar", "vector", vm)
        am = P.emit("scalar", lambda s: s.activation(
            out=msq4, in_=stat[:, scol + 8:scol + 12], func=AF.Square), inc=1)
        P.wait("vector", "scalar", am)
        vnv = 0
        for i in range(n):
            vnv = P.emit("vector", lambda v, i=i: v.scalar_tensor_tensor(
                out=stat[:, scol + 12 + i:scol + 13 + i],
                in0=stat[:, scol + 4 + i:scol + 5 + i], scalar=INV_D,
                in1=msq4[:, i:i + 1], op0=ALU.mult, op1=ALU.subtract), inc=1)
        P.wait("scalar", "vector", vnv)
        asd = P.emit("scalar", lambda s: s.activation(
            out=sd4, in_=stat[:, scol + 12:scol + 16], func=AF.Sqrt,
            bias=eps_c), inc=1)
        P.wait("vector", "scalar", asd)
        P.emit("vector", lambda v: v.reciprocal(out=rstd4, in_=sd4))

    prev_rd = 0
    prev_asm = 0
    for g in range(NT2 // 4):
        tis = [g * 4 + i for i in range(4)]
        P.wait("sync", "vector", P.cnt["vector"])
        P.wait("sync", "scalar", P.cnt["scalar"])
        w_x = 0
        for i, ti in enumerate(tis):
            bb_, jj = ti // 2, (ti % 2) * 128
            w_x = dma(P, "sync", xt_b[i], xs_in[bb_, jj:jj + 128, :])
        P.wait("vector", "sync", w_x)
        P.wait("scalar", "sync", w_x)
        tm_stats(xt_b, 0)
        for i in range(4):
            P.emit("vector", lambda v, i=i: v.tensor_scalar(
                out=xn_b[i], in0=xt_b[i], scalar1=stat[:, 8 + i:9 + i],
                scalar2=None, op0=ALU.subtract))
        for i in range(4):
            P.emit("vector", lambda v, i=i: v.tensor_scalar(
                out=xn_b[i], in0=xn_b[i], scalar1=rstd4[:, i:i + 1],
                scalar2=None, op0=ALU.mult))
        P.wait("scalar", "vector", P.cnt["vector"])
        tm_stats(xn_b, 0)
        for i in range(4):
            P.emit("vector", lambda v, i=i: v.tensor_scalar(
                out=xx_b[i], in0=xn_b[i], scalar1=stat[:, 8 + i:9 + i],
                scalar2=None, op0=ALU.subtract))
        for i in range(4):
            P.emit("vector", lambda v, i=i: v.tensor_scalar(
                out=xx_b[i], in0=xx_b[i], scalar1=rstd4[:, i:i + 1],
                scalar2=None, op0=ALU.mult))
        for i in range(4):
            P.emit("vector", lambda v, i=i: v.tensor_tensor(
                out=xx_b[i], in0=xx_b[i], in1=g1bc, op=ALU.mult))
        for i in range(4):
            P.emit("vector", lambda v, i=i: v.tensor_tensor(
                out=xx_b[i], in0=xx_b[i], in1=b1bc, op=ALU.add))
        vready = P.cnt["vector"]
        P.wait("tensor", "vector", vready)
        P.wait("vector", "sync", prev_asm)
        P.wait("scalar", "sync", prev_asm)
        evs = []
        for i in range(4):
            for dc in range(8):
                reg = pbig[:, (dc % 2) * 512:(dc % 2) * 512 + 128]
                reg2 = pbig[:, 1024 + (dc % 2) * 512:1024 + (dc % 2) * 512 + 128]
                k = i * 8 + dc
                if k >= 2:
                    P.wait("tensor", "scalar", evs[k - 2])
                tw = transp(P, reg, xn_b[i][:, dc * 128:(dc + 1) * 128], idf)
                tw2 = transp(P, reg2, xx_b[i][:, dc * 128:(dc + 1) * 128], idf)
                P.wait("scalar", "tensor", tw2)
                sf = P.emit("scalar", lambda s, reg=reg, dc=dc, i=i: s.copy(
                    out=asm_f[:, dc * 512 + i * 128: dc * 512 + i * 128 + 128],
                    in_=reg), inc=1)
                sx = P.emit("scalar", lambda s, reg2=reg2, dc=dc, i=i: s.copy(
                    out=asm_x[:, dc * 512 + i * 128: dc * 512 + i * 128 + 128],
                    in_=reg2), inc=1)
                evs.append(sx)
                P.wait("vector", "scalar", sf)
                P.emit("vector", lambda v, dc=dc, i=i: v.tensor_copy(
                    out=asm_b[:, dc * 512 + i * 128: dc * 512 + i * 128 + 128],
                    in_=asm_f[:, dc * 512 + i * 128: dc * 512 + i * 128 + 128]))
        P.wait("sync", "scalar", P.cnt["scalar"])
        P.wait("sync", "vector", P.cnt["vector"])
        for dc in range(8):
            dma(P, "sync", xn1f[dc, :, g * 512:(g + 1) * 512],
                asm_f[:, dc * 512:(dc + 1) * 512])
            dma(P, "sync", xn1b[dc, :, g * 512:(g + 1) * 512],
                asm_b[:, dc * 512:(dc + 1) * 512])
            prev_asm = dma(P, "sync", xxf[dc, :, g * 512:(g + 1) * 512],
                           asm_x[:, dc * 512:(dc + 1) * 512])
    P.barrier()
    A.release()

    A.mark()
    wk_s = A.b16(128, 8 * D)
    wv_s = A.b16(128, 8 * D)
    wr_s = A.b16(128, 8 * D)
    km_s = A.b16(128, 8 * 512)
    vm_s = A.b16(128, 8 * 512)
    rm_s = A.b16(128, 8 * 512)
    xwin = [A.f32(128, 768) for _ in range(2)]
    kev = A.f32(128, 512)
    vev = A.f32(128, 512)
    rev = A.f32(128, 512)

    wW = 0
    for dc in range(8):
        for (wsrc, wdst) in ((wkt, wk_s), (wvt, wv_s), (wrt, wr_s)):
            wW = dma(P, "gpsimd", wdst[:, dc * D:(dc + 1) * D],
                     wsrc[dc * 128:(dc + 1) * 128, :])
    for TT in range(8):
        t0 = TT * 512
        lo = max(0, t0 - 256)
        for ic in range(8):
            P.wait("sync", "vector", P.cnt["vector"])
            wx = P.dma_ch("sync", "ld%d" % (ic % 2), xwin[ic % 2][:, 0:t0 + 512 - lo],
                          xxf[ic, :, lo:t0 + 512])
            P.wait_ch("vector", "ld%d" % (ic % 2), wx)
            cur = xwin[ic % 2][:, 256:768] if TT > 0 else xwin[ic % 2][:, 0:512]
            for (mix, msb, mvec) in ((0, km_s, tmk), (1, vm_s, tmv), (2, rm_s, tmr)):
                dsl = msb[:, ic * 512:(ic + 1) * 512]
                if TT == 0:
                    P.emit("vector", lambda v, dsl=dsl, cur=cur, mvec=mvec, ic=ic:
                           v.tensor_scalar(out=dsl[:, 0:256], in0=cur[:, 0:256],
                                           scalar1=mvec[:, ic:ic + 1], scalar2=None,
                                           op0=ALU.mult))
                    sh = xwin[ic % 2][:, 0:256]
                    cu2 = xwin[ic % 2][:, 256:512]
                    P.emit("vector", lambda v, cu2=cu2, sh=sh: v.tensor_tensor(
                        out=kev[:, 0:256], in0=cu2, in1=sh, op=ALU.subtract))
                    P.emit("vector", lambda v, dsl=dsl, sh=sh, mvec=mvec, ic=ic:
                           v.scalar_tensor_tensor(
                               out=dsl[:, 256:512], in0=kev[:, 0:256],
                               scalar=mvec[:, ic:ic + 1], in1=sh,
                               op0=ALU.mult, op1=ALU.add))
                else:
                    sh = xwin[ic % 2][:, 0:512]
                    P.emit("vector", lambda v, cur=cur, sh=sh: v.tensor_tensor(
                        out=kev, in0=cur, in1=sh, op=ALU.subtract))
                    P.emit("vector", lambda v, dsl=dsl, sh=sh, mvec=mvec, ic=ic:
                           v.scalar_tensor_tensor(
                               out=dsl, in0=kev, scalar=mvec[:, ic:ic + 1],
                               in1=sh, op0=ALU.mult, op1=ALU.add))
        vmix = P.cnt["vector"]
        P.wait("tensor", "vector", vmix)
        if TT == 0:
            P.wait("tensor", "gpsimd", wW)
        for oc in range(8):
            for (wsb, msb, psm) in ((wk_s, km_s, pst0), (wv_s, vm_s, pst1),
                                    (wr_s, rm_s, ppv)):
                for ic in range(8):
                    mm(P, psm[:, :], wsb[:, ic * D + oc * 128: ic * D + oc * 128 + 128],
                       msb[:, ic * 512:(ic + 1) * 512], ic == 0, ic == 7,
                       inc=(1 if ic == 7 else 0))
            kw_ = P.cnt["tensor"]
            P.wait("scalar", "tensor", kw_)
            ke = P.emit("scalar", lambda s: s.copy(out=kev, in_=pst0[:, :]), inc=1)
            re_ = P.emit("scalar", lambda s: s.copy(out=rev, in_=ppv[:, :]), inc=1)
            P.wait("vector", "tensor", kw_)
            ve = P.emit("vector", lambda v: v.tensor_copy(out=vev, in_=pst1[:, :]),
                        inc=1)
            P.wait("sync", "scalar", re_)
            P.wait("sync", "vector", ve)
            dma(P, "sync", kf[oc, :, t0:t0 + 512], kev)
            dma(P, "sync", vf[oc, :, t0:t0 + 512], vev)
            w_st = dma(P, "sync", rf[oc, :, t0:t0 + 512], rev)
            P.wait("tensor", "sync", w_st)
    P.barrier()
    A.release()

    A.mark()
    aa = A.f32(128, 2048)
    bb = A.f32(128, 2048)
    pp = A.f32(128, 2048)
    kb_s = A.f32(128, 2048)
    vb_s = A.f32(128, 2048)
    rb_s = A.f32(128, 2048)
    t_a = A.f32(128, 2048)
    t_b = A.f32(128, 2048)
    t_c = A.f32(128, 2048)
    t_d = A.f32(128, 2048)
    t_e = A.f32(128, 2048)
    t_f = A.f32(128, 2048)
    t_g = A.f32(128, 2048)
    rw_o = A.b16(128, 2048)

    P.emit("vector", lambda v: v.memset(aa, 0.0))
    P.emit("vector", lambda v: v.memset(bb, 0.0))
    P.emit("vector", lambda v: v.memset(pp, -1e38))
    ppc, aac, bbc = pp, aa, bb
    for step in range(B):
        c0 = step * JPC
        P.wait("sync", "vector", P.cnt["vector"])
        P.wait("sync", "scalar", P.cnt["scalar"])
        wl = 0
        for dc in range(8):
            dma(P, "sync", kb_s[:, dc * 256:(dc + 1) * 256], kf[dc, :, c0:c0 + JPC])
            dma(P, "sync", vb_s[:, dc * 256:(dc + 1) * 256], vf[dc, :, c0:c0 + JPC])
            wl = dma(P, "sync", rb_s[:, dc * 256:(dc + 1) * 256],
                     rf[dc, :, c0:c0 + JPC])
        P.wait("vector", "sync", wl)
        P.wait("scalar", "sync", wl)
        for dc in range(8):
            P.emit("vector", lambda v, dc=dc: v.tensor_scalar(
                out=t_a[:, dc * 256:(dc + 1) * 256],
                in0=kb_s[:, dc * 256:(dc + 1) * 256],
                scalar1=tfu[:, dc:dc + 1], scalar2=None, op0=ALU.add))
        P.emit("vector", lambda v: v.tensor_tensor(out=t_b, in0=pp, in1=t_a,
                                                   op=ALU.max))
        P.emit("vector", lambda v: v.tensor_tensor(out=t_c, in0=pp, in1=t_b,
                                                   op=ALU.subtract))
        P.emit("vector", lambda v: v.tensor_tensor(out=t_d, in0=t_a, in1=t_b,
                                                   op=ALU.subtract))
        vv_ = P.cnt["vector"]
        P.wait("scalar", "vector", vv_)
        e1w = P.emit("scalar", lambda s: s.activation(out=t_c, in_=t_c, func=AF.Exp),
                     inc=1)
        e2w = P.emit("scalar", lambda s: s.activation(out=t_d, in_=t_d, func=AF.Exp),
                     inc=1)
        P.wait("vector", "scalar", e2w)
        P.emit("vector", lambda v: v.tensor_tensor(out=t_e, in0=t_c, in1=aa,
                                                   op=ALU.mult))
        P.emit("vector", lambda v: v.tensor_tensor(out=t_f, in0=t_d, in1=vb_s,
                                                   op=ALU.mult))
        P.emit("vector", lambda v: v.tensor_tensor(out=t_e, in0=t_e, in1=t_f,
                                                   op=ALU.add))
        P.emit("vector", lambda v: v.tensor_tensor(out=t_f, in0=t_c, in1=bb,
                                                   op=ALU.mult))
        P.emit("vector", lambda v: v.tensor_tensor(out=t_f, in0=t_f, in1=t_d,
                                                   op=ALU.add))
        P.emit("vector", lambda v: v.reciprocal(out=t_f, in_=t_f))
        P.emit("vector", lambda v: v.tensor_tensor(out=t_e, in0=t_e, in1=t_f,
                                                   op=ALU.mult))
        vv_ = P.cnt["vector"]
        P.wait("scalar", "vector", vv_)
        rsg = P.emit("scalar", lambda s: s.activation(out=t_f, in_=rb_s,
                                                      func=AF.Sigmoid), inc=1)
        P.wait("vector", "scalar", rsg)
        vrw = P.emit("vector", lambda v: v.tensor_tensor(out=rw_o, in0=t_f, in1=t_e,
                                                         op=ALU.mult), inc=1)
        P.wait("sync", "vector", vrw)
        for dc in range(8):
            dma(P, "sync", rwb[dc, :, c0:c0 + JPC], rw_o[:, dc * 256:(dc + 1) * 256])
        for dc in range(8):
            P.emit("vector", lambda v, dc=dc: v.tensor_scalar(
                out=t_a[:, dc * 256:(dc + 1) * 256],
                in0=pp[:, dc * 256:(dc + 1) * 256],
                scalar1=wneg[:, dc:dc + 1], scalar2=None, op0=ALU.add))
        P.emit("vector", lambda v: v.tensor_tensor(out=t_b, in0=t_a, in1=kb_s,
                                                   op=ALU.max))
        P.emit("vector", lambda v: v.tensor_tensor(out=t_c, in0=t_a, in1=t_b,
                                                   op=ALU.subtract))
        P.emit("vector", lambda v: v.tensor_tensor(out=t_d, in0=kb_s, in1=t_b,
                                                   op=ALU.subtract))
        vv_ = P.cnt["vector"]
        P.wait("scalar", "vector", vv_)
        e1w = P.emit("scalar", lambda s: s.activation(out=t_c, in_=t_c, func=AF.Exp),
                     inc=1)
        e2w = P.emit("scalar", lambda s: s.activation(out=t_d, in_=t_d, func=AF.Exp),
                     inc=1)
        P.wait("vector", "scalar", e2w)
        P.emit("vector", lambda v: v.tensor_tensor(out=t_e, in0=t_c, in1=aa,
                                                   op=ALU.mult))
        P.emit("vector", lambda v: v.tensor_tensor(out=t_g, in0=t_d, in1=vb_s,
                                                   op=ALU.mult))
        P.emit("vector", lambda v: v.tensor_tensor(out=aa, in0=t_e, in1=t_g,
                                                   op=ALU.add))
        P.emit("vector", lambda v: v.tensor_tensor(out=t_e, in0=t_c, in1=bb,
                                                   op=ALU.mult))
        P.emit("vector", lambda v: v.tensor_tensor(out=bb, in0=t_e, in1=t_d,
                                                   op=ALU.add))
        P.emit("vector", lambda v: v.tensor_copy(out=pp, in_=t_b))
    P.barrier()
    A.release()

    A.mark()
    wo_s = A.b16(128, 8 * D)
    rwt = A.b16(128, 8 * 512)
    xn1t = A.f32(128, 512)
    x2t = A.f32(128, 8 * 512)
    sqt = A.f32(128, 512)
    mrow = A.f32(1, 512)
    e2row = A.f32(1, 512)
    nvrow = A.f32(1, 512)
    sdrow = A.f32(1, 512)
    rsrow = A.f32(1, 512)
    yyt = A.f32(128, 512)
    u1t = A.f32(128, 512)

    wW = 0
    for dc in range(8):
        wW = dma(P, "gpsimd", wo_s[:, dc * D:(dc + 1) * D],
                 wot[dc * 128:(dc + 1) * 128, :])
    for TT in range(8):
        t0 = TT * 512
        P.wait("sync", "tensor", P.cnt["tensor"])
        wl = 0
        for dc in range(8):
            wl = dma(P, "sync", rwt[:, dc * 512:(dc + 1) * 512],
                     rwb[dc, :, t0:t0 + 512])
        P.wait("tensor", "sync", wl)
        if TT == 0:
            P.wait("tensor", "gpsimd", wW)
        for oc in range(8):
            psm = PST[oc % 2]
            for ic in range(8):
                mm(P, psm[:, :], wo_s[:, ic * D + oc * 128: ic * D + oc * 128 + 128],
                   rwt[:, ic * 512:(ic + 1) * 512], ic == 0, ic == 7,
                   inc=(1 if ic == 7 else 0))
            ow = P.cnt["tensor"]
            P.wait("sync", "vector", P.cnt["vector"])
            wx = P.dma_ch("sync", "ld0", xn1t, xn1f[oc, :, t0:t0 + 512])
            P.wait_ch("vector", "ld0", wx)
            P.wait("vector", "tensor", ow)
            vx2 = P.emit("vector", lambda v, oc=oc: v.tensor_tensor(
                out=x2t[:, oc * 512:(oc + 1) * 512], in0=xn1t, in1=PST[oc % 2][:, :],
                op=ALU.add), inc=1)
            P.wait("scalar", "vector", vx2)
            sq_w = P.emit("scalar", lambda s, oc=oc: s.activation(
                out=sqt, in_=x2t[:, oc * 512:(oc + 1) * 512], func=AF.Square), inc=1)
            P.wait("tensor", "vector", vx2)
            P.wait("tensor", "scalar", sq_w)
            mm(P, pbig[0:1, 0:512], onec, x2t[:, oc * 512:(oc + 1) * 512],
               oc == 0, oc == 7)
            mm(P, pbig[0:1, 512:1024], onec, sqt, oc == 0, oc == 7,
               inc=(1 if oc == 7 else 0))
        st_w = P.cnt["tensor"]
        P.wait("sync", "vector", P.cnt["vector"])
        for oc in range(8):
            dma(P, "sync", x2f[oc, :, t0:t0 + 512], x2t[:, oc * 512:(oc + 1) * 512])
        P.wait("vector", "tensor", st_w)
        P.emit("vector", lambda v: v.tensor_scalar(
            out=mrow, in0=pbig[0:1, 0:512], scalar1=INV_D, scalar2=None,
            op0=ALU.mult))
        P.emit("vector", lambda v: v.tensor_scalar(
            out=e2row, in0=pbig[0:1, 512:1024], scalar1=INV_D, scalar2=None,
            op0=ALU.mult))
        vst = P.cnt["vector"]
        P.wait("scalar", "vector", vst)
        am = P.emit("scalar", lambda s: s.activation(
            out=nvrow, in_=mrow, func=AF.Square), inc=1)
        P.wait("vector", "scalar", am)
        P.emit("vector", lambda v: v.tensor_tensor(
            out=nvrow, in0=e2row, in1=nvrow, op=ALU.subtract))
        vnv = P.cnt["vector"]
        P.wait("scalar", "vector", vnv)
        asd = P.emit("scalar", lambda s: s.activation(
            out=sdrow, in_=nvrow, func=AF.Sqrt, bias=eps_c[0:1, 0:1]), inc=1)
        P.wait("vector", "scalar", asd)
        P.emit("vector", lambda v: v.reciprocal(out=rsrow, in_=sdrow))
        vrs = P.cnt["vector"]
        P.wait("tensor", "vector", vrs)
        bm = mm(P, pbig[:, 1024:1536], onef[0:1, 0:128], mrow, True, True, inc=1)
        br = mm(P, pbig[:, 1536:2048], onef[0:1, 0:128], rsrow, True, True, inc=1)
        P.wait("vector", "tensor", br)
        for oc in range(8):
            P.emit("vector", lambda v, oc=oc: v.tensor_tensor(
                out=u1t, in0=x2t[:, oc * 512:(oc + 1) * 512],
                in1=pbig[:, 1024:1536], op=ALU.subtract))
            P.emit("vector", lambda v: v.tensor_tensor(
                out=u1t, in0=u1t, in1=pbig[:, 1536:2048], op=ALU.mult))
            vy = P.emit("vector", lambda v, oc=oc: v.tensor_scalar(
                out=yyt, in0=u1t, scalar1=ln2g[:, oc:oc + 1],
                scalar2=ln2b[:, oc:oc + 1], op0=ALU.mult, op1=ALU.add), inc=1)
            P.wait("sync", "vector", vy)
            dma(P, "sync", yyf[oc, :, t0:t0 + 512], yyt)
            P.wait("vector", "sync", P.cnt["sync"])
        P.wait("tensor", "sync", P.cnt["sync"])
    P.barrier()
    A.release()

    A.mark()
    wck_s = A.b16(128, 8 * FF2)
    ym_s = A.b16(128, 8 * 512)
    ywin = [A.f32(128, 768) for _ in range(2)]
    dtmp = A.f32(128, 512)
    rk_t = A.f32(128, 512)
    sq_b16 = A.b16(128, 512)

    wW = 0
    for dc in range(8):
        wW = dma(P, "gpsimd", wck_s[:, dc * FF2:(dc + 1) * FF2],
                 wckt[dc * 128:(dc + 1) * 128, :])
    for TT in range(8):
        t0 = TT * 512
        lo = max(0, t0 - 256)
        for ic in range(8):
            P.wait("sync", "vector", P.cnt["vector"])
            wx = P.dma_ch("sync", "ld%d" % (ic % 2), ywin[ic % 2][:, 0:t0 + 512 - lo],
                          yyf[ic, :, lo:t0 + 512])
            P.wait_ch("vector", "ld%d" % (ic % 2), wx)
            cur = ywin[ic % 2][:, 256:768] if TT > 0 else ywin[ic % 2][:, 0:512]
            dsl = ym_s[:, ic * 512:(ic + 1) * 512]
            if TT == 0:
                P.emit("vector", lambda v, dsl=dsl, cur=cur, ic=ic: v.tensor_scalar(
                    out=dsl[:, 0:256], in0=cur[:, 0:256], scalar1=cmk[:, ic:ic + 1],
                    scalar2=None, op0=ALU.mult))
                sh = ywin[ic % 2][:, 0:256]
                cu2 = ywin[ic % 2][:, 256:512]
                P.emit("vector", lambda v, cu2=cu2, sh=sh: v.tensor_tensor(
                    out=dtmp[:, 0:256], in0=cu2, in1=sh, op=ALU.subtract))
                P.emit("vector", lambda v, dsl=dsl, sh=sh, ic=ic:
                       v.scalar_tensor_tensor(
                           out=dsl[:, 256:512], in0=dtmp[:, 0:256],
                           scalar=cmk[:, ic:ic + 1], in1=sh,
                           op0=ALU.mult, op1=ALU.add))
            else:
                sh = ywin[ic % 2][:, 0:512]
                P.emit("vector", lambda v, cur=cur, sh=sh: v.tensor_tensor(
                    out=dtmp, in0=cur, in1=sh, op=ALU.subtract))
                P.emit("vector", lambda v, dsl=dsl, sh=sh, ic=ic:
                       v.scalar_tensor_tensor(
                           out=dsl, in0=dtmp, scalar=cmk[:, ic:ic + 1],
                           in1=sh, op0=ALU.mult, op1=ALU.add))
        vmix = P.cnt["vector"]
        P.wait("tensor", "vector", vmix)
        if TT == 0:
            P.wait("tensor", "gpsimd", wW)
        for fc in range(32):
            psm = PST[fc % 2]
            for ic in range(8):
                mm(P, psm[:, :],
                   wck_s[:, ic * FF2 + fc * 128: ic * FF2 + fc * 128 + 128],
                   ym_s[:, ic * 512:(ic + 1) * 512], ic == 0, ic == 7,
                   inc=(1 if ic == 7 else 0))
            kw_ = P.cnt["tensor"]
            P.wait("vector", "tensor", kw_)
            P.wait("vector", "scalar", P.cnt["scalar"])
            vrk = P.emit("vector", lambda v, fc=fc: v.tensor_scalar(
                out=rk_t, in0=PST[fc % 2][:, :], scalar1=0.0, scalar2=None,
                op0=ALU.max), inc=1)
            P.wait("scalar", "vector", vrk)
            asq = P.emit("scalar", lambda s: s.activation(
                out=sq_b16, in_=rk_t, func=AF.Square), inc=1)
            P.wait("sync", "scalar", asq)
            w_st = dma(P, "sync", sqb[fc, :, t0:t0 + 512], sq_b16)
            P.wait("scalar", "sync", w_st)
            P.wait("tensor", "vector", vrk)
    P.barrier()
    A.release()

    A.mark()
    wcv_s = A.b16(128, 32 * D)
    wcr_s = A.b16(128, 8 * D)
    rm2_s = A.b16(128, 8 * 512)
    sq_s = A.b16(128, 32 * 512)
    ywin2 = [A.f32(128, 768) for _ in range(2)]
    dtmp2 = A.f32(128, 512)
    rs_t = A.f32(128, 512)
    s1_t = A.f32(128, 512)
    x2t2 = A.f32(128, 512)

    wW = 0
    for fc in range(32):
        wW = dma(P, "gpsimd", wcv_s[:, fc * D:(fc + 1) * D],
                 wcvt[fc * 128:(fc + 1) * 128, :])
    for dc in range(8):
        wW = dma(P, "gpsimd", wcr_s[:, dc * D:(dc + 1) * D],
                 wcrt[dc * 128:(dc + 1) * 128, :])
    for TT in range(8):
        t0 = TT * 512
        lo = max(0, t0 - 256)
        for ic in range(8):
            P.wait("sync", "vector", P.cnt["vector"])
            wx = P.dma_ch("sync", "ld%d" % (ic % 2), ywin2[ic % 2][:, 0:t0 + 512 - lo],
                          yyf[ic, :, lo:t0 + 512])
            P.wait_ch("vector", "ld%d" % (ic % 2), wx)
            cur = ywin2[ic % 2][:, 256:768] if TT > 0 else ywin2[ic % 2][:, 0:512]
            dsl = rm2_s[:, ic * 512:(ic + 1) * 512]
            if TT == 0:
                P.emit("vector", lambda v, dsl=dsl, cur=cur, ic=ic: v.tensor_scalar(
                    out=dsl[:, 0:256], in0=cur[:, 0:256], scalar1=cmr[:, ic:ic + 1],
                    scalar2=None, op0=ALU.mult))
                sh = ywin2[ic % 2][:, 0:256]
                cu2 = ywin2[ic % 2][:, 256:512]
                P.emit("vector", lambda v, cu2=cu2, sh=sh: v.tensor_tensor(
                    out=dtmp2[:, 0:256], in0=cu2, in1=sh, op=ALU.subtract))
                P.emit("vector", lambda v, dsl=dsl, sh=sh, ic=ic:
                       v.scalar_tensor_tensor(
                           out=dsl[:, 256:512], in0=dtmp2[:, 0:256],
                           scalar=cmr[:, ic:ic + 1], in1=sh,
                           op0=ALU.mult, op1=ALU.add))
            else:
                sh = ywin2[ic % 2][:, 0:512]
                P.emit("vector", lambda v, cur=cur, sh=sh: v.tensor_tensor(
                    out=dtmp2, in0=cur, in1=sh, op=ALU.subtract))
                P.emit("vector", lambda v, dsl=dsl, sh=sh, ic=ic:
                       v.scalar_tensor_tensor(
                           out=dsl, in0=dtmp2, scalar=cmr[:, ic:ic + 1],
                           in1=sh, op0=ALU.mult, op1=ALU.add))
        P.wait("sync", "scalar", P.cnt["scalar"])
        wq = 0
        for fc in range(32):
            wq = dma(P, "sync", sq_s[:, fc * 512:(fc + 1) * 512],
                     sqb[fc, :, t0:t0 + 512])
        vmix = P.cnt["vector"]
        P.wait("tensor", "vector", vmix)
        P.wait("tensor", "sync", wq)
        if TT == 0:
            P.wait("tensor", "gpsimd", wW)
        for oc in range(8):
            for fc in range(32):
                mm(P, pst0[:, :],
                   wcv_s[:, fc * D + oc * 128: fc * D + oc * 128 + 128],
                   sq_s[:, fc * 512:(fc + 1) * 512], fc == 0, fc == 31,
                   inc=(1 if fc == 31 else 0))
            vvw = P.cnt["tensor"]
            for ic in range(8):
                mm(P, pst1[:, :],
                   wcr_s[:, ic * D + oc * 128: ic * D + oc * 128 + 128],
                   rm2_s[:, ic * 512:(ic + 1) * 512], ic == 0, ic == 7,
                   inc=(1 if ic == 7 else 0))
            rrw = P.cnt["tensor"]
            P.wait("scalar", "tensor", rrw)
            ars = P.emit("scalar", lambda s: s.activation(
                out=rs_t, in_=pst1[:, :], func=AF.Sigmoid), inc=1)
            P.wait("sync", "vector", P.cnt["vector"])
            wx2 = P.dma_ch("sync", "ld0", x2t2, x2f[oc, :, t0:t0 + 512])
            P.wait_ch("vector", "ld0", wx2)
            P.wait("vector", "scalar", ars)
            P.wait("vector", "tensor", vvw)
            P.emit("vector", lambda v: v.tensor_tensor(
                out=s1_t, in0=rs_t, in1=pst0[:, :], op=ALU.mult))
            vs1 = P.emit("vector", lambda v: v.tensor_tensor(
                out=s1_t, in0=s1_t, in1=x2t2, op=ALU.add), inc=1)
            P.wait("sync", "vector", vs1)
            w_st = dma(P, "sync", s1f[oc, :, t0:t0 + 512], s1_t)
            P.wait("vector", "sync", w_st)
            P.wait("tensor", "vector", vs1)
    P.barrier()
    A.release()

    A.mark()
    w1_s = A.b16(128, 8 * FF2)
    w2_s = A.b16(128, 32 * D)
    b1row = A.b16(1, FF2)
    oneb = A.b16(1, 512)
    b2row = A.f32(1, D)
    thc = A.f32(128, 32)
    a2c = A.f32(128, 32)
    qtmp1 = A.f32(128, 32)
    qtmp2 = A.f32(128, 32)
    qr_s = A.b16(128, 32 * 256)
    xn1t2 = A.b16(128, 256)
    h_t = A.f32(128, 256)
    a_t = A.f32(128, 256)
    s4_t = A.f32(128, 256)
    s2_t = A.f32(128, 256)
    c2_t = A.f32(128, 256)
    g1_t = A.f32(128, 256)
    s1t2 = A.f32(128, 256)
    tr_t = A.f32(128, 256)
    oasm = A.f32(128, 2 * D)
    x1t = A.f32(128, D)
    qv_s = A.f32(128, 32 * 3)

    wW = dma(P, "gpsimd", b1row, b1_r)
    wW = dma(P, "sync", b2row, b2_r)
    P.wait("vector", "gpsimd", P.cnt["gpsimd"])
    P.emit("vector", lambda v: v.memset(oneb, 1.0))
    wW = dma(P, "sync", qv_s[:, 0:32], qph_f)
    wW = dma(P, "sync", qv_s[:, 32:64], qwr_f)
    wW = dma(P, "sync", qv_s[:, 64:96], qwi_f)
    for dc in range(8):
        dma(P, "gpsimd", w1_s[:, dc * FF2:(dc + 1) * FF2],
            w1_d[dc * 128:(dc + 1) * 128, :])
    wW2 = 0
    for fc in range(32):
        wW2 = dma(P, "gpsimd", w2_s[:, fc * D:(fc + 1) * D],
                  w2_d[fc * 128:(fc + 1) * 128, :])
    PI = float(np.pi)
    P.wait("vector", "sync", wW)
    P.emit("vector", lambda v: v.tensor_copy(out=thc, in_=qv_s[:, 0:32]))
    for _ in range(5):
        P.emit("vector", lambda v: v.tensor_single_scalar(
            out=qtmp1, in_=thc, scalar=PI, op=ALU.is_gt))
        P.emit("vector", lambda v: v.scalar_tensor_tensor(
            out=thc, in0=qtmp1, scalar=-2 * PI, in1=thc, op0=ALU.mult, op1=ALU.add))
        P.emit("vector", lambda v: v.tensor_single_scalar(
            out=qtmp1, in_=thc, scalar=-PI, op=ALU.is_lt))
        P.emit("vector", lambda v: v.scalar_tensor_tensor(
            out=thc, in0=qtmp1, scalar=2 * PI, in1=thc, op0=ALU.mult, op1=ALU.add))
    vth = P.cnt["vector"]
    P.wait("scalar", "vector", vth)
    asn = P.emit("scalar", lambda s: s.activation(out=qtmp1, in_=thc, func=AF.Sin),
                 inc=1)
    ash = P.emit("scalar", lambda s: s.activation(out=qtmp2, in_=thc, func=AF.Sin,
                                                  scale=0.5), inc=1)
    P.wait("vector", "scalar", ash)
    P.emit("vector", lambda v: v.tensor_tensor(out=qtmp2, in0=qtmp2, in1=qtmp2,
                                               op=ALU.mult))
    P.emit("vector", lambda v: v.tensor_scalar(out=qtmp2, in0=qtmp2, scalar1=-2.0,
                                               scalar2=1.0, op0=ALU.mult,
                                               op1=ALU.add))
    P.emit("vector", lambda v: v.tensor_tensor(out=qtmp2, in0=qtmp2,
                                               in1=qv_s[:, 32:64], op=ALU.mult))
    P.emit("vector", lambda v: v.tensor_tensor(out=qtmp1, in0=qtmp1,
                                               in1=qv_s[:, 64:96], op=ALU.mult))
    P.emit("vector", lambda v: v.tensor_tensor(out=a2c, in0=qtmp2, in1=qtmp1,
                                               op=ALU.add))
    P.emit("vector", lambda v: v.tensor_tensor(out=a2c, in0=a2c,
                                               in1=qv_s[:, 64:96], op=ALU.mult))
    P.emit("vector", lambda v: v.tensor_scalar(out=a2c, in0=a2c, scalar1=0.2,
                                               scalar2=None, op0=ALU.mult))

    xnt_s = A.b16(128, 8 * 256)

    for T4 in range(16):
        t0 = T4 * 256
        P.wait("sync", "tensor", P.cnt["tensor"])
        wxt = 0
        for ic in range(8):
            wxt = dma(P, "sync", xnt_s[:, ic * 256:(ic + 1) * 256],
                      xn1b[ic, :, t0:t0 + 256])
        P.wait("tensor", "sync", wxt)
        if T4 == 0:
            P.wait("tensor", "gpsimd", wW2)
            P.wait("tensor", "vector", P.cnt["vector"])
        ahel = []
        for fc in range(32):
            psm = PST[fc % 2]
            if fc >= 2:
                P.wait("tensor", "scalar", ahel[fc - 2])
            for ic in range(8):
                mm(P, psm[:, 0:256],
                   w1_s[:, ic * FF2 + fc * 128: ic * FF2 + fc * 128 + 128],
                   xnt_s[:, ic * 256:(ic + 1) * 256], ic == 0, False)
            hw = mm(P, psm[:, 0:256], b1row[0:1, fc * 128:(fc + 1) * 128],
                    oneb[0:1, 0:256], False, True, inc=1)
            P.wait("scalar", "tensor", hw)
            if fc >= 1:
                P.wait("scalar", "vector", vqr)
            ahe = P.emit("scalar", lambda s, psm=psm: s.copy(
                out=h_t, in_=psm[:, 0:256]), inc=1)
            ahel.append(ahe)
            P.wait("vector", "scalar", ahe)
            P.emit("vector", lambda v, fc=fc: v.tensor_scalar(
                out=a_t, in0=h_t, scalar1=0.1, scalar2=thc[:, fc:fc + 1],
                op0=ALU.mult, op1=ALU.add))
            va = P.cnt["vector"]
            P.wait("scalar", "vector", va)
            as2 = P.emit("scalar", lambda s: s.activation(
                out=s2_t, in_=a_t, func=AF.Sin, scale=0.5), inc=1)
            as4 = P.emit("scalar", lambda s: s.activation(
                out=s4_t, in_=a_t, func=AF.Sin, scale=0.25), inc=1)
            P.wait("vector", "scalar", as4)
            P.emit("vector", lambda v: v.tensor_tensor(
                out=c2_t, in0=s4_t, in1=s4_t, op=ALU.mult))
            P.emit("vector", lambda v: v.tensor_scalar(
                out=c2_t, in0=c2_t, scalar1=-2.0, scalar2=1.0,
                op0=ALU.mult, op1=ALU.add))
            P.emit("vector", lambda v: v.tensor_tensor(
                out=c2_t, in0=c2_t, in1=s2_t, op=ALU.mult))
            P.emit("vector", lambda v, fc=fc: v.scalar_tensor_tensor(
                out=c2_t, in0=c2_t, scalar=a2c[:, fc:fc + 1], in1=h_t,
                op0=ALU.mult, op1=ALU.add))
            vqr = P.emit("vector", lambda v, fc=fc: v.tensor_scalar(
                out=qr_s[:, fc * 256:(fc + 1) * 256], in0=c2_t, scalar1=0.0,
                scalar2=None, op0=ALU.max), inc=1)
        P.wait("tensor", "vector", P.cnt["vector"])
        asll = []
        for oc in range(8):
            psm = PST[oc % 2]
            if oc >= 2:
                P.wait("tensor", "scalar", asll[oc - 2])
            for fc in range(32):
                mm(P, psm[:, 0:256],
                   w2_s[:, fc * D + oc * 128: fc * D + oc * 128 + 128],
                   qr_s[:, fc * 256:(fc + 1) * 256], fc == 0, False)
            ow = mm(P, psm[:, 0:256], b2row[0:1, oc * 128:(oc + 1) * 128],
                    onef[0:1, 0:256], False, True, inc=1)
            P.wait("scalar", "tensor", ow)
            if oc >= 1:
                P.wait("scalar", "vector", vtr[-1])
            asl = P.emit("scalar", lambda s, psm=psm: s.activation(
                out=g1_t, in_=psm[:, 0:256], func=AF.Silu), inc=1)
            asll.append(asl)
            P.wait("sync", "vector", P.cnt["vector"])
            ws1 = P.dma_ch("sync", "ld0", s1t2, s1f[oc, :, t0:t0 + 256])
            P.wait_ch("vector", "ld0", ws1)
            P.wait("vector", "scalar", asl)
            if oc == 0:
                vtr = []
            vtr.append(P.emit("vector", lambda v: v.tensor_tensor(
                out=tr_t, in0=g1_t, in1=s1t2, op=ALU.mult), inc=1))
            P.wait("tensor", "vector", vtr[-1])
            for ts_ in range(2):
                regt = pbig[:, (ts_ % 2) * 512:(ts_ % 2) * 512 + 128]
                tw = transp(P, regt, tr_t[:, ts_ * 128:(ts_ + 1) * 128], idf)
                P.wait("vector", "tensor", tw)
                P.emit("vector", lambda v, regt=regt, ts_=ts_, oc=oc: v.tensor_copy(
                    out=oasm[:, ts_ * D + oc * 128: ts_ * D + oc * 128 + 128],
                    in_=regt), inc=1)
            P.wait("tensor", "vector", P.cnt["vector"])
        for ts_ in range(2):
            tg = t0 + ts_ * 128
            bb_, jj = tg // JPC, tg % JPC
            P.wait("sync", "vector", P.cnt["vector"])
            wx1 = P.dma_ch("sync", "ld1", x1t, xs_in[bb_, jj:jj + 128, :])
            P.wait_ch("vector", "ld1", wx1)
            vo = P.emit("vector", lambda v, ts_=ts_: v.tensor_tensor(
                out=oasm[:, ts_ * D:(ts_ + 1) * D],
                in0=oasm[:, ts_ * D:(ts_ + 1) * D], in1=x1t, op=ALU.add), inc=1)
            P.wait("sync", "vector", vo)
            dma(P, "sync", out_d[bb_, jj:jj + 128, :], oasm[:, ts_ * D:(ts_ + 1) * D])
        P.wait("vector", "sync", P.cnt["sync"])
    A.release()
    A.release()


def build_l1(debug=False):
    env = _mk_env()
    P, nc, ctx = env["P"], env["nc"], env["ctx"]
    emit_l1(env, debug=debug)
    fin = P.cnt["sync"]
    P.wait("sync", "sync", fin)
    P.wait("gpsimd", "sync", fin)
    block = ctx.enter_context(nc.Block())
    P.finalize(block)
    ctx.close()
    return nc


def build_fused(debug=False):
    env = _mk_env()
    P, nc, ctx = env["P"], env["nc"], env["ctx"]
    x1loc = nc.dram_tensor("x1loc", [B, JPC, D], F32, kind="Internal").ap()
    shared_d = emit_l0(env, debug=debug, x1loc=x1loc)
    P.barrier()
    emit_l1(env, debug=debug, xs_src=x1loc, shared_d=shared_d)
    fin = P.cnt["sync"]
    P.wait("sync", "sync", fin)
    P.wait("gpsimd", "gpsimd", P.cnt["gpsimd"])
    P.wait("gpsimd", "sync", fin)
    block = ctx.enter_context(nc.Block())
    P.finalize(block)
    ctx.close()
    return nc


def _l0_maps(inputs):
    import ml_dtypes
    f = np.float32
    c = np.ascontiguousarray
    g = lambda k: np.asarray(inputs[k], dtype=f)
    shared = dict(
        wqt=c(g("L0_attn_Wq").T), wkt=c(g("L0_attn_Wk").T),
        wvt=c(g("L0_attn_Wv").T), wot=c(g("L0_attn_Wo").T),
        bq=g("L0_attn_bq").reshape(1, D), bk=g("L0_attn_bk").reshape(1, D),
        bv=g("L0_attn_bv").reshape(1, D), bo=g("L0_attn_bo").reshape(1, D),
        wqst=c(g("L0_attn_Wqs").T), wkst=c(g("L0_attn_Wks").T),
        wvst=c(g("L0_attn_Wvs").T),
        bqs=g("L0_attn_bqs").reshape(1, 3), bks=g("L0_attn_bks").reshape(1, 3),
        bvs=g("L0_attn_bvs").reshape(1, 3),
        wkst_r1=c(g("L0_attn_Wks").T[:, [1, 2, 0]]),
        wkst_r2=c(g("L0_attn_Wks").T[:, [2, 0, 1]]),
        wvst_r1=c(g("L0_attn_Wvs").T[:, [1, 2, 0]]),
        wvst_r2=c(g("L0_attn_Wvs").T[:, [2, 0, 1]]),
        bks_r1=c(g("L0_attn_bks")[[1, 2, 0]].reshape(1, 3)),
        bks_r2=c(g("L0_attn_bks")[[2, 0, 1]].reshape(1, 3)),
        bvs_r1=c(g("L0_attn_bvs")[[1, 2, 0]].reshape(1, 3)),
        bvs_r2=c(g("L0_attn_bvs")[[2, 0, 1]].reshape(1, 3)),
        bqc=c(g("L0_attn_bq").reshape(8, 128).T),
        bkc=c(g("L0_attn_bk").reshape(8, 128).T),
        bvc=c(g("L0_attn_bv").reshape(8, 128).T),
        temp=g("L0_attn_temp").reshape(1, 1),
        rw=c(g("L0_ffn_router_w")), rb=g("L0_ffn_router_b").reshape(1, E),
        w1=c(g("L0_ffn_w1")), b1r=c(g("L0_ffn_b1").reshape(E, 1, FF1)),
        w2=c(g("L0_ffn_w2")), b2r=c(g("L0_ffn_b2").reshape(E, 1, D)),
        iden=np.eye(128, dtype=ml_dtypes.bfloat16),
        idenf=np.eye(128, dtype=f),
        onesf=np.ones((1, 512), f),
        iota=np.broadcast_to(np.arange(E, dtype=f), (128, E)).copy(),
        iotam=np.broadcast_to(np.arange(E, dtype=f) - 999.0, (128, E)).copy(),
    )
    x = np.asarray(inputs["L0_attn_x"], dtype=f)
    return [dict(shared, x=c(np.roll(x, -i * JPC, axis=1)))
            for i in range(NCORES)]


def _l1_maps(inputs, x1=None):
    f = np.float32
    c = np.ascontiguousarray
    g = lambda k: np.asarray(inputs[k], dtype=f)
    fm8 = lambda k: c(g(k).reshape(8, 128).T)
    fm32 = lambda k: c(g(k).reshape(32, 128).T)
    shared = dict(
        ln1g=g("L1_attn_ln1_g").reshape(1, D), ln1b=g("L1_attn_ln1_b").reshape(1, D),
        ln2gf=fm8("L1_attn_ln2_g"), ln2bf=fm8("L1_attn_ln2_b"),
        tmkf=fm8("L1_attn_time_mix_k"), tmvf=fm8("L1_attn_time_mix_v"),
        tmrf=fm8("L1_attn_time_mix_r"), tdf=fm8("L1_attn_time_decay"),
        tff=fm8("L1_attn_time_first"),
        cmkf=fm8("L1_attn_cmix_k"), cmrf=fm8("L1_attn_cmix_r"),
        wkt1=c(g("L1_attn_Wk").T), wvt1=c(g("L1_attn_Wv").T),
        wrt1=c(g("L1_attn_Wr").T), wot1=c(g("L1_attn_Wo").T),
        wckt=c(g("L1_attn_Wck").T), wcvt=c(g("L1_attn_Wcv").T),
        wcrt=c(g("L1_attn_Wcr").T),
        w1f=c(g("L1_ffn_W1")), w2f=c(g("L1_ffn_W2")),
        b1f=g("L1_ffn_b1").reshape(1, FF2), b2f=g("L1_ffn_b2").reshape(1, D),
        qphf=fm32("L1_ffn_quantum_phases"),
        qwrf=fm32("L1_ffn_quantum_weights_real"),
        qwif=fm32("L1_ffn_quantum_weights_imag"),
        idenf=np.eye(128, dtype=f),
        onesf=np.ones((1, 512), f),
        onescol=np.ones((128, 1), f),
    )
    maps = []
    for core in range(NCORES):
        m = dict(shared)
        if x1 is not None:
            m["xs"] = c(x1[:, core * JPC:(core + 1) * JPC, :])
        maps.append(m)
    return maps


def kernel(**inputs):
    maps0 = _l0_maps(inputs)
    maps1 = _l1_maps(inputs)
    maps = [dict(m0, **m1) for m0, m1 in zip(maps0, maps1)]
    nc = build_fused()
    res = run_bass_kernel_spmd(nc, maps, core_ids=list(range(NCORES)))
    out = np.concatenate([r["out"] for r in res.results], axis=1)
    return out


def kernel_unfused(**inputs):
    maps0 = _l0_maps(inputs)
    nc0 = build_l0()
    res0 = run_bass_kernel_spmd(nc0, maps0, core_ids=list(range(NCORES)))
    maps1 = _l1_maps(inputs)
    for i in range(NCORES):
        maps1[i]["xs"] = np.ascontiguousarray(res0.results[i]["x1"])
    nc1 = build_l1()
    res1 = run_bass_kernel_spmd(nc1, maps1, core_ids=list(range(NCORES)))
    out = np.concatenate([r["out"] for r in res1.results], axis=1)
    return out


# version 06
# speedup vs baseline: 2.7059x; 1.2188x over previous
import numpy as np
from contextlib import ExitStack

import concourse.bass as bass
import concourse.mybir as mybir
from concourse.bass_utils import run_bass_kernel_spmd

F32 = mybir.dt.float32
BF16 = mybir.dt.bfloat16
AF = mybir.ActivationFunctionType
ALU = mybir.AluOpType
AX = mybir.AxisListType

NCORES = 8
B, T, D = 16, 2048, 1024
BPC = B // NCORES
JPC = T // NCORES
H, HD = 16, 64
E, FF1 = 16, 2048
FF2 = 4096
INV_D = 1.0 / D
EPS = 1e-5

ENGS = ("sync", "tensor", "vector", "scalar", "gpsimd")
ARENA_COLS = 53056


class Arena:
    def __init__(self, ap_f32):
        self.f = ap_f32
        self.bf = ap_f32.bitcast(BF16)
        self.off = 0
        self.marks = []

    def mark(self):
        self.marks.append(self.off)

    def release(self):
        self.off = self.marks.pop()

    def f32(self, parts, cols):
        off = (self.off + 3) & ~3
        self.off = off + 4 * cols
        assert self.off <= ARENA_COLS * 4, "arena overflow"
        return self.f[0:parts, off // 4: off // 4 + cols]

    def b16(self, parts, cols):
        off = (self.off + 3) & ~3
        self.off = off + 2 * cols
        self.off = (self.off + 3) & ~3
        assert self.off <= ARENA_COLS * 4, "arena overflow"
        return self.bf[0:parts, off // 2: off // 2 + cols]


class Plan:
    def __init__(self, nc, ctx):
        self.nc = nc
        self.ops = {e: [] for e in ENGS}
        self.sem = {e: ctx.enter_context(nc.semaphore("sem_" + e)) for e in ENGS}
        self.cnt = {e: 0 for e in ENGS}
        self.bsem = {e: ctx.enter_context(nc.semaphore("bar_" + e))
                     for e in ("sync", "gpsimd")}
        self.bcnt = {e: 0 for e in ("sync", "gpsimd")}
        self.ch = {c: ctx.enter_context(nc.semaphore("ch_" + c))
                   for c in ("ld0", "ld1")}
        self.chcnt = {c: 0 for c in self.ch}

    def dma_ch(self, queue, chan, out, in_, **kw):
        sem = self.ch[chan]
        prev = self.chcnt[chan]
        if prev:
            self.ops[queue].append(
                lambda e, sem=sem, prev=prev: e.wait_ge(sem, prev))
        self.chcnt[chan] += 16

        def fn(e, out=out, in_=in_, kw=kw, sem=sem):
            e.dma_start(out=out, in_=in_, **kw).then_inc(sem, 16)

        self.ops[queue].append(fn)
        return self.chcnt[chan]

    def wait_ch(self, eng, chan, value):
        if value <= 0:
            return
        sem = self.ch[chan]
        self.ops[eng].append(lambda e, sem=sem, value=value: e.wait_ge(sem, value))

    def emit(self, eng, fn, inc=0):
        if eng == "vector":
            sem = self.sem[eng]
            prev = self.cnt[eng]
            if prev:
                self.ops[eng].append(
                    lambda e, sem=sem, prev=prev: e.wait_ge(sem, prev))
            self.cnt[eng] += 1
            self.ops[eng].append(
                lambda e, fn=fn, sem=sem: fn(e).then_inc(sem, 1))
            return self.cnt[eng]
        if inc:
            sem = self.sem[eng]
            prev = self.cnt[eng]
            if eng in ("tensor", "scalar") and prev:
                self.ops[eng].append(
                    lambda e, sem=sem, prev=prev: e.wait_ge(sem, prev))
            self.cnt[eng] += inc

            def wrapped(e, fn=fn, sem=sem, inc=inc):
                fn(e).then_inc(sem, inc)

            self.ops[eng].append(wrapped)
            return self.cnt[eng]
        self.ops[eng].append(fn)
        return self.cnt[eng]

    def wait(self, eng, producer_eng, value):
        if value <= 0:
            return
        if producer_eng in ("sync", "gpsimd"):
            value = self.cnt[producer_eng]
        sem = self.sem[producer_eng]
        self.ops[eng].append(lambda e, sem=sem, value=value: e.wait_ge(sem, value))

    def barrier(self):
        for eng in ENGS:
            if eng in self.bsem:
                sem_d = self.sem[eng]
                val_d = self.cnt[eng]
                if val_d:
                    self.ops[eng].append(
                        lambda e, sem=sem_d, val=val_d: e.wait_ge(sem, val))
                self.bcnt[eng] += 1
                sem = self.bsem[eng]
                self.ops[eng].append(
                    lambda e, sem=sem: e.nop().then_inc(sem, 1))
            else:
                self.emit(eng, lambda e: e.nop(), inc=1)
        snap = dict(self.cnt)
        bsnap = dict(self.bcnt)
        for eng in ENGS:
            for other in ENGS:
                if other == eng:
                    continue
                if other in self.bsem:
                    sem = self.bsem[other]
                    val = bsnap[other]
                    self.ops[eng].append(
                        lambda e, sem=sem, val=val: e.wait_ge(sem, val))
                else:
                    self.wait(eng, other, snap[other])

    def finalize(self, block):
        for eng in ENGS:
            ops = self.ops[eng]
            getattr(block, eng)(lambda engobj, ops=ops: [fn(engobj) for fn in ops])


def dma(P, queue, out, in_, **kw):
    P.wait(queue, queue, P.cnt[queue])
    return P.emit(
        queue, lambda e, out=out, in_=in_, kw=kw: e.dma_start(out=out, in_=in_, **kw),
        inc=16)


def mm(P, out, lhsT, rhs, start, stop, inc=0):
    return P.emit("tensor", lambda te, out=out, l=lhsT, r=rhs, a=start, b=stop:
                  te.matmul(out, l, r, start=a, stop=b), inc=inc)


def transp(P, out, in_, iden, inc=1):
    return P.emit("tensor", lambda te, out=out, i=in_, d=iden:
                  te.transpose(out, i, d), inc=inc)


def _mk_env():
    nc = bass.Bass("TRN2", target_bir_lowering=False)
    ctx = ExitStack()
    arena_t = ctx.enter_context(nc.sbuf_tensor("arena", [128, ARENA_COLS], F32))
    A = Arena(arena_t[:, :])
    pbig = ctx.enter_context(nc.psum_tensor("pbig", [128, 2048], F32))
    pst0 = ctx.enter_context(nc.psum_tensor("pst0", [128, 512], F32))
    pst1 = ctx.enter_context(nc.psum_tensor("pst1", [128, 512], F32))
    ppv = ctx.enter_context(nc.psum_tensor("ppv", [128, 512], F32))
    pbf = ctx.enter_context(nc.psum_tensor("pbf", [128, 512], BF16))
    P = Plan(nc, ctx)
    return dict(nc=nc, ctx=ctx, A=A, P=P, pbig=pbig, pst0=pst0, pst1=pst1,
                ppv=ppv, pbf=pbf)


def emit_l0(env, debug=False, x1loc=None):
    nc, ctx, A, P = env["nc"], env["ctx"], env["A"], env["P"]
    pbig, pst0, pst1, ppv, pbf = (env["pbig"], env["pst0"], env["pst1"],
                                  env["ppv"], env["pbf"])
    di = lambda n, s, d=F32: nc.dram_tensor(n, s, d, kind="ExternalInput").ap()

    x_in = di("x", [B, T, D])
    wqt, wkt, wvt, wot = (di(n, [D, D]) for n in ("wqt", "wkt", "wvt", "wot"))
    bq, bk, bv, bo = (di(n, [1, D]) for n in ("bq", "bk", "bv", "bo"))
    wqst, wkst, wvst = (di(n, [HD, 3]) for n in ("wqst", "wkst", "wvst"))
    bqs, bks, bvs = (di(n, [1, 3]) for n in ("bqs", "bks", "bvs"))
    wkst_r1, wkst_r2, wvst_r1, wvst_r2 = (
        di(n, [HD, 3]) for n in ("wkst_r1", "wkst_r2", "wvst_r1", "wvst_r2"))
    bks_r1, bks_r2, bvs_r1, bvs_r2 = (
        di(n, [1, 3]) for n in ("bks_r1", "bks_r2", "bvs_r1", "bvs_r2"))
    bqc_d, bkc_d, bvc_d = (di(n, [128, 8]) for n in ("bqc", "bkc", "bvc"))
    temp = di("temp", [1, 1])
    rw_d = di("rw", [D, E])
    rb_d = di("rb", [1, E])
    w1_d = di("w1", [E, D, FF1])
    b1_d = di("b1r", [E, 1, FF1])
    b1c_d = di("b1c", [E, 128, 16])
    w2_d = di("w2", [E, FF1, D])
    b2_d = di("b2r", [E, 1, D])
    idb_d = di("iden", [128, 128], BF16)
    idf_d = di("idenf", [128, 128])
    onef_d = di("onesf", [1, 512])
    iota_d = di("iota", [128, E])
    iotam_d = di("iotam", [128, E])

    if x1loc is None:
        x1loc = nc.dram_tensor("x1", [B, JPC, D], F32, kind="ExternalOutput").ap()

    def scr(name, shape, dtyp=F32):
        kind = "ExternalOutput" if debug else "Internal"
        return nc.dram_tensor(name, shape, dtyp, kind=kind).ap()

    xnbf = scr("xnbf", [B, 8, 128, T], BF16)
    stm = scr("stm", [B, JPC, D])
    cdr = scr("cdr", [B, E, JPC])
    PST = (pst0, pst1)
    NTL = JPC // 128

    A.mark()
    idb = A.b16(128, 128)
    idf = A.f32(128, 128)
    onef = A.f32(1, 512)
    iota_s = A.f32(128, E)
    iotam_s = A.f32(128, E)
    tempv = A.f32(128, 1)
    rw_s = A.f32(128, 8 * E)
    rb_s = A.f32(1, E)
    bo_s = A.f32(1, D)
    bqc = A.f32(128, 8)
    bkc = A.f32(128, 8)
    bvc = A.f32(128, 8)
    wqs6 = A.b16(65, 6)
    wkab = A.b16(65, 6)
    wvba = A.b16(65, 6)
    bqs_s = A.f32(1, 3)
    eps_c = A.f32(128, 1)

    dma(P, "sync", idb, idb_d)
    dma(P, "sync", idf, idf_d)
    dma(P, "sync", onef, onef_d)
    dma(P, "sync", iota_s, iota_d)
    dma(P, "sync", iotam_s, iotam_d)
    dma(P, "sync", tempv, temp.partition_broadcast(128))
    dma(P, "sync", rb_s, rb_d)
    for dc in range(8):
        dma(P, "sync", rw_s[:, dc * E:(dc + 1) * E], rw_d[dc * 128:(dc + 1) * 128, :])
    dma(P, "sync", bo_s, bo)
    dma(P, "sync", bqc, bqc_d)
    dma(P, "sync", bkc, bkc_d)
    dma(P, "sync", bvc, bvc_d)
    dma(P, "gpsimd", wqs6[0:64, 0:3], wqst)
    dma(P, "gpsimd", wqs6[0:64, 3:6], wqst)
    dma(P, "gpsimd", wkab[0:64, 0:3], wkst_r1)
    dma(P, "gpsimd", wkab[0:64, 3:6], wkst_r2)
    dma(P, "gpsimd", wkab[64:65, 0:3], bks_r1)
    dma(P, "gpsimd", wkab[64:65, 3:6], bks_r2)
    dma(P, "gpsimd", wvba[0:64, 0:3], wvst_r2)
    dma(P, "gpsimd", wvba[0:64, 3:6], wvst_r1)
    dma(P, "gpsimd", wvba[64:65, 0:3], bvs_r2)
    dma(P, "gpsimd", wvba[64:65, 3:6], bvs_r1)
    w_pro = dma(P, "sync", bqs_s, bqs)
    g_pro = P.cnt["gpsimd"]
    P.wait("vector", "sync", w_pro)
    P.wait("vector", "gpsimd", g_pro)
    P.emit("vector", lambda v: v.tensor_scalar(
        out=wqs6[64:65, 0:3], in0=bqs_s, scalar1=tempv[0:1, 0:1],
        scalar2=None, op0=ALU.mult))
    P.emit("vector", lambda v: v.tensor_scalar(
        out=wqs6[64:65, 3:6], in0=bqs_s, scalar1=tempv[0:1, 0:1],
        scalar2=None, op0=ALU.mult))
    P.emit("vector", lambda v: v.tensor_scalar(
        out=wvba[0:65, 3:6], in0=wvba[0:65, 3:6], scalar1=-1.0,
        scalar2=None, op0=ALU.mult))
    P.emit("vector", lambda v: v.memset(eps_c, EPS))
    P.barrier()

    A.mark()
    xt_b = [A.f32(128, D) for _ in range(4)]
    xn_b = [A.f32(128, D) for _ in range(4)]
    xsq = A.f32(128, D)
    xnf = A.f32(128, D)
    stat = A.f32(128, 16)
    sd4 = A.f32(128, 4)
    rstd4 = A.f32(128, 4)
    msq4 = A.f32(128, 4)
    asm_b = A.b16(128, 8 * 512)
    Lb = A.f32(128, NTL * E)
    curb = A.f32(128, NTL * E)
    eqb = A.f32(128, NTL * E)
    ub = eqb
    umb = A.f32(128, NTL * E)
    maskb = A.f32(128, NTL * E)
    mx0 = A.f32(128, NTL)
    nmx0 = A.f32(128, NTL)
    mxr = A.f32(128, NTL)
    selb = A.f32(128, NTL)
    den16 = A.f32(128, NTL)
    rden = A.f32(128, NTL)
    ctmp = A.f32(16, 128)
    ctile = A.f32(128, E)
    Sl = lambda t_, ti: t_[:, ti * E:(ti + 1) * E]

    for b in range(B):
        prev_xn_reads = 0
        prev_sq = 0
        prev_lg = 0
        prev_vxn = 0
        prev_asm = 0
        for g in range(4):
            tis = [g * 4 + i for i in range(4)]
            P.wait("sync", "vector", prev_vxn)
            P.wait("sync", "scalar", prev_sq)
            w_x = 0
            for i, ti in enumerate(tis):
                w_x = dma(P, "sync", xt_b[i], x_in[b, ti * 128:(ti + 1) * 128, :])
            P.wait("vector", "sync", w_x)
            P.wait("scalar", "sync", w_x)
            for i in range(4):
                P.emit("vector", lambda v, i=i: v.reduce_sum(
                    out=stat[:, i:i + 1], in_=xt_b[i], axis=AX.X))
            vsq = []
            for i in range(4):
                if i >= 1:
                    P.wait("scalar", "vector", vsq[i - 1])
                sqw = P.emit("scalar", lambda s, i=i: s.activation(
                    out=xsq, in_=xt_b[i], func=AF.Square), inc=1)
                P.wait("vector", "scalar", sqw)
                vsq.append(P.emit("vector", lambda v, i=i: v.reduce_sum(
                    out=stat[:, 4 + i:5 + i], in_=xsq, axis=AX.X), inc=1))
            prev_sq = P.cnt["scalar"]
            vm = 0
            for i in range(4):
                vm = P.emit("vector", lambda v, i=i: v.tensor_scalar(
                    out=stat[:, 8 + i:9 + i], in0=stat[:, i:i + 1], scalar1=INV_D,
                    scalar2=None, op0=ALU.mult), inc=1)
            P.wait("scalar", "vector", vm)
            am = P.emit("scalar", lambda s: s.activation(
                out=msq4, in_=stat[:, 8:12], func=AF.Square), inc=1)
            P.wait("vector", "scalar", am)
            vnv = 0
            for i in range(4):
                vnv = P.emit("vector", lambda v, i=i: v.scalar_tensor_tensor(
                    out=stat[:, 12 + i:13 + i], in0=stat[:, 4 + i:5 + i],
                    scalar=INV_D, in1=msq4[:, i:i + 1],
                    op0=ALU.mult, op1=ALU.subtract), inc=1)
            P.wait("scalar", "vector", vnv)
            asd = P.emit("scalar", lambda s: s.activation(
                out=sd4, in_=stat[:, 12:16], func=AF.Sqrt, bias=eps_c), inc=1)
            P.wait("vector", "scalar", asd)
            P.wait("vector", "tensor", prev_xn_reads)
            P.wait("vector", "sync", prev_asm)
            P.emit("vector", lambda v: v.reciprocal(out=rstd4, in_=sd4))
            for i in range(4):
                P.emit("vector", lambda v, i=i: v.tensor_scalar(
                    out=xn_b[i], in0=xt_b[i], scalar1=stat[:, 8 + i:9 + i],
                    scalar2=None, op0=ALU.subtract))
            vxn = 0
            for i in range(4):
                vxn = P.emit("vector", lambda v, i=i: v.tensor_scalar(
                    out=xn_b[i], in0=xn_b[i], scalar1=rstd4[:, i:i + 1],
                    scalar2=None, op0=ALU.mult), inc=1)
            prev_vxn = vxn

            for i, ti in enumerate(tis):
                P.wait("tensor", "vector", vxn)
                P.wait("tensor", "scalar", P.cnt["scalar"])
                P.wait("tensor", "vector", P.cnt["vector"])
                is_loc = ti < NTL
                if is_loc:
                    P.wait("scalar", "tensor", prev_lg)
                evs = []
                for dc in range(8):
                    reg = pbig[:, (dc % 2) * 512:(dc % 2) * 512 + 128]
                    if dc >= 2:
                        P.wait("tensor", "scalar", evs[dc - 2])
                        P.wait("tensor", "vector", P.cnt["vector"])
                    tw = transp(P, reg, xn_b[i][:, dc * 128:(dc + 1) * 128], idf)
                    P.wait("scalar", "tensor", tw)
                    sf = P.emit("scalar", lambda s, dc=dc, reg=reg: s.copy(
                        out=xnf[:, dc * 128:(dc + 1) * 128], in_=reg), inc=1)
                    evs.append(sf)
                    P.wait("vector", "scalar", sf)
                    P.emit("vector", lambda v, dc=dc, i=i: v.tensor_copy(
                        out=asm_b[:, dc * 512 + i * 128: dc * 512 + i * 128 + 128],
                        in_=xnf[:, dc * 128:(dc + 1) * 128]))
                if is_loc:
                    lreg = pbig[:, 1920:1920 + E]
                    P.wait("tensor", "scalar", evs[-1])
                    for dc in range(8):
                        mm(P, lreg, xnf[:, dc * 128:(dc + 1) * 128],
                           rw_s[:, dc * E:(dc + 1) * E], dc == 0, False)
                    lg = mm(P, lreg, onef[0:1, 0:128], rb_s, False, True, inc=1)
                    prev_lg = lg
                    P.wait("vector", "tensor", lg)
                    P.emit("vector", lambda v, ti=ti, lreg=lreg: v.tensor_copy(
                        out=Sl(Lb, ti), in_=lreg), inc=1)
                prev_xn_reads = P.cnt["tensor"]
            P.wait("sync", "vector", P.cnt["vector"])
            for dc in range(8):
                prev_asm = dma(P, "sync", xnbf[b, dc, :, g * 512:(g + 1) * 512],
                               asm_b[:, dc * 512:(dc + 1) * 512])

        P.wait("vector", "vector", P.cnt["vector"])
        P.emit("vector", lambda v: v.tensor_copy(out=curb, in_=Lb))
        for ti in range(NTL):
            P.emit("vector", lambda v, ti=ti: v.reduce_max(
                out=nmx0[:, ti:ti + 1], in_=Sl(Lb, ti), axis=AX.X, negate=True))
        vtk = 0
        for r in range(4):
            for ti in range(NTL):
                P.emit("vector", lambda v, ti=ti: v.reduce_max(
                    out=mxr[:, ti:ti + 1], in_=Sl(curb, ti), axis=AX.X))
            if r == 0:
                P.emit("vector", lambda v: v.tensor_copy(out=mx0, in_=mxr))
            for ti in range(NTL):
                P.emit("vector", lambda v, ti=ti: v.tensor_scalar(
                    out=Sl(eqb, ti), in0=Sl(curb, ti), scalar1=mxr[:, ti:ti + 1],
                    scalar2=None, op0=ALU.is_equal))
            for ti in range(NTL):
                P.emit("vector", lambda v, ti=ti: v.tensor_tensor(
                    out=Sl(umb, ti), in0=Sl(eqb, ti), in1=iotam_s, op=ALU.mult))
            for ti in range(NTL):
                P.emit("vector", lambda v, ti=ti: v.tensor_scalar(
                    out=Sl(umb, ti), in0=Sl(umb, ti), scalar1=999.0, scalar2=None,
                    op0=ALU.add))
            for ti in range(NTL):
                P.emit("vector", lambda v, ti=ti: v.tensor_reduce(
                    out=selb[:, ti:ti + 1], in_=Sl(umb, ti), axis=AX.X, op=ALU.min))
            for ti in range(NTL):
                P.emit("vector", lambda v, ti=ti: v.tensor_scalar(
                    out=Sl(eqb, ti), in0=iota_s, scalar1=selb[:, ti:ti + 1],
                    scalar2=None, op0=ALU.is_equal))
            if r == 0:
                P.emit("vector", lambda v: v.tensor_copy(out=maskb, in_=eqb))
            else:
                vtk = P.emit("vector", lambda v: v.tensor_tensor(
                    out=maskb, in0=maskb, in1=eqb, op=ALU.add),
                    inc=(1 if r == 3 else 0))
            if r < 3:
                for ti in range(NTL):
                    P.emit("vector", lambda v, ti=ti: v.scalar_tensor_tensor(
                        out=Sl(curb, ti), in0=Sl(eqb, ti), scalar=-1e30,
                        in1=Sl(curb, ti), op0=ALU.mult, op1=ALU.add))
        P.wait("scalar", "vector", vtk)
        au = 0
        for ti in range(NTL):
            au = P.emit("scalar", lambda s, ti=ti: s.activation(
                out=Sl(ub, ti), in_=Sl(Lb, ti), func=AF.Exp,
                bias=nmx0[:, ti:ti + 1], scale=1.0), inc=1)
        P.wait("vector", "scalar", au)
        P.emit("vector", lambda v: v.tensor_tensor(out=ub, in0=ub, in1=maskb,
                                                   op=ALU.mult))
        for ti in range(NTL):
            P.emit("vector", lambda v, ti=ti: v.reduce_sum(
                out=den16[:, ti:ti + 1], in_=Sl(ub, ti), axis=AX.X))
        P.emit("vector", lambda v: v.reciprocal(out=rden, in_=den16))
        prev_ct = 0
        for ti in range(NTL):
            P.wait("vector", "gpsimd", prev_ct)
            vc = P.emit("vector", lambda v, ti=ti: v.tensor_scalar(
                out=ctile, in0=Sl(ub, ti), scalar1=rden[:, ti:ti + 1],
                scalar2=None, op0=ALU.mult), inc=1)
            P.wait("tensor", "vector", vc)
            tc = transp(P, pbig[0:E, 512:640], ctile, idf)
            P.wait("vector", "tensor", tc)
            vt2 = P.emit("vector", lambda v: v.tensor_copy(
                out=ctmp, in_=pbig[0:E, 512:640]), inc=1)
            P.wait("gpsimd", "vector", vt2)
            prev_ct = dma(P, "gpsimd", cdr[b, :, ti * 128:(ti + 1) * 128], ctmp)
        P.wait("vector", "gpsimd", prev_ct)
    P.barrier()
    A.release()

    A.mark()
    xnb = A.b16(128, 8 * T)
    ofm = A.b16(128, 8 * JPC)
    vt = A.b16(128, 16 * 65)
    pt = [A.b16(128, 256) for _ in range(2)]
    nmax = A.f32(128, 2)
    nmax8 = A.f32(128, 8)
    nm2 = A.f32(2, 128)
    wq_s = A.b16(128, 8 * D)
    wk_s = A.b16(128, 8 * D)
    wv_s = A.b16(128, 8 * D)
    wo_s = A.b16(128, 8 * D)
    q_sb = A.b16(65, 2 * JPC)
    k_sb = A.b16(65, 2 * T)
    v_sb = A.b16(65, 2 * T)
    skb_s = [A.f32(6, T) for _ in range(2)]
    pp_b = [A.b16(6, T) for _ in range(2)]
    qstp_b = A.b16(6, 2 * JPC)
    rc_row = A.f32(1, 256)
    ot_s = A.b16(64, 256)
    sf_s = A.f32(128, 256)
    stma = A.f32(128, 2 * D)

    P.emit("vector", lambda v: v.memset(
        vt.rearrange("p (k c) -> p k c", c=65)[:, :, 64:65], 1.0))
    P.emit("vector", lambda v: v.memset(k_sb[64:65, :], 1.0))
    P.emit("vector", lambda v: v.memset(v_sb[64:65, :], 1.0))
    wW = 0
    for dc in range(8):
        for (wsrc, wdst) in ((wqt, wq_s), (wkt, wk_s), (wvt, wv_s), (wot, wo_s)):
            wW = dma(P, "gpsimd", wdst[:, dc * D:(dc + 1) * D],
                     wsrc[dc * 128:(dc + 1) * 128, :])

    v_pbig = P.cnt["vector"]
    s_pst = 0
    v_pst = P.cnt["vector"]
    s_ppv = 0
    for b in range(B):
        P.wait("sync", "tensor", P.cnt["tensor"])
        P.wait("sync", "vector", P.cnt["vector"])
        wxb = 0
        for dc in range(8):
            wxb = dma(P, "sync", xnb[:, dc * T:(dc + 1) * T], xnbf[b, dc, :, :])
        for fc in range(8):
            if b == 0 and fc == 0:
                P.wait("tensor", "gpsimd", wW)
            P.wait("tensor", "sync", wxb)
            P.wait("tensor", "vector", v_pbig)
            ev_prev = P.cnt["vector"]
            for wave in range(2):
                w1 = 0
                for (pj, (wfull, dst, bcol)) in enumerate(
                        ((wk_s, k_sb, bkc), (wv_s, v_sb, bvc))):
                    for tgi in range(2):
                        tg = wave * 2 + tgi
                        reg = pbig[:, (pj * 2 + tgi) * 512:(pj * 2 + tgi + 1) * 512]
                        if wave == 1 and pj == 0 and tgi == 0:
                            P.wait("tensor", "vector", ev_prev)
                        for dc in range(8):
                            mm(P, reg,
                               wfull[:, dc * D + fc * 128: dc * D + fc * 128 + 128],
                               xnb[:, dc * T + tg * 512: dc * T + tg * 512 + 512],
                               dc == 0, dc == 7, inc=(1 if dc == 7 else 0))
                w1 = P.cnt["tensor"]
                P.wait("vector", "tensor", w1)
                for (pj, (wfull, dst, bcol)) in enumerate(
                        ((wk_s, k_sb, bkc), (wv_s, v_sb, bvc))):
                    for tgi in range(2):
                        tg = wave * 2 + tgi
                        reg = pbig[:, (pj * 2 + tgi) * 512:(pj * 2 + tgi + 1) * 512]
                        for hh2 in range(2):
                            ev_prev = P.emit(
                                "vector",
                                lambda v, reg=reg, dst=dst, tg=tg, hh2=hh2,
                                bcol=bcol, fc=fc:
                                v.tensor_scalar(
                                    out=dst[0:64, hh2 * T + tg * 512:
                                            hh2 * T + (tg + 1) * 512],
                                    in0=reg[64 * hh2:64 * hh2 + 64, :],
                                    scalar1=bcol[64 * hh2:64 * hh2 + 64, fc:fc + 1],
                                    scalar2=None, op0=ALU.add), inc=1)
            P.wait("tensor", "scalar", s_pst)
            P.wait("tensor", "vector", v_pst)
            reg = pst0[:, 0:256]
            wq3 = 0
            for dc in range(8):
                wq3 = mm(P, reg, wq_s[:, dc * D + fc * 128: dc * D + fc * 128 + 128],
                         xnb[:, dc * T: dc * T + 256], dc == 0, dc == 7,
                         inc=(1 if dc == 7 else 0))
            P.wait("vector", "tensor", wq3)
            for hh2 in range(2):
                P.emit("vector", lambda v, reg=reg, hh2=hh2, fc=fc: v.tensor_scalar(
                    out=q_sb[0:64, hh2 * JPC: hh2 * JPC + 256],
                    in0=reg[64 * hh2:64 * hh2 + 64, :],
                    scalar1=bqc[64 * hh2:64 * hh2 + 64, fc:fc + 1],
                    scalar2=tempv[0:64, 0:1], op0=ALU.add, op1=ALU.mult), inc=1)
            P.wait("vector", "gpsimd", P.cnt["gpsimd"])
            evq = P.emit("vector", lambda v: v.memset(q_sb[64:65, :], 1.0), inc=1)
            v_pst = evq
            P.wait("tensor", "vector", evq)
            wqs_mm = mm(P, pst1[0:6, 0:512], wqs6, q_sb[0:65, 0:512],
                        True, True, inc=1)

            def emit_stp(hh2):
                ho2 = hh2 * T
                P.wait("tensor", "vector", v_pbig)
                stpw = 0
                for tg in range(4):
                    mm(P, pbig[0:6, tg * 512:(tg + 1) * 512], wkab,
                       k_sb[0:65, ho2 + tg * 512: ho2 + (tg + 1) * 512],
                       True, True)
                    stpw = mm(P, pbig[32:38, tg * 512:(tg + 1) * 512], wvba,
                              v_sb[0:65, ho2 + tg * 512: ho2 + (tg + 1) * 512],
                              True, True, inc=1)
                P.wait("vector", "tensor", stpw)
                P.emit("vector", lambda v, hh2=hh2: v.tensor_copy(
                    out=skb_s[hh2], in_=pbig[0:6, 0:T]), inc=1)
                return P.emit("vector", lambda v, hh2=hh2: v.tensor_tensor(
                    out=pp_b[hh2], in0=skb_s[hh2], in1=pbig[32:38, 0:T],
                    op=ALU.mult), inc=1)

            vcr_h = [None, None]
            P.wait("vector", "tensor", wqs_mm)
            evqs = P.emit("vector", lambda v: v.tensor_copy(
                out=qstp_b, in_=pst1[0:6, 0:512]), inc=1)
            v_pst = evqs
            vcr_h[0] = emit_stp(0)
            for hh in range(2):
                ho, hq = hh * T, hh * JPC
                vcr = vcr_h[hh]
                P.wait("tensor", "vector", vcr)
                P.wait("tensor", "vector", evqs)
                P.wait("tensor", "vector", v_pbig)
                rid = {}
                for qt in range(2):
                    for kg in range(4):
                        bank = pbig[:, kg * 512:(kg + 1) * 512]
                        if qt == 1:
                            P.wait("tensor", "vector", rid[(0, kg)])
                        mm(P, bank, q_sb[0:64, hq + qt * 128: hq + (qt + 1) * 128],
                           k_sb[0:64, ho + kg * 512: ho + (kg + 1) * 512],
                           True, False)
                        mw = mm(P, bank, qstp_b[:, hq + qt * 128: hq + (qt + 1) * 128],
                                pp_b[hh][:, kg * 512:(kg + 1) * 512], False, True,
                                inc=1)
                        P.wait("vector", "tensor", mw)
                        rid[(qt, kg)] = P.emit(
                            "vector", lambda v, qt=qt, kg=kg, bank=bank: v.reduce_max(
                                out=nmax8[:, 4 * qt + kg:4 * qt + kg + 1], in_=bank,
                                axis=AX.X, negate=True), inc=1)
                    P.emit("vector", lambda v, qt=qt: v.tensor_reduce(
                        out=nmax[:, qt:qt + 1], in_=nmax8[:, 4 * qt:4 * qt + 4],
                        op=ALU.min, axis=AX.X), inc=1)
                rmin1 = P.cnt["vector"]
                P.wait("tensor", "vector", rmin1)
                P.wait("tensor", "scalar", s_ppv)
                tn = transp(P, ppv[0:2, 0:128], nmax, idf)
                P.wait("vector", "tensor", tn)
                vnm = P.emit("vector", lambda v: v.tensor_copy(
                    out=nm2, in_=ppv[0:2, 0:128]), inc=1)
                P.wait("gpsimd", "vector", vnm)
                gn = dma(P, "gpsimd", q_sb[64:65, hq:hq + 256], nm2)
                P.wait("tensor", "vector", vnm)
                P.wait("tensor", "scalar", s_pst)
                vte = []
                vbanks = (pbf[:, :], pst0[:, :].bitcast(BF16),
                          pst1[:, :].bitcast(BF16), ppv[:, :].bitcast(BF16))
                for g in range(4):
                    bank = vbanks[g]
                    tpw = 0
                    for j in range(4):
                        kc = 4 * g + j
                        tpw = transp(P, bank[:, j * 64:(j + 1) * 64],
                                     v_sb[0:64, ho + kc * 128: ho + (kc + 1) * 128],
                                     idb[0:64, 0:64], inc=1)
                    P.wait("vector", "tensor", tpw)
                    vte.append(P.emit(
                        "vector", lambda v, bank=bank, g=g: v.tensor_copy(
                            out=vt.rearrange("p (k c) -> p k c", c=65)[
                                :, 4 * g:4 * g + 4, 0:64],
                            in_=bank[:, 0:256].rearrange("p (k c) -> p k c", c=64)),
                        inc=1))
                v_pst = vte[2]
                if hh == 0:
                    vcr_h[1] = emit_stp(1)
                aexp, pvw = [], []
                for kc in range(16):
                    reg = PST[kc % 2][:, 0:256]
                    if kc >= 2:
                        P.wait("tensor", "scalar", aexp[kc - 2])
                    elif kc == 0:
                        P.wait("tensor", "vector", vte[1])
                        P.wait("tensor", "gpsimd", gn)
                    else:
                        P.wait("tensor", "vector", vte[2])
                    mm(P, reg, k_sb[0:65, ho + kc * 128: ho + (kc + 1) * 128],
                       q_sb[0:65, hq: hq + 256], True, False)
                    stw = mm(P, reg, pp_b[hh][:, kc * 128:(kc + 1) * 128],
                             qstp_b[:, hq: hq + 256], False, True, inc=1)
                    P.wait("scalar", "tensor", stw)
                    if kc >= 2:
                        P.wait("scalar", "tensor", pvw[kc - 2])
                    aexp.append(P.emit("scalar", lambda s, reg=reg, kc=kc:
                                       s.activation(out=pt[kc % 2], in_=reg,
                                                    func=AF.Exp), inc=1))
                    if kc >= 1:
                        P.wait("tensor", "scalar", aexp[kc - 1])
                        if kc == 1:
                            P.wait("tensor", "vector", vte[3])
                        pvw.append(mm(P, ppv[0:65, 0:256],
                                      vt[:, (kc - 1) * 65:kc * 65],
                                      pt[(kc - 1) % 2], kc == 1, False, inc=1))
                P.wait("tensor", "scalar", aexp[15])
                pvw.append(mm(P, ppv[0:65, 0:256], vt[:, 15 * 65:16 * 65],
                              pt[15 % 2], False, True, inc=1))
                pv_last = pvw[-1]
                s_pst = aexp[15]
                P.wait("vector", "tensor", pv_last)
                vrc = P.emit("vector", lambda v: v.reciprocal(
                    out=rc_row, in_=ppv[64:65, 0:256]), inc=1)
                P.wait("tensor", "vector", vrc)
                if hh == 0:
                    P.wait("tensor", "vector", vcr_h[1])
                bc = mm(P, pbig[0:64, 0:256], onef[0:1, 0:64], rc_row,
                        True, True, inc=1)
                P.wait("scalar", "tensor", pv_last)
                P.wait("scalar", "vector", vrc)
                aev = P.emit("scalar", lambda s: s.copy(
                    out=ot_s, in_=ppv[0:64, 0:256]), inc=1)
                s_ppv = aev
                P.wait("vector", "tensor", bc)
                P.wait("vector", "scalar", aev)
                hs2 = 64 * hh
                vo = P.emit("vector", lambda v, fc=fc, hs2=hs2: v.tensor_tensor(
                    out=ofm[hs2:hs2 + 64, fc * JPC: fc * JPC + 256],
                    in0=ot_s, in1=pbig[0:64, 0:256], op=ALU.mult), inc=1)
                v_pbig = vo
        P.wait("tensor", "vector", P.cnt["vector"])
        P.wait("tensor", "scalar", P.cnt["scalar"])
        vt2l = []
        for fo in range(8):
            reg = pst0[:, 0:256]
            for dc in range(8):
                mm(P, reg, wo_s[:, dc * D + fo * 128: dc * D + fo * 128 + 128],
                   ofm[:, dc * JPC: dc * JPC + 256], dc == 0, False)
            mw = mm(P, reg, bo_s[0:1, fo * 128:(fo + 1) * 128], onef[0:1, 0:256],
                    False, True, inc=1)
            P.wait("scalar", "tensor", mw)
            aev = P.emit("scalar", lambda s, reg=reg: s.copy(out=sf_s, in_=reg),
                         inc=1)
            P.wait("tensor", "scalar", aev)
            for ts_ in range(2):
                regt = pbig[:, (ts_ % 2) * 512:(ts_ % 2) * 512 + 128]
                if len(vt2l) >= 2:
                    P.wait("tensor", "vector", vt2l[-2])
                tw = transp(P, regt, sf_s[:, ts_ * 128:(ts_ + 1) * 128], idf)
                P.wait("vector", "tensor", tw)
                vt2l.append(P.emit("vector",
                                   lambda v, regt=regt, ts_=ts_, fo=fo: v.tensor_copy(
                                       out=stma[:, ts_ * D + fo * 128:
                                                ts_ * D + fo * 128 + 128],
                                       in_=regt), inc=1))
            P.wait("scalar", "tensor", P.cnt["tensor"])
        P.wait("sync", "vector", P.cnt["vector"])
        for ts_ in range(2):
            dma(P, "sync", stm[b, ts_ * 128:(ts_ + 1) * 128, :],
                stma[:, ts_ * D:(ts_ + 1) * D])
        P.wait("vector", "sync", P.cnt["sync"])
        v_pbig = P.cnt["vector"]
        s_pst = P.cnt["scalar"]
    P.barrier()
    A.release()

    A.mark()
    xn_sg = A.b16(128, 8 * 1024)
    acc = [A.f32(128, 1024) for _ in range(8)]
    w1_s = [A.b16(128, 8 * FF1), A.b16(128, 8 * FF1)]
    w2_s = A.b16(128, 16 * D)
    hm_s = A.b16(128, 16 * 512)
    b1c_s = A.f32(128, 16)
    b2_row = A.f32(1, D)
    ce_row = A.f32(1, 1024)
    ce_s = A.f32(128, 512)
    gt_s = A.f32(128, 128)
    gtm = A.f32(128, 1024)
    xst = A.f32(128, 1024)
    sst = A.f32(128, 1024)

    vhm_prev = 0
    for sg in range(4):
        bs = [sg * 4 + i for i in range(4)]
        P.wait("sync", "tensor", P.cnt["tensor"])
        wx = 0
        for dc in range(8):
            for i, bb_ in enumerate(bs):
                wx = dma(P, "sync", xn_sg[:, dc * 1024 + i * 256: dc * 1024 + (i + 1) * 256],
                         xnbf[bb_, dc, :, 0:JPC])
        P.wait("vector", "tensor", P.cnt["tensor"])
        P.wait("vector", "scalar", P.cnt["scalar"])
        for oc in range(8):
            P.emit("vector", lambda v, oc=oc: v.memset(acc[oc], 0.0))
        P.wait("tensor", "sync", wx)
        if sg == 0:
            w1w = 0
            for dc in range(8):
                w1w = dma(P, "gpsimd", w1_s[0][:, dc * FF1:(dc + 1) * FF1],
                          w1_d[0, dc * 128:(dc + 1) * 128, :])
            w2w = 0
            for fcc in range(16):
                w2w = dma(P, "gpsimd", w2_s[:, fcc * D:(fcc + 1) * D],
                          w2_d[0, fcc * 128:(fcc + 1) * 128, :])
            w1ld, w2ld = w1w, w2w
        for e in range(E):
            wbuf = w1_s[e % 2]
            w1w, w2w = w1ld, w2ld
            P.wait("sync", "tensor", P.cnt["tensor"])
            P.wait("sync", "scalar", P.cnt["scalar"])
            bw1 = dma(P, "sync", b1c_s, b1c_d[e])
            bw2 = dma(P, "sync", b2_row, b2_d[e])
            for i, bb_ in enumerate(bs):
                dma(P, "sync", ce_row[0:1, i * 256:(i + 1) * 256],
                    cdr[bb_, e:e + 1, :])
            cw = P.cnt["sync"]
            for tt in range(2):
                P.wait("tensor", "sync", cw)
                bcw = mm(P, pbig[:, 0:512], onef[0:1, 0:128],
                         ce_row[0:1, tt * 512:(tt + 1) * 512], True, True, inc=1)
                P.wait("vector", "tensor", bcw)
                vce = P.emit("vector", lambda v: v.tensor_copy(
                    out=ce_s, in_=pbig[:, 0:512]), inc=1)
                if tt == 0:
                    P.wait("tensor", "gpsimd", P.cnt["gpsimd"])
                    P.wait("scalar", "sync", cw)
                P.wait("scalar", "vector", vhm_prev)
                ag = []
                vhm = 0
                for fcc in range(16):
                    reg = PST[fcc % 2][:, :]
                    if fcc >= 2:
                        P.wait("tensor", "scalar", ag[fcc - 2])
                    for dc in range(8):
                        hw = mm(P, reg,
                                wbuf[:, dc * FF1 + fcc * 128: dc * FF1 + fcc * 128 + 128],
                                xn_sg[:, dc * 1024 + tt * 512: dc * 1024 + tt * 512 + 512],
                                dc == 0, dc == 7, inc=(1 if dc == 7 else 0))
                    P.wait("scalar", "tensor", hw)
                    ag.append(P.emit("scalar", lambda s, reg=reg, fcc=fcc:
                                     s.activation(
                                         out=hm_s[:, fcc * 512:(fcc + 1) * 512],
                                         in_=reg, func=AF.Gelu,
                                         bias=b1c_s[:, fcc:fcc + 1]), inc=1))
                    P.wait("vector", "scalar", ag[fcc])
                    vhm = P.emit("vector", lambda v, fcc=fcc: v.tensor_tensor(
                        out=hm_s[:, fcc * 512:(fcc + 1) * 512],
                        in0=hm_s[:, fcc * 512:(fcc + 1) * 512], in1=ce_s,
                        op=ALU.mult), inc=1)
                vhm_prev = vhm
                if tt == 1:
                    en, bn = (e + 1) % E, (e + 1) % 2
                    srcidx = e + 1 if e + 1 < E else 0
                    P.wait("gpsimd", "tensor", P.cnt["tensor"])
                    w1n = 0
                    for dc in range(8):
                        w1n = dma(P, "gpsimd", w1_s[bn][:, dc * FF1:(dc + 1) * FF1],
                                  w1_d[srcidx, dc * 128:(dc + 1) * 128, :])
                    w1ld = w1n
                P.wait("tensor", "vector", vhm)
                if tt == 0:
                    P.wait("tensor", "gpsimd", w2w)
                va = []
                for oc in range(8):
                    reg = PST[oc % 2][:, :]
                    if oc >= 2:
                        P.wait("tensor", "vector", va[oc - 2])
                    for fcc in range(16):
                        mm(P, reg, w2_s[:, fcc * D + oc * 128: fcc * D + oc * 128 + 128],
                           hm_s[:, fcc * 512:(fcc + 1) * 512], fcc == 0, False)
                    ow = mm(P, reg, b2_row[0:1, oc * 128:(oc + 1) * 128],
                            ce_row[0:1, tt * 512:(tt + 1) * 512], False, True, inc=1)
                    P.wait("vector", "tensor", ow)
                    va.append(P.emit("vector", lambda v, reg=reg, oc=oc, tt=tt:
                                     v.tensor_tensor(
                                         out=acc[oc][:, tt * 512:(tt + 1) * 512],
                                         in0=acc[oc][:, tt * 512:(tt + 1) * 512],
                                         in1=reg, op=ALU.add), inc=1))
                P.wait("tensor", "vector", va[-1])
                P.wait("scalar", "tensor", P.cnt["tensor"])
                if tt == 1:
                    srcidx = e + 1 if e + 1 < E else 0
                    P.wait("gpsimd", "tensor", P.cnt["tensor"])
                    w2n = 0
                    for fcc in range(16):
                        w2n = dma(P, "gpsimd", w2_s[:, fcc * D:(fcc + 1) * D],
                                  w2_d[srcidx, fcc * 128:(fcc + 1) * 128, :])
                    w2ld = w2n
        prev_out = 0
        for t8 in range(8):
            bb_ = sg * 4 + t8 // 2
            j0 = (t8 % 2) * 128
            P.wait("sync", "vector", P.cnt["vector"])
            wxx = dma(P, "sync", xst, x_in[bb_, j0:j0 + 128, :])
            wss = dma(P, "sync", sst, stm[bb_, j0:j0 + 128, :])
            P.wait("vector", "sync", prev_out)
            P.wait("tensor", "vector", P.cnt["vector"])
            vgt = []
            for oc in range(8):
                asil = P.emit("scalar", lambda s, oc=oc, t8=t8: s.activation(
                    out=gt_s, in_=acc[oc][:, t8 * 128:(t8 + 1) * 128],
                    func=AF.Silu), inc=1)
                P.wait("tensor", "scalar", asil)
                regt = pbig[:, (oc % 2) * 512:(oc % 2) * 512 + 128]
                if oc >= 2:
                    P.wait("tensor", "vector", vgt[oc - 2])
                tw = transp(P, regt, gt_s, idf)
                P.wait("scalar", "tensor", tw)
                P.wait("vector", "tensor", tw)
                vgt.append(P.emit("vector", lambda v, regt=regt, oc=oc:
                                  v.tensor_copy(
                                      out=gtm[:, oc * 128:(oc + 1) * 128],
                                      in_=regt), inc=1))
            P.wait("vector", "sync", wxx)
            P.wait("vector", "sync", wss)
            P.wait("vector", "vector", vgt[-1])
            vmu = P.emit("vector", lambda v: v.tensor_tensor(
                out=gtm, in0=gtm, in1=sst, op=ALU.mult), inc=1)
            P.wait("vector", "vector", vmu)
            vf = P.emit("vector", lambda v: v.tensor_tensor(
                out=gtm, in0=gtm, in1=xst, op=ALU.add), inc=1)
            P.wait("sync", "vector", vf)
            prev_out = dma(P, "sync", x1loc[bb_, j0:j0 + 128, :], gtm)
    A.release()
    A.release()
    return dict(idf_d=idf_d, onef_d=onef_d)


def build_l0(debug=False):
    env = _mk_env()
    P, nc, ctx = env["P"], env["nc"], env["ctx"]
    emit_l0(env, debug=debug)
    fin = P.cnt["sync"]
    P.wait("sync", "sync", fin)
    P.wait("gpsimd", "sync", fin)
    block = ctx.enter_context(nc.Block())
    P.finalize(block)
    ctx.close()
    return nc


NTOK = B * JPC


def emit_l1(env, debug=False, xs_src=None, shared_d=None):
    nc, ctx, A, P = env["nc"], env["ctx"], env["A"], env["P"]
    pbig, pst0, pst1, ppv = env["pbig"], env["pst0"], env["pst1"], env["ppv"]
    di = lambda n, s, d=F32: nc.dram_tensor(n, s, d, kind="ExternalInput").ap()

    xs_in = di("xs", [B, JPC, D]) if xs_src is None else xs_src
    ln1g_r = di("ln1g", [1, D])
    ln1b_r = di("ln1b", [1, D])
    ln2g_f = di("ln2gf", [128, 8])
    ln2b_f = di("ln2bf", [128, 8])
    tmk_f = di("tmkf", [128, 8])
    tmv_f = di("tmvf", [128, 8])
    tmr_f = di("tmrf", [128, 8])
    td_f = di("tdf", [128, 8])
    tf_f = di("tff", [128, 8])
    cmk_f = di("cmkf", [128, 8])
    cmr_f = di("cmrf", [128, 8])
    wkt = di("wkt1", [D, D])
    wvt = di("wvt1", [D, D])
    wrt = di("wrt1", [D, D])
    wot = di("wot1", [D, D])
    wckt = di("wckt", [D, FF2])
    wcvt = di("wcvt", [FF2, D])
    wcrt = di("wcrt", [D, D])
    w1_d = di("w1f", [D, FF2])
    w2_d = di("w2f", [FF2, D])
    b1_r = di("b1f", [1, FF2])
    b2_r = di("b2f", [1, D])
    qph_f = di("qphf", [128, 32])
    qwr_f = di("qwrf", [128, 32])
    qwi_f = di("qwif", [128, 32])
    if shared_d is None:
        idf_d = di("idenf", [128, 128])
        onef_d = di("onesf", [1, 512])
    else:
        idf_d = shared_d["idf_d"]
        onef_d = shared_d["onef_d"]
    onec_d = di("onescol", [128, 1])

    out_d = nc.dram_tensor("out", [B, JPC, D], F32, kind="ExternalOutput").ap()

    def scr(name, shape, dtyp=F32):
        kind = "ExternalOutput" if debug else "Internal"
        return nc.dram_tensor(name, shape, dtyp, kind=kind).ap()

    xn1f = scr("xn1f", [8, 128, NTOK])
    xn1b = scr("xn1b", [8, 128, NTOK], BF16)
    xxf = scr("xxf", [8, 128, NTOK])
    kf = scr("kf", [8, 128, NTOK])
    vf = scr("vf", [8, 128, NTOK])
    rf = scr("rf", [8, 128, NTOK])
    rwb = scr("rwb", [8, 128, NTOK], BF16)
    x2f = scr("x2f", [8, 128, NTOK])
    yyf = scr("yyf", [8, 128, NTOK])
    sqb = scr("sqb", [32, 128, NTOK], BF16)
    s1f = scr("s1f", [8, 128, NTOK])
    PST = (pst0, pst1)

    A.mark()
    idf = A.f32(128, 128)
    onef = A.f32(1, 512)
    onec = A.f32(128, 1)
    eps_c = A.f32(128, 1)
    ln1g = A.f32(1, D)
    ln1b = A.f32(1, D)
    ln2g = A.f32(128, 8)
    ln2b = A.f32(128, 8)
    tmk = A.f32(128, 8)
    tmv = A.f32(128, 8)
    tmr = A.f32(128, 8)
    tfu = A.f32(128, 8)
    wneg = A.f32(128, 8)
    cmk = A.f32(128, 8)
    cmr = A.f32(128, 8)
    g1bc = A.f32(128, D)
    b1bc = A.f32(128, D)
    tdtmp = A.f32(128, 8)

    for (dst, srcd) in ((idf, idf_d), (onef, onef_d), (onec, onec_d),
                        (ln1g, ln1g_r), (ln1b, ln1b_r), (ln2g, ln2g_f),
                        (ln2b, ln2b_f), (tmk, tmk_f), (tmv, tmv_f),
                        (tmr, tmr_f), (tfu, tf_f), (tdtmp, td_f),
                        (cmk, cmk_f), (cmr, cmr_f)):
        dma(P, "sync", dst, srcd)
    w_pro = P.cnt["sync"]
    P.wait("vector", "sync", w_pro)
    P.emit("vector", lambda v: v.memset(eps_c, EPS))
    P.wait("scalar", "sync", w_pro)
    aexp = P.emit("scalar", lambda s: s.activation(
        out=wneg, in_=tdtmp, func=AF.Exp), inc=1)
    P.wait("vector", "scalar", aexp)
    P.emit("vector", lambda v: v.tensor_scalar(
        out=wneg, in0=wneg, scalar1=-1.0, scalar2=None, op0=ALU.mult))
    P.wait("tensor", "sync", w_pro)
    for (row, bc, bank) in ((ln1g, g1bc, 0), (ln1b, b1bc, 1)):
        for hlf in range(2):
            P.wait("tensor", "vector", P.cnt["vector"])
            w = mm(P, pbig[:, bank * 512: bank * 512 + 512],
                   onef[0:1, 0:128], row[0:1, hlf * 512:(hlf + 1) * 512],
                   True, True, inc=1)
            P.wait("vector", "tensor", w)
            P.emit("vector", lambda v, bc=bc, hlf=hlf, bank=bank: v.tensor_copy(
                out=bc[:, hlf * 512:(hlf + 1) * 512],
                in_=pbig[:, bank * 512: bank * 512 + 512]))
    P.barrier()

    NT2 = NTOK // 128

    A.mark()
    xt_b = [A.f32(128, D) for _ in range(4)]
    xn_b = [A.f32(128, D) for _ in range(4)]
    xx_b = [A.f32(128, D) for _ in range(4)]
    xsq = A.f32(128, D)
    stat = A.f32(128, 16)
    stat2 = A.f32(128, 16)
    sd4 = A.f32(128, 4)
    rstd4 = A.f32(128, 4)
    msq4 = A.f32(128, 4)
    asm_f = A.f32(128, 8 * 512)
    asm_b = A.b16(128, 8 * 512)
    asm_x = A.f32(128, 8 * 512)

    def tm_stats(src_tiles, scol, n=4):
        for i in range(n):
            P.emit("vector", lambda v, i=i: v.reduce_sum(
                out=stat[:, scol + i:scol + i + 1], in_=src_tiles[i], axis=AX.X))
        vsq = []
        for i in range(n):
            if i >= 1:
                P.wait("scalar", "vector", vsq[i - 1])
            sqw = P.emit("scalar", lambda s, i=i: s.activation(
                out=xsq, in_=src_tiles[i], func=AF.Square), inc=1)
            P.wait("vector", "scalar", sqw)
            vsq.append(P.emit("vector", lambda v, i=i: v.reduce_sum(
                out=stat[:, scol + 4 + i:scol + 5 + i], in_=xsq, axis=AX.X), inc=1))
        vm = 0
        for i in range(n):
            vm = P.emit("vector", lambda v, i=i: v.tensor_scalar(
                out=stat[:, scol + 8 + i:scol + 9 + i],
                in0=stat[:, scol + i:scol + i + 1], scalar1=INV_D,
                scalar2=None, op0=ALU.mult), inc=1)
        P.wait("scalar", "vector", vm)
        am = P.emit("scalar", lambda s: s.activation(
            out=msq4, in_=stat[:, scol + 8:scol + 12], func=AF.Square), inc=1)
        P.wait("vector", "scalar", am)
        vnv = 0
        for i in range(n):
            vnv = P.emit("vector", lambda v, i=i: v.scalar_tensor_tensor(
                out=stat[:, scol + 12 + i:scol + 13 + i],
                in0=stat[:, scol + 4 + i:scol + 5 + i], scalar=INV_D,
                in1=msq4[:, i:i + 1], op0=ALU.mult, op1=ALU.subtract), inc=1)
        P.wait("scalar", "vector", vnv)
        asd = P.emit("scalar", lambda s: s.activation(
            out=sd4, in_=stat[:, scol + 12:scol + 16], func=AF.Sqrt,
            bias=eps_c), inc=1)
        P.wait("vector", "scalar", asd)
        P.emit("vector", lambda v: v.reciprocal(out=rstd4, in_=sd4))

    prev_rd = 0
    prev_asm = 0
    for g in range(NT2 // 4):
        tis = [g * 4 + i for i in range(4)]
        P.wait("sync", "vector", P.cnt["vector"])
        P.wait("sync", "scalar", P.cnt["scalar"])
        w_x = 0
        for i, ti in enumerate(tis):
            bb_, jj = ti // 2, (ti % 2) * 128
            w_x = dma(P, "sync", xt_b[i], xs_in[bb_, jj:jj + 128, :])
        P.wait("vector", "sync", w_x)
        P.wait("scalar", "sync", w_x)
        tm_stats(xt_b, 0)
        for i in range(4):
            P.emit("vector", lambda v, i=i: v.tensor_scalar(
                out=xn_b[i], in0=xt_b[i], scalar1=stat[:, 8 + i:9 + i],
                scalar2=None, op0=ALU.subtract))
        for i in range(4):
            P.emit("vector", lambda v, i=i: v.tensor_scalar(
                out=xn_b[i], in0=xn_b[i], scalar1=rstd4[:, i:i + 1],
                scalar2=None, op0=ALU.mult))
        P.wait("scalar", "vector", P.cnt["vector"])
        tm_stats(xn_b, 0)
        for i in range(4):
            P.emit("vector", lambda v, i=i: v.tensor_scalar(
                out=xx_b[i], in0=xn_b[i], scalar1=stat[:, 8 + i:9 + i],
                scalar2=None, op0=ALU.subtract))
        for i in range(4):
            P.emit("vector", lambda v, i=i: v.tensor_scalar(
                out=xx_b[i], in0=xx_b[i], scalar1=rstd4[:, i:i + 1],
                scalar2=None, op0=ALU.mult))
        for i in range(4):
            P.emit("vector", lambda v, i=i: v.tensor_tensor(
                out=xx_b[i], in0=xx_b[i], in1=g1bc, op=ALU.mult))
        for i in range(4):
            P.emit("vector", lambda v, i=i: v.tensor_tensor(
                out=xx_b[i], in0=xx_b[i], in1=b1bc, op=ALU.add))
        vready = P.cnt["vector"]
        P.wait("tensor", "vector", vready)
        P.wait("vector", "sync", prev_asm)
        P.wait("scalar", "sync", prev_asm)
        evs = []
        for i in range(4):
            for dc in range(8):
                reg = pbig[:, (dc % 2) * 512:(dc % 2) * 512 + 128]
                reg2 = pbig[:, 1024 + (dc % 2) * 512:1024 + (dc % 2) * 512 + 128]
                k = i * 8 + dc
                if k >= 2:
                    P.wait("tensor", "scalar", evs[k - 2])
                tw = transp(P, reg, xn_b[i][:, dc * 128:(dc + 1) * 128], idf)
                tw2 = transp(P, reg2, xx_b[i][:, dc * 128:(dc + 1) * 128], idf)
                P.wait("scalar", "tensor", tw2)
                sf = P.emit("scalar", lambda s, reg=reg, dc=dc, i=i: s.copy(
                    out=asm_f[:, dc * 512 + i * 128: dc * 512 + i * 128 + 128],
                    in_=reg), inc=1)
                sx = P.emit("scalar", lambda s, reg2=reg2, dc=dc, i=i: s.copy(
                    out=asm_x[:, dc * 512 + i * 128: dc * 512 + i * 128 + 128],
                    in_=reg2), inc=1)
                evs.append(sx)
                P.wait("vector", "scalar", sf)
                P.emit("vector", lambda v, dc=dc, i=i: v.tensor_copy(
                    out=asm_b[:, dc * 512 + i * 128: dc * 512 + i * 128 + 128],
                    in_=asm_f[:, dc * 512 + i * 128: dc * 512 + i * 128 + 128]))
        P.wait("sync", "scalar", P.cnt["scalar"])
        P.wait("sync", "vector", P.cnt["vector"])
        for dc in range(8):
            dma(P, "sync", xn1f[dc, :, g * 512:(g + 1) * 512],
                asm_f[:, dc * 512:(dc + 1) * 512])
            dma(P, "sync", xn1b[dc, :, g * 512:(g + 1) * 512],
                asm_b[:, dc * 512:(dc + 1) * 512])
            prev_asm = dma(P, "sync", xxf[dc, :, g * 512:(g + 1) * 512],
                           asm_x[:, dc * 512:(dc + 1) * 512])
    P.barrier()
    A.release()

    A.mark()
    wk_s = A.b16(128, 8 * D)
    wv_s = A.b16(128, 8 * D)
    wr_s = A.b16(128, 8 * D)
    km_s = A.b16(128, 8 * 512)
    vm_s = A.b16(128, 8 * 512)
    rm_s = A.b16(128, 8 * 512)
    xwin = [A.f32(128, 768) for _ in range(2)]
    kev = A.f32(128, 512)
    vev = A.f32(128, 512)
    rev = A.f32(128, 512)

    wW = 0
    for dc in range(8):
        for (wsrc, wdst) in ((wkt, wk_s), (wvt, wv_s), (wrt, wr_s)):
            wW = dma(P, "gpsimd", wdst[:, dc * D:(dc + 1) * D],
                     wsrc[dc * 128:(dc + 1) * 128, :])
    for TT in range(8):
        t0 = TT * 512
        lo = max(0, t0 - 256)
        for ic in range(8):
            P.wait("sync", "vector", P.cnt["vector"])
            wx = P.dma_ch("sync", "ld%d" % (ic % 2), xwin[ic % 2][:, 0:t0 + 512 - lo],
                          xxf[ic, :, lo:t0 + 512])
            P.wait_ch("vector", "ld%d" % (ic % 2), wx)
            cur = xwin[ic % 2][:, 256:768] if TT > 0 else xwin[ic % 2][:, 0:512]
            for (mix, msb, mvec) in ((0, km_s, tmk), (1, vm_s, tmv), (2, rm_s, tmr)):
                dsl = msb[:, ic * 512:(ic + 1) * 512]
                if TT == 0:
                    P.emit("vector", lambda v, dsl=dsl, cur=cur, mvec=mvec, ic=ic:
                           v.tensor_scalar(out=dsl[:, 0:256], in0=cur[:, 0:256],
                                           scalar1=mvec[:, ic:ic + 1], scalar2=None,
                                           op0=ALU.mult))
                    sh = xwin[ic % 2][:, 0:256]
                    cu2 = xwin[ic % 2][:, 256:512]
                    P.emit("vector", lambda v, cu2=cu2, sh=sh: v.tensor_tensor(
                        out=kev[:, 0:256], in0=cu2, in1=sh, op=ALU.subtract))
                    P.emit("vector", lambda v, dsl=dsl, sh=sh, mvec=mvec, ic=ic:
                           v.scalar_tensor_tensor(
                               out=dsl[:, 256:512], in0=kev[:, 0:256],
                               scalar=mvec[:, ic:ic + 1], in1=sh,
                               op0=ALU.mult, op1=ALU.add))
                else:
                    sh = xwin[ic % 2][:, 0:512]
                    P.emit("vector", lambda v, cur=cur, sh=sh: v.tensor_tensor(
                        out=kev, in0=cur, in1=sh, op=ALU.subtract))
                    P.emit("vector", lambda v, dsl=dsl, sh=sh, mvec=mvec, ic=ic:
                           v.scalar_tensor_tensor(
                               out=dsl, in0=kev, scalar=mvec[:, ic:ic + 1],
                               in1=sh, op0=ALU.mult, op1=ALU.add))
        vmix = P.cnt["vector"]
        P.wait("tensor", "vector", vmix)
        if TT == 0:
            P.wait("tensor", "gpsimd", wW)
        for oc in range(8):
            for (wsb, msb, psm) in ((wk_s, km_s, pst0), (wv_s, vm_s, pst1),
                                    (wr_s, rm_s, ppv)):
                for ic in range(8):
                    mm(P, psm[:, :], wsb[:, ic * D + oc * 128: ic * D + oc * 128 + 128],
                       msb[:, ic * 512:(ic + 1) * 512], ic == 0, ic == 7,
                       inc=(1 if ic == 7 else 0))
            kw_ = P.cnt["tensor"]
            P.wait("scalar", "tensor", kw_)
            ke = P.emit("scalar", lambda s: s.copy(out=kev, in_=pst0[:, :]), inc=1)
            re_ = P.emit("scalar", lambda s: s.copy(out=rev, in_=ppv[:, :]), inc=1)
            P.wait("vector", "tensor", kw_)
            ve = P.emit("vector", lambda v: v.tensor_copy(out=vev, in_=pst1[:, :]),
                        inc=1)
            P.wait("sync", "scalar", re_)
            P.wait("sync", "vector", ve)
            dma(P, "sync", kf[oc, :, t0:t0 + 512], kev)
            dma(P, "sync", vf[oc, :, t0:t0 + 512], vev)
            w_st = dma(P, "sync", rf[oc, :, t0:t0 + 512], rev)
            P.wait("tensor", "sync", w_st)
    P.barrier()
    A.release()

    A.mark()
    aa = A.f32(128, 2048)
    bb = A.f32(128, 2048)
    pp = A.f32(128, 2048)
    kb_s = A.f32(128, 2048)
    vb_s = A.f32(128, 2048)
    rb_s = A.f32(128, 2048)
    t_a = A.f32(128, 2048)
    t_b = A.f32(128, 2048)
    t_c = A.f32(128, 2048)
    t_d = A.f32(128, 2048)
    t_e = A.f32(128, 2048)
    t_f = A.f32(128, 2048)
    t_g = A.f32(128, 2048)
    rw_o = A.b16(128, 2048)

    P.emit("vector", lambda v: v.memset(aa, 0.0))
    P.emit("vector", lambda v: v.memset(bb, 0.0))
    P.emit("vector", lambda v: v.memset(pp, -1e38))
    ppc, aac, bbc = pp, aa, bb
    for step in range(B):
        c0 = step * JPC
        P.wait("sync", "vector", P.cnt["vector"])
        P.wait("sync", "scalar", P.cnt["scalar"])
        wl = 0
        for dc in range(8):
            dma(P, "sync", kb_s[:, dc * 256:(dc + 1) * 256], kf[dc, :, c0:c0 + JPC])
            dma(P, "sync", vb_s[:, dc * 256:(dc + 1) * 256], vf[dc, :, c0:c0 + JPC])
            wl = dma(P, "sync", rb_s[:, dc * 256:(dc + 1) * 256],
                     rf[dc, :, c0:c0 + JPC])
        P.wait("vector", "sync", wl)
        P.wait("scalar", "sync", wl)
        for dc in range(8):
            P.emit("vector", lambda v, dc=dc: v.tensor_scalar(
                out=t_a[:, dc * 256:(dc + 1) * 256],
                in0=kb_s[:, dc * 256:(dc + 1) * 256],
                scalar1=tfu[:, dc:dc + 1], scalar2=None, op0=ALU.add))
        P.emit("vector", lambda v: v.tensor_tensor(out=t_b, in0=pp, in1=t_a,
                                                   op=ALU.max))
        P.emit("vector", lambda v: v.tensor_tensor(out=t_c, in0=pp, in1=t_b,
                                                   op=ALU.subtract))
        P.emit("vector", lambda v: v.tensor_tensor(out=t_d, in0=t_a, in1=t_b,
                                                   op=ALU.subtract))
        vv_ = P.cnt["vector"]
        P.wait("scalar", "vector", vv_)
        e1w = P.emit("scalar", lambda s: s.activation(out=t_c, in_=t_c, func=AF.Exp),
                     inc=1)
        e2w = P.emit("scalar", lambda s: s.activation(out=t_d, in_=t_d, func=AF.Exp),
                     inc=1)
        P.wait("vector", "scalar", e2w)
        P.emit("vector", lambda v: v.tensor_tensor(out=t_e, in0=t_c, in1=aa,
                                                   op=ALU.mult))
        P.emit("vector", lambda v: v.tensor_tensor(out=t_f, in0=t_d, in1=vb_s,
                                                   op=ALU.mult))
        P.emit("vector", lambda v: v.tensor_tensor(out=t_e, in0=t_e, in1=t_f,
                                                   op=ALU.add))
        P.emit("vector", lambda v: v.tensor_tensor(out=t_f, in0=t_c, in1=bb,
                                                   op=ALU.mult))
        P.emit("vector", lambda v: v.tensor_tensor(out=t_f, in0=t_f, in1=t_d,
                                                   op=ALU.add))
        P.emit("vector", lambda v: v.reciprocal(out=t_f, in_=t_f))
        P.emit("vector", lambda v: v.tensor_tensor(out=t_e, in0=t_e, in1=t_f,
                                                   op=ALU.mult))
        vv_ = P.cnt["vector"]
        P.wait("scalar", "vector", vv_)
        rsg = P.emit("scalar", lambda s: s.activation(out=t_f, in_=rb_s,
                                                      func=AF.Sigmoid), inc=1)
        P.wait("vector", "scalar", rsg)
        vrw = P.emit("vector", lambda v: v.tensor_tensor(out=rw_o, in0=t_f, in1=t_e,
                                                         op=ALU.mult), inc=1)
        P.wait("sync", "vector", vrw)
        for dc in range(8):
            dma(P, "sync", rwb[dc, :, c0:c0 + JPC], rw_o[:, dc * 256:(dc + 1) * 256])
        for dc in range(8):
            P.emit("vector", lambda v, dc=dc: v.tensor_scalar(
                out=t_a[:, dc * 256:(dc + 1) * 256],
                in0=pp[:, dc * 256:(dc + 1) * 256],
                scalar1=wneg[:, dc:dc + 1], scalar2=None, op0=ALU.add))
        P.emit("vector", lambda v: v.tensor_tensor(out=t_b, in0=t_a, in1=kb_s,
                                                   op=ALU.max))
        P.emit("vector", lambda v: v.tensor_tensor(out=t_c, in0=t_a, in1=t_b,
                                                   op=ALU.subtract))
        P.emit("vector", lambda v: v.tensor_tensor(out=t_d, in0=kb_s, in1=t_b,
                                                   op=ALU.subtract))
        vv_ = P.cnt["vector"]
        P.wait("scalar", "vector", vv_)
        e1w = P.emit("scalar", lambda s: s.activation(out=t_c, in_=t_c, func=AF.Exp),
                     inc=1)
        e2w = P.emit("scalar", lambda s: s.activation(out=t_d, in_=t_d, func=AF.Exp),
                     inc=1)
        P.wait("vector", "scalar", e2w)
        P.emit("vector", lambda v: v.tensor_tensor(out=t_e, in0=t_c, in1=aa,
                                                   op=ALU.mult))
        P.emit("vector", lambda v: v.tensor_tensor(out=t_g, in0=t_d, in1=vb_s,
                                                   op=ALU.mult))
        P.emit("vector", lambda v: v.tensor_tensor(out=aa, in0=t_e, in1=t_g,
                                                   op=ALU.add))
        P.emit("vector", lambda v: v.tensor_tensor(out=t_e, in0=t_c, in1=bb,
                                                   op=ALU.mult))
        P.emit("vector", lambda v: v.tensor_tensor(out=bb, in0=t_e, in1=t_d,
                                                   op=ALU.add))
        P.emit("vector", lambda v: v.tensor_copy(out=pp, in_=t_b))
    P.barrier()
    A.release()

    A.mark()
    wo_s = A.b16(128, 8 * D)
    rwt = A.b16(128, 8 * 512)
    xn1t = A.f32(128, 512)
    x2t = A.f32(128, 8 * 512)
    sqt = A.f32(128, 512)
    mrow = A.f32(1, 512)
    e2row = A.f32(1, 512)
    nvrow = A.f32(1, 512)
    sdrow = A.f32(1, 512)
    rsrow = A.f32(1, 512)
    yyt = A.f32(128, 512)
    u1t = A.f32(128, 512)

    wW = 0
    for dc in range(8):
        wW = dma(P, "gpsimd", wo_s[:, dc * D:(dc + 1) * D],
                 wot[dc * 128:(dc + 1) * 128, :])
    for TT in range(8):
        t0 = TT * 512
        P.wait("sync", "tensor", P.cnt["tensor"])
        wl = 0
        for dc in range(8):
            wl = dma(P, "sync", rwt[:, dc * 512:(dc + 1) * 512],
                     rwb[dc, :, t0:t0 + 512])
        P.wait("tensor", "sync", wl)
        if TT == 0:
            P.wait("tensor", "gpsimd", wW)
        for oc in range(8):
            psm = PST[oc % 2]
            for ic in range(8):
                mm(P, psm[:, :], wo_s[:, ic * D + oc * 128: ic * D + oc * 128 + 128],
                   rwt[:, ic * 512:(ic + 1) * 512], ic == 0, ic == 7,
                   inc=(1 if ic == 7 else 0))
            ow = P.cnt["tensor"]
            P.wait("sync", "vector", P.cnt["vector"])
            wx = P.dma_ch("sync", "ld0", xn1t, xn1f[oc, :, t0:t0 + 512])
            P.wait_ch("vector", "ld0", wx)
            P.wait("vector", "tensor", ow)
            vx2 = P.emit("vector", lambda v, oc=oc: v.tensor_tensor(
                out=x2t[:, oc * 512:(oc + 1) * 512], in0=xn1t, in1=PST[oc % 2][:, :],
                op=ALU.add), inc=1)
            P.wait("scalar", "vector", vx2)
            sq_w = P.emit("scalar", lambda s, oc=oc: s.activation(
                out=sqt, in_=x2t[:, oc * 512:(oc + 1) * 512], func=AF.Square), inc=1)
            P.wait("tensor", "vector", vx2)
            P.wait("tensor", "scalar", sq_w)
            mm(P, pbig[0:1, 0:512], onec, x2t[:, oc * 512:(oc + 1) * 512],
               oc == 0, oc == 7)
            mm(P, pbig[0:1, 512:1024], onec, sqt, oc == 0, oc == 7,
               inc=(1 if oc == 7 else 0))
        st_w = P.cnt["tensor"]
        P.wait("sync", "vector", P.cnt["vector"])
        for oc in range(8):
            dma(P, "sync", x2f[oc, :, t0:t0 + 512], x2t[:, oc * 512:(oc + 1) * 512])
        P.wait("vector", "tensor", st_w)
        P.emit("vector", lambda v: v.tensor_scalar(
            out=mrow, in0=pbig[0:1, 0:512], scalar1=INV_D, scalar2=None,
            op0=ALU.mult))
        P.emit("vector", lambda v: v.tensor_scalar(
            out=e2row, in0=pbig[0:1, 512:1024], scalar1=INV_D, scalar2=None,
            op0=ALU.mult))
        vst = P.cnt["vector"]
        P.wait("scalar", "vector", vst)
        am = P.emit("scalar", lambda s: s.activation(
            out=nvrow, in_=mrow, func=AF.Square), inc=1)
        P.wait("vector", "scalar", am)
        P.emit("vector", lambda v: v.tensor_tensor(
            out=nvrow, in0=e2row, in1=nvrow, op=ALU.subtract))
        vnv = P.cnt["vector"]
        P.wait("scalar", "vector", vnv)
        asd = P.emit("scalar", lambda s: s.activation(
            out=sdrow, in_=nvrow, func=AF.Sqrt, bias=eps_c[0:1, 0:1]), inc=1)
        P.wait("vector", "scalar", asd)
        P.emit("vector", lambda v: v.reciprocal(out=rsrow, in_=sdrow))
        vrs = P.cnt["vector"]
        P.wait("tensor", "vector", vrs)
        bm = mm(P, pbig[:, 1024:1536], onef[0:1, 0:128], mrow, True, True, inc=1)
        br = mm(P, pbig[:, 1536:2048], onef[0:1, 0:128], rsrow, True, True, inc=1)
        P.wait("vector", "tensor", br)
        for oc in range(8):
            P.emit("vector", lambda v, oc=oc: v.tensor_tensor(
                out=u1t, in0=x2t[:, oc * 512:(oc + 1) * 512],
                in1=pbig[:, 1024:1536], op=ALU.subtract))
            P.emit("vector", lambda v: v.tensor_tensor(
                out=u1t, in0=u1t, in1=pbig[:, 1536:2048], op=ALU.mult))
            vy = P.emit("vector", lambda v, oc=oc: v.tensor_scalar(
                out=yyt, in0=u1t, scalar1=ln2g[:, oc:oc + 1],
                scalar2=ln2b[:, oc:oc + 1], op0=ALU.mult, op1=ALU.add), inc=1)
            P.wait("sync", "vector", vy)
            dma(P, "sync", yyf[oc, :, t0:t0 + 512], yyt)
            P.wait("vector", "sync", P.cnt["sync"])
        P.wait("tensor", "sync", P.cnt["sync"])
    P.barrier()
    A.release()

    A.mark()
    wck_s = A.b16(128, 8 * FF2)
    ym_s = A.b16(128, 8 * 512)
    ywin = [A.f32(128, 768) for _ in range(2)]
    dtmp = A.f32(128, 512)
    rk_t = A.f32(128, 512)
    sq_b16 = A.b16(128, 512)

    wW = 0
    for dc in range(8):
        wW = dma(P, "gpsimd", wck_s[:, dc * FF2:(dc + 1) * FF2],
                 wckt[dc * 128:(dc + 1) * 128, :])
    for TT in range(8):
        t0 = TT * 512
        lo = max(0, t0 - 256)
        for ic in range(8):
            P.wait("sync", "vector", P.cnt["vector"])
            wx = P.dma_ch("sync", "ld%d" % (ic % 2), ywin[ic % 2][:, 0:t0 + 512 - lo],
                          yyf[ic, :, lo:t0 + 512])
            P.wait_ch("vector", "ld%d" % (ic % 2), wx)
            cur = ywin[ic % 2][:, 256:768] if TT > 0 else ywin[ic % 2][:, 0:512]
            dsl = ym_s[:, ic * 512:(ic + 1) * 512]
            if TT == 0:
                P.emit("vector", lambda v, dsl=dsl, cur=cur, ic=ic: v.tensor_scalar(
                    out=dsl[:, 0:256], in0=cur[:, 0:256], scalar1=cmk[:, ic:ic + 1],
                    scalar2=None, op0=ALU.mult))
                sh = ywin[ic % 2][:, 0:256]
                cu2 = ywin[ic % 2][:, 256:512]
                P.emit("vector", lambda v, cu2=cu2, sh=sh: v.tensor_tensor(
                    out=dtmp[:, 0:256], in0=cu2, in1=sh, op=ALU.subtract))
                P.emit("vector", lambda v, dsl=dsl, sh=sh, ic=ic:
                       v.scalar_tensor_tensor(
                           out=dsl[:, 256:512], in0=dtmp[:, 0:256],
                           scalar=cmk[:, ic:ic + 1], in1=sh,
                           op0=ALU.mult, op1=ALU.add))
            else:
                sh = ywin[ic % 2][:, 0:512]
                P.emit("vector", lambda v, cur=cur, sh=sh: v.tensor_tensor(
                    out=dtmp, in0=cur, in1=sh, op=ALU.subtract))
                P.emit("vector", lambda v, dsl=dsl, sh=sh, ic=ic:
                       v.scalar_tensor_tensor(
                           out=dsl, in0=dtmp, scalar=cmk[:, ic:ic + 1],
                           in1=sh, op0=ALU.mult, op1=ALU.add))
        vmix = P.cnt["vector"]
        P.wait("tensor", "vector", vmix)
        if TT == 0:
            P.wait("tensor", "gpsimd", wW)
        for fc in range(32):
            psm = PST[fc % 2]
            for ic in range(8):
                mm(P, psm[:, :],
                   wck_s[:, ic * FF2 + fc * 128: ic * FF2 + fc * 128 + 128],
                   ym_s[:, ic * 512:(ic + 1) * 512], ic == 0, ic == 7,
                   inc=(1 if ic == 7 else 0))
            kw_ = P.cnt["tensor"]
            P.wait("vector", "tensor", kw_)
            P.wait("vector", "scalar", P.cnt["scalar"])
            vrk = P.emit("vector", lambda v, fc=fc: v.tensor_scalar(
                out=rk_t, in0=PST[fc % 2][:, :], scalar1=0.0, scalar2=None,
                op0=ALU.max), inc=1)
            P.wait("scalar", "vector", vrk)
            asq = P.emit("scalar", lambda s: s.activation(
                out=sq_b16, in_=rk_t, func=AF.Square), inc=1)
            P.wait("sync", "scalar", asq)
            w_st = dma(P, "sync", sqb[fc, :, t0:t0 + 512], sq_b16)
            P.wait("scalar", "sync", w_st)
            P.wait("tensor", "vector", vrk)
    P.barrier()
    A.release()

    A.mark()
    wcv_s = A.b16(128, 32 * D)
    wcr_s = A.b16(128, 8 * D)
    rm2_s = A.b16(128, 8 * 512)
    sq_s = A.b16(128, 32 * 512)
    ywin2 = [A.f32(128, 768) for _ in range(2)]
    dtmp2 = A.f32(128, 512)
    rs_t = A.f32(128, 512)
    s1_t = A.f32(128, 512)
    x2t2 = A.f32(128, 512)

    wW = 0
    for fc in range(32):
        wW = dma(P, "gpsimd", wcv_s[:, fc * D:(fc + 1) * D],
                 wcvt[fc * 128:(fc + 1) * 128, :])
    for dc in range(8):
        wW = dma(P, "gpsimd", wcr_s[:, dc * D:(dc + 1) * D],
                 wcrt[dc * 128:(dc + 1) * 128, :])
    for TT in range(8):
        t0 = TT * 512
        lo = max(0, t0 - 256)
        for ic in range(8):
            P.wait("sync", "vector", P.cnt["vector"])
            wx = P.dma_ch("sync", "ld%d" % (ic % 2), ywin2[ic % 2][:, 0:t0 + 512 - lo],
                          yyf[ic, :, lo:t0 + 512])
            P.wait_ch("vector", "ld%d" % (ic % 2), wx)
            cur = ywin2[ic % 2][:, 256:768] if TT > 0 else ywin2[ic % 2][:, 0:512]
            dsl = rm2_s[:, ic * 512:(ic + 1) * 512]
            if TT == 0:
                P.emit("vector", lambda v, dsl=dsl, cur=cur, ic=ic: v.tensor_scalar(
                    out=dsl[:, 0:256], in0=cur[:, 0:256], scalar1=cmr[:, ic:ic + 1],
                    scalar2=None, op0=ALU.mult))
                sh = ywin2[ic % 2][:, 0:256]
                cu2 = ywin2[ic % 2][:, 256:512]
                P.emit("vector", lambda v, cu2=cu2, sh=sh: v.tensor_tensor(
                    out=dtmp2[:, 0:256], in0=cu2, in1=sh, op=ALU.subtract))
                P.emit("vector", lambda v, dsl=dsl, sh=sh, ic=ic:
                       v.scalar_tensor_tensor(
                           out=dsl[:, 256:512], in0=dtmp2[:, 0:256],
                           scalar=cmr[:, ic:ic + 1], in1=sh,
                           op0=ALU.mult, op1=ALU.add))
            else:
                sh = ywin2[ic % 2][:, 0:512]
                P.emit("vector", lambda v, cur=cur, sh=sh: v.tensor_tensor(
                    out=dtmp2, in0=cur, in1=sh, op=ALU.subtract))
                P.emit("vector", lambda v, dsl=dsl, sh=sh, ic=ic:
                       v.scalar_tensor_tensor(
                           out=dsl, in0=dtmp2, scalar=cmr[:, ic:ic + 1],
                           in1=sh, op0=ALU.mult, op1=ALU.add))
        P.wait("sync", "scalar", P.cnt["scalar"])
        wq = 0
        for fc in range(32):
            wq = dma(P, "sync", sq_s[:, fc * 512:(fc + 1) * 512],
                     sqb[fc, :, t0:t0 + 512])
        vmix = P.cnt["vector"]
        P.wait("tensor", "vector", vmix)
        P.wait("tensor", "sync", wq)
        if TT == 0:
            P.wait("tensor", "gpsimd", wW)
        for oc in range(8):
            for fc in range(32):
                mm(P, pst0[:, :],
                   wcv_s[:, fc * D + oc * 128: fc * D + oc * 128 + 128],
                   sq_s[:, fc * 512:(fc + 1) * 512], fc == 0, fc == 31,
                   inc=(1 if fc == 31 else 0))
            vvw = P.cnt["tensor"]
            for ic in range(8):
                mm(P, pst1[:, :],
                   wcr_s[:, ic * D + oc * 128: ic * D + oc * 128 + 128],
                   rm2_s[:, ic * 512:(ic + 1) * 512], ic == 0, ic == 7,
                   inc=(1 if ic == 7 else 0))
            rrw = P.cnt["tensor"]
            P.wait("scalar", "tensor", rrw)
            ars = P.emit("scalar", lambda s: s.activation(
                out=rs_t, in_=pst1[:, :], func=AF.Sigmoid), inc=1)
            P.wait("sync", "vector", P.cnt["vector"])
            wx2 = P.dma_ch("sync", "ld0", x2t2, x2f[oc, :, t0:t0 + 512])
            P.wait_ch("vector", "ld0", wx2)
            P.wait("vector", "scalar", ars)
            P.wait("vector", "tensor", vvw)
            P.emit("vector", lambda v: v.tensor_tensor(
                out=s1_t, in0=rs_t, in1=pst0[:, :], op=ALU.mult))
            vs1 = P.emit("vector", lambda v: v.tensor_tensor(
                out=s1_t, in0=s1_t, in1=x2t2, op=ALU.add), inc=1)
            P.wait("sync", "vector", vs1)
            w_st = dma(P, "sync", s1f[oc, :, t0:t0 + 512], s1_t)
            P.wait("vector", "sync", w_st)
            P.wait("tensor", "vector", vs1)
    P.barrier()
    A.release()

    A.mark()
    w1_s = A.b16(128, 8 * FF2)
    w2_s = A.b16(128, 32 * D)
    b1row = A.b16(1, FF2)
    oneb = A.b16(1, 512)
    b2row = A.f32(1, D)
    thc = A.f32(128, 32)
    a2c = A.f32(128, 32)
    qtmp1 = A.f32(128, 32)
    qtmp2 = A.f32(128, 32)
    qr_s = A.b16(128, 32 * 256)
    xn1t2 = A.b16(128, 256)
    h_t = A.f32(128, 256)
    a_t = A.f32(128, 256)
    s4_t = A.f32(128, 256)
    s2_t = A.f32(128, 256)
    c2_t = A.f32(128, 256)
    g1_t = A.f32(128, 256)
    s1t2 = A.f32(128, 256)
    tr_t = A.f32(128, 256)
    oasm = A.f32(128, 2 * D)
    x1t = A.f32(128, D)
    qv_s = A.f32(128, 32 * 3)

    wW = dma(P, "gpsimd", b1row, b1_r)
    wW = dma(P, "sync", b2row, b2_r)
    P.wait("vector", "gpsimd", P.cnt["gpsimd"])
    P.emit("vector", lambda v: v.memset(oneb, 1.0))
    wW = dma(P, "sync", qv_s[:, 0:32], qph_f)
    wW = dma(P, "sync", qv_s[:, 32:64], qwr_f)
    wW = dma(P, "sync", qv_s[:, 64:96], qwi_f)
    for dc in range(8):
        dma(P, "gpsimd", w1_s[:, dc * FF2:(dc + 1) * FF2],
            w1_d[dc * 128:(dc + 1) * 128, :])
    wW2 = 0
    for fc in range(32):
        wW2 = dma(P, "gpsimd", w2_s[:, fc * D:(fc + 1) * D],
                  w2_d[fc * 128:(fc + 1) * 128, :])
    PI = float(np.pi)
    P.wait("vector", "sync", wW)
    P.emit("vector", lambda v: v.tensor_copy(out=thc, in_=qv_s[:, 0:32]))
    for _ in range(5):
        P.emit("vector", lambda v: v.tensor_single_scalar(
            out=qtmp1, in_=thc, scalar=PI, op=ALU.is_gt))
        P.emit("vector", lambda v: v.scalar_tensor_tensor(
            out=thc, in0=qtmp1, scalar=-2 * PI, in1=thc, op0=ALU.mult, op1=ALU.add))
        P.emit("vector", lambda v: v.tensor_single_scalar(
            out=qtmp1, in_=thc, scalar=-PI, op=ALU.is_lt))
        P.emit("vector", lambda v: v.scalar_tensor_tensor(
            out=thc, in0=qtmp1, scalar=2 * PI, in1=thc, op0=ALU.mult, op1=ALU.add))
    vth = P.cnt["vector"]
    P.wait("scalar", "vector", vth)
    asn = P.emit("scalar", lambda s: s.activation(out=qtmp1, in_=thc, func=AF.Sin),
                 inc=1)
    ash = P.emit("scalar", lambda s: s.activation(out=qtmp2, in_=thc, func=AF.Sin,
                                                  scale=0.5), inc=1)
    P.wait("vector", "scalar", ash)
    P.emit("vector", lambda v: v.tensor_tensor(out=qtmp2, in0=qtmp2, in1=qtmp2,
                                               op=ALU.mult))
    P.emit("vector", lambda v: v.tensor_scalar(out=qtmp2, in0=qtmp2, scalar1=-2.0,
                                               scalar2=1.0, op0=ALU.mult,
                                               op1=ALU.add))
    P.emit("vector", lambda v: v.tensor_tensor(out=qtmp2, in0=qtmp2,
                                               in1=qv_s[:, 32:64], op=ALU.mult))
    P.emit("vector", lambda v: v.tensor_tensor(out=qtmp1, in0=qtmp1,
                                               in1=qv_s[:, 64:96], op=ALU.mult))
    P.emit("vector", lambda v: v.tensor_tensor(out=a2c, in0=qtmp2, in1=qtmp1,
                                               op=ALU.add))
    P.emit("vector", lambda v: v.tensor_tensor(out=a2c, in0=a2c,
                                               in1=qv_s[:, 64:96], op=ALU.mult))
    P.emit("vector", lambda v: v.tensor_scalar(out=a2c, in0=a2c, scalar1=0.2,
                                               scalar2=None, op0=ALU.mult))

    xnt_s = A.b16(128, 8 * 256)

    for T4 in range(16):
        t0 = T4 * 256
        P.wait("sync", "tensor", P.cnt["tensor"])
        wxt = 0
        for ic in range(8):
            wxt = dma(P, "sync", xnt_s[:, ic * 256:(ic + 1) * 256],
                      xn1b[ic, :, t0:t0 + 256])
        P.wait("tensor", "sync", wxt)
        if T4 == 0:
            P.wait("tensor", "gpsimd", wW2)
            P.wait("tensor", "vector", P.cnt["vector"])
        ahel = []
        for fc in range(32):
            psm = PST[fc % 2]
            if fc >= 2:
                P.wait("tensor", "scalar", ahel[fc - 2])
            for ic in range(8):
                mm(P, psm[:, 0:256],
                   w1_s[:, ic * FF2 + fc * 128: ic * FF2 + fc * 128 + 128],
                   xnt_s[:, ic * 256:(ic + 1) * 256], ic == 0, False)
            hw = mm(P, psm[:, 0:256], b1row[0:1, fc * 128:(fc + 1) * 128],
                    oneb[0:1, 0:256], False, True, inc=1)
            P.wait("scalar", "tensor", hw)
            if fc >= 1:
                P.wait("scalar", "vector", vqr)
            ahe = P.emit("scalar", lambda s, psm=psm: s.copy(
                out=h_t, in_=psm[:, 0:256]), inc=1)
            ahel.append(ahe)
            P.wait("vector", "scalar", ahe)
            P.emit("vector", lambda v, fc=fc: v.tensor_scalar(
                out=a_t, in0=h_t, scalar1=0.1, scalar2=thc[:, fc:fc + 1],
                op0=ALU.mult, op1=ALU.add))
            va = P.cnt["vector"]
            P.wait("scalar", "vector", va)
            as2 = P.emit("scalar", lambda s: s.activation(
                out=s2_t, in_=a_t, func=AF.Sin, scale=0.5), inc=1)
            as4 = P.emit("scalar", lambda s: s.activation(
                out=s4_t, in_=a_t, func=AF.Sin, scale=0.25), inc=1)
            P.wait("vector", "scalar", as4)
            P.emit("vector", lambda v: v.tensor_tensor(
                out=c2_t, in0=s4_t, in1=s4_t, op=ALU.mult))
            P.emit("vector", lambda v: v.tensor_scalar(
                out=c2_t, in0=c2_t, scalar1=-2.0, scalar2=1.0,
                op0=ALU.mult, op1=ALU.add))
            P.emit("vector", lambda v: v.tensor_tensor(
                out=c2_t, in0=c2_t, in1=s2_t, op=ALU.mult))
            P.emit("vector", lambda v, fc=fc: v.scalar_tensor_tensor(
                out=c2_t, in0=c2_t, scalar=a2c[:, fc:fc + 1], in1=h_t,
                op0=ALU.mult, op1=ALU.add))
            vqr = P.emit("vector", lambda v, fc=fc: v.tensor_scalar(
                out=qr_s[:, fc * 256:(fc + 1) * 256], in0=c2_t, scalar1=0.0,
                scalar2=None, op0=ALU.max), inc=1)
        P.wait("tensor", "vector", P.cnt["vector"])
        asll = []
        for oc in range(8):
            psm = PST[oc % 2]
            if oc >= 2:
                P.wait("tensor", "scalar", asll[oc - 2])
            for fc in range(32):
                mm(P, psm[:, 0:256],
                   w2_s[:, fc * D + oc * 128: fc * D + oc * 128 + 128],
                   qr_s[:, fc * 256:(fc + 1) * 256], fc == 0, False)
            ow = mm(P, psm[:, 0:256], b2row[0:1, oc * 128:(oc + 1) * 128],
                    onef[0:1, 0:256], False, True, inc=1)
            P.wait("scalar", "tensor", ow)
            if oc >= 1:
                P.wait("scalar", "vector", vtr[-1])
            asl = P.emit("scalar", lambda s, psm=psm: s.activation(
                out=g1_t, in_=psm[:, 0:256], func=AF.Silu), inc=1)
            asll.append(asl)
            P.wait("sync", "vector", P.cnt["vector"])
            ws1 = P.dma_ch("sync", "ld0", s1t2, s1f[oc, :, t0:t0 + 256])
            P.wait_ch("vector", "ld0", ws1)
            P.wait("vector", "scalar", asl)
            if oc == 0:
                vtr = []
            vtr.append(P.emit("vector", lambda v: v.tensor_tensor(
                out=tr_t, in0=g1_t, in1=s1t2, op=ALU.mult), inc=1))
            P.wait("tensor", "vector", vtr[-1])
            for ts_ in range(2):
                regt = pbig[:, (ts_ % 2) * 512:(ts_ % 2) * 512 + 128]
                tw = transp(P, regt, tr_t[:, ts_ * 128:(ts_ + 1) * 128], idf)
                P.wait("vector", "tensor", tw)
                P.emit("vector", lambda v, regt=regt, ts_=ts_, oc=oc: v.tensor_copy(
                    out=oasm[:, ts_ * D + oc * 128: ts_ * D + oc * 128 + 128],
                    in_=regt), inc=1)
            P.wait("tensor", "vector", P.cnt["vector"])
        for ts_ in range(2):
            tg = t0 + ts_ * 128
            bb_, jj = tg // JPC, tg % JPC
            P.wait("sync", "vector", P.cnt["vector"])
            wx1 = P.dma_ch("sync", "ld1", x1t, xs_in[bb_, jj:jj + 128, :])
            P.wait_ch("vector", "ld1", wx1)
            vo = P.emit("vector", lambda v, ts_=ts_: v.tensor_tensor(
                out=oasm[:, ts_ * D:(ts_ + 1) * D],
                in0=oasm[:, ts_ * D:(ts_ + 1) * D], in1=x1t, op=ALU.add), inc=1)
            P.wait("sync", "vector", vo)
            dma(P, "sync", out_d[bb_, jj:jj + 128, :], oasm[:, ts_ * D:(ts_ + 1) * D])
        P.wait("vector", "sync", P.cnt["sync"])
    A.release()
    A.release()


def build_l1(debug=False):
    env = _mk_env()
    P, nc, ctx = env["P"], env["nc"], env["ctx"]
    emit_l1(env, debug=debug)
    fin = P.cnt["sync"]
    P.wait("sync", "sync", fin)
    P.wait("gpsimd", "sync", fin)
    block = ctx.enter_context(nc.Block())
    P.finalize(block)
    ctx.close()
    return nc


def build_fused(debug=False):
    env = _mk_env()
    P, nc, ctx = env["P"], env["nc"], env["ctx"]
    x1loc = nc.dram_tensor("x1loc", [B, JPC, D], F32, kind="Internal").ap()
    shared_d = emit_l0(env, debug=debug, x1loc=x1loc)
    P.barrier()
    emit_l1(env, debug=debug, xs_src=x1loc, shared_d=shared_d)
    fin = P.cnt["sync"]
    P.wait("sync", "sync", fin)
    P.wait("gpsimd", "gpsimd", P.cnt["gpsimd"])
    P.wait("gpsimd", "sync", fin)
    block = ctx.enter_context(nc.Block())
    P.finalize(block)
    ctx.close()
    return nc


def _l0_maps(inputs):
    import ml_dtypes
    f = np.float32
    c = np.ascontiguousarray
    g = lambda k: np.asarray(inputs[k], dtype=f)
    shared = dict(
        wqt=c(g("L0_attn_Wq").T), wkt=c(g("L0_attn_Wk").T),
        wvt=c(g("L0_attn_Wv").T), wot=c(g("L0_attn_Wo").T),
        bq=g("L0_attn_bq").reshape(1, D), bk=g("L0_attn_bk").reshape(1, D),
        bv=g("L0_attn_bv").reshape(1, D), bo=g("L0_attn_bo").reshape(1, D),
        wqst=c(g("L0_attn_Wqs").T), wkst=c(g("L0_attn_Wks").T),
        wvst=c(g("L0_attn_Wvs").T),
        bqs=g("L0_attn_bqs").reshape(1, 3), bks=g("L0_attn_bks").reshape(1, 3),
        bvs=g("L0_attn_bvs").reshape(1, 3),
        wkst_r1=c(g("L0_attn_Wks").T[:, [1, 2, 0]]),
        wkst_r2=c(g("L0_attn_Wks").T[:, [2, 0, 1]]),
        wvst_r1=c(g("L0_attn_Wvs").T[:, [1, 2, 0]]),
        wvst_r2=c(g("L0_attn_Wvs").T[:, [2, 0, 1]]),
        bks_r1=c(g("L0_attn_bks")[[1, 2, 0]].reshape(1, 3)),
        bks_r2=c(g("L0_attn_bks")[[2, 0, 1]].reshape(1, 3)),
        bvs_r1=c(g("L0_attn_bvs")[[1, 2, 0]].reshape(1, 3)),
        bvs_r2=c(g("L0_attn_bvs")[[2, 0, 1]].reshape(1, 3)),
        bqc=c(g("L0_attn_bq").reshape(8, 128).T),
        bkc=c(g("L0_attn_bk").reshape(8, 128).T),
        bvc=c(g("L0_attn_bv").reshape(8, 128).T),
        temp=g("L0_attn_temp").reshape(1, 1),
        rw=c(g("L0_ffn_router_w")), rb=g("L0_ffn_router_b").reshape(1, E),
        w1=c(g("L0_ffn_w1")), b1r=c(g("L0_ffn_b1").reshape(E, 1, FF1)),
        b1c=c(g("L0_ffn_b1").reshape(E, 16, 128).transpose(0, 2, 1)),
        w2=c(g("L0_ffn_w2")), b2r=c(g("L0_ffn_b2").reshape(E, 1, D)),
        iden=np.eye(128, dtype=ml_dtypes.bfloat16),
        idenf=np.eye(128, dtype=f),
        onesf=np.ones((1, 512), f),
        iota=np.broadcast_to(np.arange(E, dtype=f), (128, E)).copy(),
        iotam=np.broadcast_to(np.arange(E, dtype=f) - 999.0, (128, E)).copy(),
    )
    x = np.asarray(inputs["L0_attn_x"], dtype=f)
    return [dict(shared, x=c(np.roll(x, -i * JPC, axis=1)))
            for i in range(NCORES)]


def _l1_maps(inputs, x1=None):
    f = np.float32
    c = np.ascontiguousarray
    g = lambda k: np.asarray(inputs[k], dtype=f)
    fm8 = lambda k: c(g(k).reshape(8, 128).T)
    fm32 = lambda k: c(g(k).reshape(32, 128).T)
    shared = dict(
        ln1g=g("L1_attn_ln1_g").reshape(1, D), ln1b=g("L1_attn_ln1_b").reshape(1, D),
        ln2gf=fm8("L1_attn_ln2_g"), ln2bf=fm8("L1_attn_ln2_b"),
        tmkf=fm8("L1_attn_time_mix_k"), tmvf=fm8("L1_attn_time_mix_v"),
        tmrf=fm8("L1_attn_time_mix_r"), tdf=fm8("L1_attn_time_decay"),
        tff=fm8("L1_attn_time_first"),
        cmkf=fm8("L1_attn_cmix_k"), cmrf=fm8("L1_attn_cmix_r"),
        wkt1=c(g("L1_attn_Wk").T), wvt1=c(g("L1_attn_Wv").T),
        wrt1=c(g("L1_attn_Wr").T), wot1=c(g("L1_attn_Wo").T),
        wckt=c(g("L1_attn_Wck").T), wcvt=c(g("L1_attn_Wcv").T),
        wcrt=c(g("L1_attn_Wcr").T),
        w1f=c(g("L1_ffn_W1")), w2f=c(g("L1_ffn_W2")),
        b1f=g("L1_ffn_b1").reshape(1, FF2), b2f=g("L1_ffn_b2").reshape(1, D),
        qphf=fm32("L1_ffn_quantum_phases"),
        qwrf=fm32("L1_ffn_quantum_weights_real"),
        qwif=fm32("L1_ffn_quantum_weights_imag"),
        idenf=np.eye(128, dtype=f),
        onesf=np.ones((1, 512), f),
        onescol=np.ones((128, 1), f),
    )
    maps = []
    for core in range(NCORES):
        m = dict(shared)
        if x1 is not None:
            m["xs"] = c(x1[:, core * JPC:(core + 1) * JPC, :])
        maps.append(m)
    return maps


def kernel(**inputs):
    maps0 = _l0_maps(inputs)
    maps1 = _l1_maps(inputs)
    maps = [dict(m0, **m1) for m0, m1 in zip(maps0, maps1)]
    nc = build_fused()
    res = run_bass_kernel_spmd(nc, maps, core_ids=list(range(NCORES)))
    out = np.concatenate([r["out"] for r in res.results], axis=1)
    return out


def kernel_unfused(**inputs):
    maps0 = _l0_maps(inputs)
    nc0 = build_l0()
    res0 = run_bass_kernel_spmd(nc0, maps0, core_ids=list(range(NCORES)))
    maps1 = _l1_maps(inputs)
    for i in range(NCORES):
        maps1[i]["xs"] = np.ascontiguousarray(res0.results[i]["x1"])
    nc1 = build_l1()
    res1 = run_bass_kernel_spmd(nc1, maps1, core_ids=list(range(NCORES)))
    out = np.concatenate([r["out"] for r in res1.results], axis=1)
    return out
